# version 22
# baseline (speedup 1.0000x reference)
import math
import numpy as np
import concourse.bass as bass
import concourse.mybir as mybir
from concourse.bass_utils import run_bass_kernel_spmd

F32 = mybir.dt.float32
F32R = mybir.dt.float32r
BF16 = mybir.dt.bfloat16
FP16 = mybir.dt.float16
AL = mybir.AluOpType
AF = mybir.ActivationFunctionType

T = 1024
D = 2048
H = 16
BLK = 256
NB = T // BLK
TC = T // 128
FC = D // 128
ICN = 64
NHG = 8
EPS_LN = 1e-5
EPS_RMS = 1e-6

ENGS = ("sync", "tensor", "vector", "scalar", "gpsimd")


def _slopes():
    start = 2 ** (-2 ** -(math.log2(H) - 3))
    return [start * start ** i for i in range(H)]


def build_program(stop_phase=None):
    nc = bass.Bass("TRN2", target_bir_lowering=False)
    slopes = _slopes()
    bdec = [float(np.exp(np.float32(-s * BLK))) for s in slopes]

    x_own = nc.dram_tensor("x_own", [TC, 128, D], F32, kind="ExternalInput").ap()
    x_prev = nc.dram_tensor("x_prev", [TC, 128, D], F32, kind="ExternalInput").ap()
    maskc = nc.dram_tensor("maskc", [128, 1], F32, kind="ExternalInput").ap()
    id_b = nc.dram_tensor("id_b", [128, 128], BF16, kind="ExternalInput").ap()
    ones_r = nc.dram_tensor("ones_r", [128, 128], BF16, kind="ExternalInput").ap()
    kdec_d = nc.dram_tensor("kdec", [128, 2 * H], F32, kind="ExternalInput").ap()
    kdecp_d = nc.dram_tensor("kdecp", [128, 128], F32, kind="ExternalInput").ap()
    normw_d = nc.dram_tensor("normw", [128, FC], F32, kind="ExternalInput").ap()
    qdec_d = nc.dram_tensor("qdec", [NHG, 128, 512], FP16, kind="ExternalInput").ap()
    ddt_d = nc.dram_tensor("ddt", [NHG, 128, 1024], FP16, kind="ExternalInput").ap()
    w_kv = nc.dram_tensor("w_kv", [NHG, 4, 128, 2048], F32, kind="ExternalInput").ap()
    w_qk = nc.dram_tensor("w_qk", [NHG, 128, 4 * FC * 128], F32, kind="ExternalInput").ap()
    w_v = nc.dram_tensor("w_v", [NHG, 128, FC * 256], F32, kind="ExternalInput").ap()
    w_g = nc.dram_tensor("w_g", [FC, 128, FC * 128], F32, kind="ExternalInput").ap()
    w_o = nc.dram_tensor("w_o", [4, 128, FC * 512], F32, kind="ExternalInput").ap()
    w_gate = nc.dram_tensor("w_gate", [ICN, 128, FC * 128], F32, kind="ExternalInput").ap()
    w_up = nc.dram_tensor("w_up", [ICN, 128, FC * 128], F32, kind="ExternalInput").ap()
    w_down = nc.dram_tensor("w_down", [2, 16, 128, 4096], F32, kind="ExternalInput").ap()
    out_d = nc.dram_tensor("out", [TC, 128, D], F32, kind="ExternalOutput").ap()

    AWORDS = 53120
    _ptr = [0]
    def region(words, align=128):
        p = (_ptr[0] + align - 1) // align * align
        assert p + words <= AWORDS, f"arena overflow {p}+{words}"
        _ptr[0] = p + words
        return p

    o_idb = region(64)
    o_ones = region(128)
    o_kdec = region(32)
    o_kdecp = region(128)
    o_normw = region(16)
    o_mask = region(8)
    o_stats = region(64)
    o_mv = region(16)
    o_inv = region(16)
    o_xln = region(8192)
    o_ofm = region(8192)
    o_grp = region(8192)
    o_scr = region(8192)
    o_xt = region(4096)
    o_xlntm = region(2048)
    o_nrm = region(2048)
    o_qkw = region(4096)
    o_kvw = region(2048)
    o_gu = region(2048)
    o_sil = region(2048)
    o_vw = region(512)

    o_srow = o_nrm
    o_at4 = o_ofm
    o_x2b = o_xln
    o_hsb = o_ofm
    o_ogt = o_grp
    o_scr_qdec = o_scr
    o_scr_dds = o_scr + 512
    o_scr_kvini = o_scr + 1536
    o_scr_kvpp = o_scr + 2560
    o_scr_kvppb = o_scr + 3072
    o_scr_sT = o_scr + 3328
    o_scr_ksc = o_scr + 3584
    o_scr_vsc = o_scr + 3840
    o_pro_ksc = o_scr + 4096
    o_kvp = o_nrm
    o_xres = o_scr
    o_o2 = o_xlntm
    o_fft = o_qkw
    o_gu2 = o_kvw
    o_outt = o_xt
    o_wow = o_vw

    prog = {e: [] for e in ENGS}
    last_wait = {e: {} for e in ENGS}

    class Sem:
        def __init__(self, name):
            self.name = name
            self.n = 0
            self.h = None

    TE, VE, SE = Sem("TE"), Sem("VE"), Sem("SE")
    SEMS = [TE, VE, SE]

    def mksem(name):
        sm = Sem(name)
        SEMS.append(sm)
        return sm

    CONST = mksem("cst")
    S_xt = [mksem("sxt0"), mksem("sxt1")]
    S_kv = [mksem("skv0"), mksem("skv1")]
    S_kv2 = [mksem("skv20"), mksem("skv21")]
    S_dq = [mksem("sdq0"), mksem("sdq1")]
    S_qk = [mksem("sqk0"), mksem("sqk1")]
    S_vw = [mksem("svw0"), mksem("svw1")]
    S_wo = [mksem("swo0"), mksem("swo1")]
    S_gu = [mksem("sgu0"), mksem("sgu1")]
    S_gu2 = [mksem("sgu20"), mksem("sgu21")]
    S_dw = [mksem("sdw0"), mksem("sdw1")]
    S_dw2 = [mksem("sdw20"), mksem("sdw21")]
    S_wo2 = [mksem("swo20"), mksem("swo21")]
    S_out = [mksem("sou0"), mksem("sou1")]

    def E(eng, fn):
        prog[eng].append(fn)

    def W(eng, sem, val):
        if val is None or val <= 0:
            return
        assert val <= sem.n, f"wait {sem.name}>={val} but only {sem.n} planned"
        if last_wait[eng].get(sem.name, -1) >= val:
            return
        last_wait[eng][sem.name] = val
        E(eng, lambda e, s=sem, v=val: e.wait_ge(s.h, v))

    def OP(eng, fn, sem=None, k=1):
        if sem is None:
            E(eng, lambda e, f=fn: f(e))
            return None
        sem.n += k
        def run(e, f=fn, s=sem, kk=k):
            f(e).then_inc(s.h, kk)
        E(eng, run)
        return sem.n

    def DRAIN(eng):
        E(eng, lambda e: e.drain())

    ARh = [None]
    PS = []

    def ap(off, words, dtype=F32, parts=(0, 128)):
        a = ARh[0][parts[0]:parts[1], off:off + words]
        if dtype == F32:
            return a
        return a.bitcast(dtype)

    bank_free = {}

    def bank_wait(eng, bank):
        ent = bank_free.get(bank)
        if ent is not None:
            W(eng, ent[0], ent[1])

    def logical():
        for dst, src in (
            (ap(o_idb, 64, BF16), id_b),
            (ap(o_ones, 64, BF16), ones_r),
            (ap(o_kdec, 2 * H), kdec_d),
            (ap(o_kdecp, 128), kdecp_d),
            (ap(o_normw, FC), normw_d),
            (ap(o_mask, 1), maskc),
        ):
            OP("sync", lambda e, d=dst, s=src: e.dma_start(out=d, in_=s), CONST, 16)
        const_n = CONST.n

        ln_par = [0]
        def ln_tile(src_ap, out_ap, eps):
            ln_par[0] ^= 1
            pr = ln_par[0]
            st = ap(o_stats + pr * 32, 24); mv = ap(o_mv + pr * 8, 2); inv = ap(o_inv + pr * 8, 1)
            for g in range(4):
                OP("vector", lambda e, a=st[:, g * 6:(g + 1) * 6],
                   b=src_ap[:, g * 512:(g + 1) * 512]: e.bn_stats(a, b))
            DRAIN("vector")
            OP("vector", lambda e, a=mv, b=st: e.bn_aggr(a, b))
            DRAIN("vector")
            OP("vector", lambda e, a=inv, b=mv[:, 1:2], ep=eps: e.tensor_scalar(a, b, ep, None, AL.add))
            DRAIN("vector")
            v1 = OP("vector", lambda e, a=inv: e.reciprocal(a, a), VE)
            W("scalar", VE, v1)
            s1 = OP("scalar", lambda e, a=inv: e.activation(out=a, in_=a, func=AF.Sqrt), SE)
            W("vector", SE, s1)
            return OP("vector", lambda e, o=out_ap, x=src_ap, m=mv[:, 0:1], iv=inv:
                      e.tensor_scalar(o, x, m, iv, AL.subtract, AL.mult), VE)

        def transpose_tile(src_ap, dst_ap, tcol, banks):
            W("tensor", CONST, const_n)
            d3 = dst_ap.rearrange("p (f t) -> p f t", f=16)
            for q in range(4):
                bank = banks[q % len(banks)]
                bank_wait("tensor", bank)
                te = None
                for j in range(4):
                    f = q * 4 + j
                    pdst = PS[bank][:, j * 64:(j + 1) * 64].bitcast(BF16)
                    te = OP("tensor", lambda e, p=pdst, s=src_ap[:, f * 128:(f + 1) * 128],
                            i=ap(o_idb, 64, BF16): e.transpose(p, s, i),
                            TE if j == 3 else None)
                W("scalar", TE, te)
                psrc = PS[bank][:, 0:256].bitcast(BF16).rearrange("p (f c) -> p f c", f=4)
                dsl = d3[:, q * 4:(q + 1) * 4, tcol * 128:(tcol + 1) * 128]
                ev = OP("scalar", lambda e, o=dsl, s=psrc: e.copy(o, s), SE)
                bank_free[bank] = (SE, ev)

        xt_slot_ve = {}
        def ln_transpose_stream(x_dram, dst_ap, slot_te):
            for tc in range(TC):
                slot = tc % 2
                xt = ap(o_xt + slot * 2048, 2048)
                W("sync", VE, xt_slot_ve.get(slot))
                g1 = OP("sync", lambda e, d=xt, s=x_dram[tc]: e.dma_start(out=d, in_=s), S_xt[slot], 16)
                W("vector", S_xt[slot], g1)
                W("vector", TE, slot_te.get(slot))
                xo = ap(o_xlntm + slot * 1024, 1024, BF16)
                vdone = ln_tile(xt, xo, EPS_LN)
                xt_slot_ve[slot] = vdone
                W("tensor", VE, vdone)
                transpose_tile(xo, dst_ap, tc, (4, 5, 6, 7))
                slot_te[slot] = TE.n
            return SE.n

        xf_b = ap(o_xln, 8192, BF16)
        xf = xf_b.rearrange("p (f t) -> p f t", f=16)
        xp_b = ap(o_ofm, 8192, BF16)
        xp = xp_b.rearrange("p (f t) -> p f t", f=16)
        lnt_slot_te = {}
        lnp_se = ln_transpose_stream(x_prev, xp_b, lnt_slot_te)

        if stop_phase == "lnp":
            return
        kv_slot_te = {}
        pro_ksc_te = {}
        W("tensor", SE, lnp_se)
        for hg in range(NHG):
            mm_te = None
            for fq in range(4):
                grp = fq % 2
                W("gpsimd", TE, kv_slot_te.get(grp))
                ld1 = OP("gpsimd", lambda e, d=ap(o_kvw + grp * 1024, 512, BF16),
                         s=w_kv[hg, fq, :, 0:1024]: e.dma_start(out=d, in_=s), S_kv[grp], 16)
                ld2 = OP("gpsimd", lambda e, d=ap(o_kvw + grp * 1024 + 512, 512, BF16),
                         s=w_kv[hg, fq, :, 1024:2048]: e.dma_start(out=d, in_=s), S_kv2[grp], 16)
                for jp in range(2):
                    W("tensor", (S_kv if jp == 0 else S_kv2)[grp], ld1 if jp == 0 else ld2)
                    for b in range(NB):
                        for u in range(2):
                            bank = b * 2 + u
                            if fq == 0 and jp == 0:
                                bank_wait("tensor", bank)
                            lhs0 = b * 256 + u * 128
                            for j in (jp * 2, jp * 2 + 1):
                                f = fq * 4 + j
                                is_last = (b == NB - 1 and u == 1 and j == 3)
                                mm_te = OP("tensor", lambda e, p=PS[bank][:, 0:512],
                                           l=xp[:, f, lhs0:lhs0 + 128],
                                           r=ap(o_kvw + grp * 1024, 1024, BF16)[:, j * 512:(j + 1) * 512],
                                           st=(f == 0), sp=(f == FC - 1): e.matmul(p, l, r, start=st, stop=sp),
                                           TE if is_last else None)
                kv_slot_te[grp] = mm_te
            a_mm_te = None
            for b in range(NB):
                sdone = None
                W("scalar", CONST, const_n)
                for u in range(2):
                    bank = b * 2 + u
                    W("scalar", TE, kv_slot_te.get(0))
                    W("scalar", TE, kv_slot_te.get(1))
                    W("scalar", TE, pro_ksc_te.get("t"))
                    kvsil = ap(o_scr_ksc + u * 256, 256, BF16)
                    e1 = OP("scalar", lambda e, o=kvsil, p=PS[bank][:, 0:512]:
                            e.activation(out=o, in_=p, func=AF.Silu), SE)
                    bank_free[bank] = (SE, e1)
                    W("vector", SE, e1)
                    W("vector", TE, pro_ksc_te.get("t"))
                    for hh in range(2):
                        h = hg * 2 + hh
                        ksc = ap(o_pro_ksc + (u * 2 + hh) * 64, 64, BF16)
                        kcol = ap(o_kdecp, 128)[:, (h * 8 + b * 2 + u):(h * 8 + b * 2 + u) + 1]
                        sdone = OP("vector", lambda e, o=ksc, i=kvsil[:, hh * 256:hh * 256 + 128], sc=kcol:
                                   e.tensor_scalar(o, i, sc, None, AL.mult),
                                   VE if (u == 1 and hh == 1) else None)
                for hh in range(2):
                    bank = hh
                    if b == 0:
                        bank_wait("tensor", bank)
                    W("tensor", VE, sdone)
                    for u in range(2):
                        ksc = ap(o_pro_ksc + (u * 2 + hh) * 64, 64, BF16)
                        vsl = ap(o_scr_ksc + u * 256, 256, BF16)[:, hh * 256 + 128:hh * 256 + 256]
                        a_mm_te = OP("tensor", lambda e, p=PS[bank][:, 0:128], l=ksc, r=vsl,
                                     st=(b == 0 and u == 0), sp=(b == NB - 1 and u == 1):
                                     e.matmul(p, l, r, start=st, stop=sp),
                                     TE if (u == 1 and hh == 1) else None)
                pro_ksc_te["t"] = a_mm_te
            W("scalar", TE, a_mm_te)
            for hh in range(2):
                h = hg * 2 + hh
                vv = OP("scalar", lambda e, o=ap(o_scr_kvini + h * 64, 64, BF16), i=PS[hh][:, 0:128],
                        m=ap(o_mask, 1): e.activation(out=o, in_=i, func=AF.Copy, scale=m), SE)
                bank_free[hh] = (SE, vv)
        kv_init_done_se = SE.n

        pro_sil_se = SE.n
        if stop_phase == "prologue":
            return
        ln1_se = ln_transpose_stream(x_own, xf_b, lnt_slot_te)
        if stop_phase == "ln1":
            return
        g_q, g_qs, g_kf, g_ks, g_vt = (o_grp, o_grp + 1024, o_grp + 2048, o_grp + 3072, o_grp + 4096)

        def gsl(base, hh, lo, n):
            return ap(base, 1024, BF16)[:, hh * 1024 + lo: hh * 1024 + lo + n]

        qk_slot_te = {}
        qk_res_te = {}
        vw_slot_te = {}
        dq_slot_ve = {}
        sT_rd = {0: None, 1: None}
        kv_ve = {}
        dummy_kv_guard = kv_slot_te.get(0)
        W("tensor", SE, ln1_se)
        for hg in range(NHG):
            sdq = hg % 2
            if hg == 0:
                W("sync", SE, kv_init_done_se)
            W("sync", VE, dq_slot_ve.get(sdq))
            OP("sync", lambda e, d=ap(o_scr_dds + sdq * 512, 512, FP16), s=ddt_d[hg]:
               e.dma_start(out=d, in_=s), S_dq[sdq], 16)
            dqld = OP("sync", lambda e, d=ap(o_scr_qdec + sdq * 256, 256, FP16), s=qdec_d[hg]:
                      e.dma_start(out=d, in_=s), S_dq[sdq], 16)

            W("gpsimd", TE, qk_res_te.get(0))
            if hg == 0:
                W("gpsimd", TE, kv_slot_te.get(0))
            qk_ld = OP("gpsimd", lambda e, d=ap(o_qkw, 4096, BF16), s=w_qk[hg]:
                       e.dma_start(out=d, in_=s), S_qk[0], 16)
            W("tensor", S_qk[0], qk_ld)
            t1 = None
            for cc in range(4):
                hh, isk = cc // 2, cc % 2
                for th in range(2):
                    bank = (cc % 2) * 2 + th
                    bank_wait("tensor", bank)
                    for f in range(FC):
                        lhs = ap(o_qkw, 4096, BF16)[:, cc * 2048 + f * 128: cc * 2048 + (f + 1) * 128]
                        t1 = OP("tensor", lambda e, p=PS[bank][:, 0:512], l=lhs,
                                r=xf[:, f, th * 512:(th + 1) * 512],
                                st=(f == 0), sp=(f == FC - 1): e.matmul(p, l, r, start=st, stop=sp),
                                TE if f == FC - 1 else None)
                    W("scalar", TE, t1)
                    dst = gsl(g_kf if isk else g_q, hh, th * 512, 512)
                    ev = OP("scalar", lambda e, o=dst, p=PS[bank][:, 0:512]:
                            e.activation(out=o, in_=p, func=AF.Silu), SE)
                    bank_free[bank] = (SE, ev)
            qk_res_te[0] = t1
            qk_done_se = SE.n

            W("vector", S_dq[sdq], dqld)
            W("vector", SE, qk_done_se)
            for hh in range(2):
                for b in range(NB):
                    OP("vector", lambda e, o=gsl(g_qs, hh, b * 256, 256),
                       a=gsl(g_q, hh, b * 256, 256),
                       q=ap(o_scr_qdec + sdq * 256, 256, FP16)[:, hh * 256:hh * 256 + 256]:
                       e.tensor_tensor(o, a, q, AL.mult), VE)
            qsc_done = VE.n

            for hh in range(2):
                h = hg * 2 + hh
                for q in range(2):
                    bank = 4 + (q % 2)
                    bank_wait("tensor", bank)
                    te = None
                    for j in range(4):
                        tcl = q * 4 + j
                        te = OP("tensor", lambda e, p=PS[bank][:, j * 64:(j + 1) * 64].bitcast(BF16),
                                s=gsl(g_kf, hh, tcl * 128, 128), i=ap(o_idb, 64, BF16):
                                e.transpose(p, s, i), TE if j == 3 else None)
                    W("scalar", TE, te)
                    ev = None
                    for j in range(4):
                        tcl = q * 4 + j
                        kcol = ap(o_kdec, 2 * H)[:, h * 2 + tcl % 2: h * 2 + tcl % 2 + 1]
                        ev = OP("scalar", lambda e, o=gsl(g_ks, hh, tcl * 128, 128),
                                s=PS[bank][:, j * 64:(j + 1) * 64].bitcast(BF16), sc=kcol:
                                e.activation(out=o, in_=s, func=AF.Copy, scale=sc),
                                SE if j == 3 else None)
                    bank_free[bank] = (SE, ev)
            kst_done_se = SE.n

            W("gpsimd", TE, vw_slot_te.get(0))
            v_ld = OP("gpsimd", lambda e, d=ap(o_sil, 2048, BF16), s=w_v[hg]:
                      e.dma_start(out=d, in_=s), S_vw[0], 16)
            W("tensor", S_vw[0], v_ld)
            v_last_te = None
            for tcl in range(TC):
                bank = tcl
                bank_wait("tensor", bank)
                for f in range(FC):
                    v_last_te = OP("tensor", lambda e, p=PS[bank][:, 0:256],
                                   l=xf[:, f, tcl * 128:(tcl + 1) * 128],
                                   r=ap(o_sil, 2048, BF16)[:, f * 256:(f + 1) * 256],
                                   st=(f == 0), sp=(f == FC - 1): e.matmul(p, l, r, start=st, stop=sp),
                                   TE if f == FC - 1 else None)
            vw_slot_te[0] = v_last_te
            W("scalar", TE, v_last_te)
            v_ev_se = {}
            for tcl in range(TC):
                bank = tcl
                pv = PS[bank][:, 0:256].rearrange("p (hh c) -> p hh c", hh=2)
                dv = ap(g_vt, 1024, BF16).rearrange("p (hh t) -> p hh t", hh=2)[:, :, tcl * 128:(tcl + 1) * 128]
                ev = OP("scalar", lambda e, o=dv, s=pv: e.activation(out=o, in_=s, func=AF.Silu), SE)
                bank_free[bank] = (SE, ev)
                v_ev_se[tcl] = ev
            v_done_se = SE.n

            W("vector", S_dq[sdq], dqld)
            for b in range(NB):
                sT_ve = {}
                kvmm_te = {}
                for hh in range(2):
                    h = hg * 2 + hh
                    sb = (4, 5) if hh == 0 else (0, 1)
                    for nh in range(2):
                        bank = sb[nh]
                        bank_wait("tensor", bank)
                        OP("tensor", lambda e, p=PS[bank][:, 0:256],
                           l=gsl(g_kf, hh, b * 256 + nh * 128, 128),
                           r=gsl(g_q, hh, b * 256, 256): e.matmul(p, l, r, start=True, stop=True),
                           TE if nh == 1 else None)
                    sT_ve[hh] = TE.n
                W("tensor", SE, v_ev_se[2 * b + 1])
                for hh in range(2):
                    kb = 7 if hh == 0 else 3
                    bank_wait("tensor", kb)
                    tK = None
                    for u in range(2):
                        tK = OP("tensor", lambda e, p=PS[kb][:, 0:128],
                                l=gsl(g_ks, hh, (2 * b + u) * 128, 128),
                                r=gsl(g_vt, hh, (2 * b + u) * 128, 128),
                                st=(u == 0), sp=(u == 1): e.matmul(p, l, r, start=st, stop=sp),
                                TE if u == 1 else None)
                    kvmm_te[hh] = tK
                sT_done = {}
                for hh in range(2):
                    sb = (4, 5) if hh == 0 else (0, 1)
                    W("vector", TE, sT_ve[hh])
                    W("vector", TE, sT_rd[hh])
                    for nh in range(2):
                        vv = OP("vector", lambda e, o=ap(o_scr_sT + (hh * 2 + nh) * 128, 128, BF16),
                                p=PS[sb[nh]][:, 0:256],
                                d=ap(o_scr_dds + sdq * 512, 512, FP16)[:, (hh * 2 + nh) * 256:(hh * 2 + nh + 1) * 256]:
                                e.tensor_tensor(o, p, d, AL.mult), VE)
                        bank_free[sb[nh]] = (VE, vv)
                    sT_done[hh] = VE.n
                for hh in range(2):
                    h = hg * 2 + hh
                    ob = 6 if hh == 0 else 2
                    bank_wait("tensor", ob)
                    W("tensor", VE, sT_done[hh])
                    if b == 0:
                        kv_cur_b = ap(o_scr_kvini + h * 64, 64, BF16)
                        if hg == 0:
                            W("tensor", SE, kv_init_done_se)
                    else:
                        kv_cur_b = ap(o_scr_kvppb + (hh * 2 + (b - 1) % 2) * 64, 64, BF16)
                        W("tensor", VE, kv_ve.get(hh))
                    for nh in range(2):
                        OP("tensor", lambda e, p=PS[ob][:, 0:256],
                           l=gsl(g_vt, hh, (2 * b + nh) * 128, 128),
                           r=ap(o_scr_sT + (hh * 2 + nh) * 128, 128, BF16),
                           st=(nh == 0): e.matmul(p, l, r, start=st, stop=False))
                    tO = OP("tensor", lambda e, p=PS[ob][:, 0:256], l=kv_cur_b,
                            r=gsl(g_qs, hh, b * 256, 256): e.matmul(p, l, r, start=False, stop=True), TE)
                    sT_rd[hh] = tO
                    W("scalar", TE, tO)
                    odst = ap(o_ofm, 8192, BF16).rearrange("p (hx t) -> p hx t", hx=16)[:, h, b * 256:(b + 1) * 256]
                    evO = OP("scalar", lambda e, o=odst, p=PS[ob][:, 0:256]: e.copy(o, p), SE)
                    bank_free[ob] = (SE, evO)
                for hh in range(2):
                    h = hg * 2 + hh
                    kb = 7 if hh == 0 else 3
                    W("vector", TE, kvmm_te[hh])
                    if b == 0:
                        kv_cur_x = ap(o_scr_kvini + h * 64, 64, BF16)
                    else:
                        kv_cur_x = ap(o_scr_kvpp + (hh * 2 + (b - 1) % 2) * 128, 128)
                    kv_nxt = ap(o_scr_kvpp + (hh * 2 + b % 2) * 128, 128)
                    kv_nxt_b = ap(o_scr_kvppb + (hh * 2 + b % 2) * 64, 64, BF16)
                    DRAIN("vector")
                    OP("vector", lambda e, o=kv_nxt, i=kv_cur_x, p=PS[kb][:, 0:128],
                       bd=bdec[h]: e.scalar_tensor_tensor(o, i, bd, p, AL.mult, AL.add), VE)
                    DRAIN("vector")
                    vKb = OP("vector", lambda e, o=kv_nxt_b, i=kv_nxt: e.tensor_copy(o, i), VE)
                    kv_ve[hh] = vKb
                    bank_free[kb] = (VE, vKb)
            dq_slot_ve[sdq] = VE.n
        attn_se = SE.n
        if stop_phase == "attn":
            return

        ofm_h = ap(o_ofm, 8192, BF16)
        o2_slot_te = {}
        st_te = None
        for fc in range(FC):
            slot = fc % 2
            o2 = ap(o_o2 + slot * 512, 512, BF16)
            W("scalar", TE, o2_slot_te.get(slot))
            if fc < 2:
                W("scalar", TE, lnt_slot_te.get(fc % 2))
            sq = OP("scalar", lambda e, o=o2, i=ofm_h[:, fc * 1024:(fc + 1) * 1024]:
                    e.activation(out=o, in_=i, func=AF.Square), SE)
            W("tensor", SE, sq)
            if fc == 0:
                bank_wait("tensor", 0); bank_wait("tensor", 1)
            for half in range(2):
                st_te = OP("tensor", lambda e, p=PS[half][0:1, 0:512], l=ap(o_ones, 64, BF16)[:, 0:1],
                           r=o2[:, half * 512:(half + 1) * 512], st=(fc == 0), sp=(fc == FC - 1):
                           e.matmul(p, l, r, start=st, stop=sp), TE if half == 1 else None)
            o2_slot_te[slot] = st_te
        W("scalar", TE, st_te)
        srow = ap(o_srow, 1024, parts=(0, 1))
        sc1 = None
        for half in range(2):
            sc1 = OP("scalar", lambda e, o=srow[:, half * 512:(half + 1) * 512], p=PS[half][0:1, 0:512]:
                     e.copy(o, p), SE if half == 1 else None)
        bank_free[0] = (SE, sc1); bank_free[1] = (SE, sc1)
        W("vector", SE, sc1)
        OP("vector", lambda e, o=srow, s=1.0 / D, b=EPS_RMS: e.tensor_scalar(o, o, s, b, AL.mult, AL.add))
        DRAIN("vector")
        vr = OP("vector", lambda e, o=srow: e.reciprocal(o, o), VE)
        W("scalar", VE, vr)
        srow_b = ap(o_srow + 1024, 512, BF16, parts=(0, 1))
        sq2 = OP("scalar", lambda e, o=srow_b, i=srow:
                 e.activation(out=o, in_=i, func=AF.Sqrt), SE)
        W("tensor", SE, sq2)
        bank_wait("tensor", 4); bank_wait("tensor", 5)
        bc_te = None
        for half in range(2):
            bc_te = OP("tensor", lambda e, p=PS[4 + half][:, 0:512], l=ap(o_ones, 64, BF16)[0:1, 0:128],
                       r=srow_b[:, half * 512:(half + 1) * 512]:
                       e.matmul(p, l, r, start=True, stop=True), TE if half == 1 else None)
        W("vector", TE, bc_te)

        ogt_h = ap(o_ogt, 8192, BF16)
        wg_slot_te = {0: qk_res_te.get(0), 1: qk_res_te.get(0)}
        gv_prev = None
        for fc in range(FC):
            slot = fc % 2
            W("gpsimd", TE, wg_slot_te.get(slot))
            ld = OP("gpsimd", lambda e, d=ap(o_qkw + slot * 1024, 1024, BF16), s=w_g[fc]:
                    e.dma_start(out=d, in_=s), S_qk[slot], 16)
            W("tensor", S_qk[slot], ld)
            t1 = None
            for th in range(2):
                bank = slot * 2 + th
                bank_wait("tensor", bank)
                for f in range(FC):
                    lhs = ap(o_qkw + slot * 1024, 1024, BF16)[:, f * 128:(f + 1) * 128]
                    t1 = OP("tensor", lambda e, p=PS[bank][:, 0:512], l=lhs,
                            r=xf[:, f, th * 512:(th + 1) * 512],
                            st=(f == 0), sp=(f == FC - 1): e.matmul(p, l, r, start=st, stop=sp),
                            TE if (th == 1 and f == FC - 1) else None)
            wg_slot_te[slot] = t1
            W("scalar", TE, t1)
            W("scalar", VE, gv_prev)
            sg = None
            for th in range(2):
                sg = OP("scalar", lambda e, o=ap(o_sil, 1024)[:, th * 512:(th + 1) * 512],
                        p=PS[slot * 2 + th][:, 0:512]: e.activation(out=o, in_=p, func=AF.Sigmoid),
                        SE if th == 1 else None)
            bank_free[slot * 2] = (SE, sg); bank_free[slot * 2 + 1] = (SE, sg)
            W("vector", SE, sg)
            W("vector", CONST, const_n)
            for th in range(2):
                OP("vector", lambda e, o=ap(o_sil + 1024, 1024)[:, th * 512:(th + 1) * 512],
                   i=ofm_h[:, fc * 1024 + th * 512: fc * 1024 + (th + 1) * 512],
                   nw=ap(o_normw, FC)[:, fc:fc + 1], p=PS[4 + th][:, 0:512]:
                   e.scalar_tensor_tensor(o, i, nw, p, AL.mult, AL.mult))
            DRAIN("vector")
            gv = None
            for th in range(2):
                gv = OP("vector", lambda e, o=ogt_h[:, fc * 1024 + th * 512: fc * 1024 + (th + 1) * 512],
                        a=ap(o_sil + 1024, 1024)[:, th * 512:(th + 1) * 512],
                        b=ap(o_sil, 1024)[:, th * 512:(th + 1) * 512]: e.tensor_tensor(o, a, b, AL.mult),
                        VE if th == 1 else None)
            gv_prev = gv
        ogt_done_ve = VE.n
        bank_free[4] = (VE, ogt_done_ve); bank_free[5] = (VE, ogt_done_ve)
        if stop_phase == "c4":
            return

        x2b = ap(o_x2b, 8192, BF16)
        x2f = x2b.rearrange("p (f t) -> p f t", f=16)
        d0_slot_te = {0: lnt_slot_te.get(0), 1: lnt_slot_te.get(1)}
        W("tensor", VE, ogt_done_ve)
        wow_slot_te = {}
        at4_ve = None
        xresb = ap(o_xres, 8192, BF16)
        for tg in range(2):
            for oc in range(4):
                par = oc % 2
                wbase = o_qkw if par == 0 else o_kvw
                W("gpsimd", TE, wow_slot_te.get(par))
                if tg == 0 and oc == 0:
                    W("gpsimd", TE, wg_slot_te.get(0))
                    W("gpsimd", TE, wg_slot_te.get(1))
                if tg == 0 and oc == 1:
                    W("gpsimd", TE, dummy_kv_guard)
                wld1 = OP("gpsimd", lambda e, d=ap(wbase, 2048, BF16), s=w_o[oc, :, 0:4096]:
                          e.dma_start(out=d, in_=s), S_wo[par], 16)
                wld2 = OP("gpsimd", lambda e, d=ap(wbase + 2048, 2048, BF16), s=w_o[oc, :, 4096:8192]:
                          e.dma_start(out=d, in_=s), S_wo2[par], 16)
                te_oc = None
                for hp in range(2):
                    W("tensor", (S_wo if hp == 0 else S_wo2)[par], wld1 if hp == 0 else wld2)
                    for t in range(4):
                        bank = 4 + t
                        if hp == 0:
                            bank_wait("tensor", bank)
                        for hd in range(hp * 8, hp * 8 + 8):
                            lhs = ogt_h[:, hd * 1024 + (tg * 4 + t) * 128: hd * 1024 + (tg * 4 + t) * 128 + 128]
                            te_oc = OP("tensor", lambda e, p=PS[bank][:, 0:512], l=lhs,
                                       r=ap(wbase, 4096, BF16)[:, hd * 512:(hd + 1) * 512],
                                       st=(hd == 0), sp=(hd == FC - 1): e.matmul(p, l, r, start=st, stop=sp),
                                       TE if hd == hp * 8 + 7 else None)
                wow_slot_te[par] = te_oc
                W("scalar", TE, te_oc)
                if tg == 1 and oc == 0:
                    W("scalar", VE, at4_ve)
                if tg == 0 and oc == 0:
                    W("scalar", VE, ogt_done_ve)
                ev = None
                for t in range(4):
                    bank = 4 + t
                    dst = ap(o_at4 + tg * 4096, 4096, BF16)[:, t * 2048 + oc * 512: t * 2048 + (oc + 1) * 512]
                    ev = OP("scalar", lambda e, o=dst, p=PS[bank][:, 0:512]: e.copy(o, p),
                            SE if t == 3 else None)
                for t in range(4):
                    bank_free[4 + t] = (SE, ev)
            at4_se = SE.n
            for t in range(4):
                tt = tg * 4 + t
                slot = tt % 2
                xt = ap(o_xt + slot * 2048, 2048)
                W("sync", VE, xt_slot_ve.get(slot))
                gx = OP("sync", lambda e, d=xt, s=x_own[tt]: e.dma_start(out=d, in_=s), S_xt[slot], 16)
                W("vector", SE, at4_se)
                at = ap(o_at4 + tg * 4096, 4096, BF16)[:, t * 2048:(t + 1) * 2048]
                ln_tile(at, ap(o_nrm, 2048), EPS_LN)
                W("vector", S_xt[slot], gx)
                DRAIN("vector")
                vres = OP("vector", lambda e, o=xresb[:, tt * 2048:(tt + 1) * 2048],
                          a=ap(o_nrm, 2048), b=xt: e.tensor_tensor(o, a, b, AL.add), VE)
                xt_slot_ve[slot] = vres
            at4_ve = VE.n
            if tg == 0:
                W("scalar", TE, wg_slot_te[(FC - 1) % 2])
                for t in range(4):
                    tt = t
                    slot = tt % 2
                    xo = ap(o_xlntm + slot * 1024, 1024, BF16)
                    W("vector", TE, d0_slot_te.get(slot))
                    if t < 2:
                        W("vector", TE, o2_slot_te.get(slot))
                    vdone = ln_tile(xresb[:, tt * 2048:(tt + 1) * 2048], xo, EPS_LN)
                    W("tensor", VE, vdone)
                    transpose_tile(xo, x2b, tt, (2, 3))
                    d0_slot_te[slot] = TE.n
                ln2_half_se = SE.n
        xres_done_ve = VE.n

        def emit_d0_second_half():
            for t in range(4):
                tt = 4 + t
                slot = tt % 2
                xo = ap(o_xlntm + slot * 1024, 1024, BF16)
                W("vector", TE, d0_slot_te.get(slot))
                vdone = ln_tile(xresb[:, tt * 2048:(tt + 1) * 2048], xo, EPS_LN)
                W("tensor", VE, vdone)
                transpose_tile(xo, x2b, tt, (6, 7))
                d0_slot_te[slot] = TE.n
        if stop_phase == "c5":
            return

        if stop_phase == "d0":
            return
        hsb = ap(o_hsb, 16384, BF16)
        fft_tc_se = {}
        gu_slot_te = {}
        dw_slot_te = {}
        sil_slot_ve = {}
        out_lg = {}
        for th in range(2):
            for ic in range(ICN):
                slot = ic % 2
                W("gpsimd", TE, gu_slot_te.get(slot))
                if th == 1 and ic < 2:
                    W("gpsimd", TE, dw_slot_te.get(0))
                    W("gpsimd", TE, dw_slot_te.get(1))
                if th == 0 and ic < 2:
                    W("gpsimd", TE, wow_slot_te.get(0))
                    W("gpsimd", TE, wow_slot_te.get(1))
                if th == 1 and ic < 2:
                    W("gpsimd", TE, dw_slot_te.get(slot))
                ldg = OP("gpsimd", lambda e, d=ap(o_gu + slot * 1024, 1024, BF16), s=w_gate[ic]:
                         e.dma_start(out=d, in_=s), S_gu[slot], 16)
                ldu = OP("gpsimd", lambda e, d=ap(o_gu2 + slot * 1024, 1024, BF16), s=w_up[ic]:
                         e.dma_start(out=d, in_=s), S_gu2[slot], 16)
                W("tensor", S_gu[slot], ldg)
                W("tensor", S_gu2[slot], ldu)
                if th == 0 and ic == 0:
                    W("tensor", SE, ln2_half_se)
                if th == 1 and ic == 0:
                    W("tensor", SE, ln2_se)
                bga = slot * 2
                bgb = 1 + slot * 2
                bank_wait("tensor", bga)
                tg1 = None
                for f in range(FC):
                    lhs = ap(o_gu + slot * 1024, 1024, BF16)[:, f * 128:(f + 1) * 128]
                    tg1 = OP("tensor", lambda e, p=PS[bga][:, 0:512], l=lhs,
                             r=x2f[:, f, th * 512:(th + 1) * 512],
                             st=(f == 0), sp=(f == FC - 1): e.matmul(p, l, r, start=st, stop=sp),
                             TE if f == FC - 1 else None)
                bank_wait("tensor", bgb)
                tg2 = None
                for f in range(FC):
                    lhs = ap(o_gu2 + slot * 1024, 1024, BF16)[:, f * 128:(f + 1) * 128]
                    tg2 = OP("tensor", lambda e, p=PS[bgb][:, 0:512], l=lhs,
                             r=x2f[:, f, th * 512:(th + 1) * 512],
                             st=(f == 0), sp=(f == FC - 1): e.matmul(p, l, r, start=st, stop=sp),
                             TE if f == FC - 1 else None)
                gu_slot_te[slot] = tg2
                if th == 0 and ic == 3:
                    emit_d0_second_half()
                    ln2_se = SE.n
                W("scalar", TE, tg1)
                W("scalar", VE, sil_slot_ve.get(slot))
                sil = ap(o_sil + slot * 512, 512)
                sev = OP("scalar", lambda e, o=sil, p=PS[bga][:, 0:512]:
                         e.activation(out=o, in_=p, func=AF.Silu), SE)
                bank_free[bga] = (SE, sev)
                W("vector", TE, tg2)
                W("vector", SE, sev)
                hv = OP("vector", lambda e, o=hsb[:, ic * 512:(ic + 1) * 512], p=PS[bgb][:, 0:512], s=sil:
                        e.tensor_tensor(o, p, s, AL.mult), VE)
                sil_slot_ve[slot] = hv
                bank_free[bgb] = (VE, hv)
            h_done_ve = VE.n
            W("tensor", VE, h_done_ve)
            for oh in range(2):
                dte = None
                for icq in range(16):
                    grp = icq % 2
                    doff = o_kvw if grp == 0 else o_gu
                    W("gpsimd", TE, dw_slot_te.get(grp))
                    if oh == 0 and icq < 2:
                        W("gpsimd", TE, gu_slot_te.get(0))
                        W("gpsimd", TE, gu_slot_te.get(1))
                    dld = OP("gpsimd", lambda e, d=ap(doff, 2048, BF16), s=w_down[oh, icq]:
                             e.dma_start(out=d, in_=s), S_dw[grp], 16)
                    for jp in range(2):
                        W("tensor", S_dw[grp], dld)
                        for b in range(8):
                            tc4, oc = b // 2, b % 2
                            if icq == 0 and jp == 0:
                                bank_wait("tensor", b)
                            for j in (jp * 2, jp * 2 + 1):
                                ic = icq * 4 + j
                                lhs = hsb[:, ic * 512 + tc4 * 128: ic * 512 + tc4 * 128 + 128]
                                rhs = ap(doff + j * 512, 512, BF16)[:, oc * 512:(oc + 1) * 512]
                                dte = OP("tensor", lambda e, p=PS[b][:, 0:512], l=lhs, r=rhs,
                                         st=(ic == 0), sp=(ic == ICN - 1): e.matmul(p, l, r, start=st, stop=sp),
                                         TE if (b == 7 and j == jp * 2 + 1) else None)
                    dw_slot_te[grp] = dte
                W("scalar", TE, dte)
                if th == 0 and oh == 0:
                    W("scalar", TE, wg_slot_te[(FC - 1) % 2])
                ev = None
                for b in range(8):
                    tc4, oc = b // 2, b % 2
                    dst = ap(o_fft, 4096, BF16)[:, tc4 * 2048 + oh * 1024 + oc * 512:
                                                tc4 * 2048 + oh * 1024 + (oc + 1) * 512]
                    ev = OP("scalar", lambda e, o=dst, p=PS[b][:, 0:512]: e.copy(o, p), SE)
                    if oh == 1:
                        fft_tc_se[tc4] = ev
                for b in range(8):
                    bank_free[b] = (SE, ev)
            for tc4 in range(4):
                tt = th * 4 + tc4
                slot = tt % 2
                W("vector", SE, fft_tc_se[tc4])
                ft = ap(o_fft, 4096, BF16)[:, tc4 * 2048:(tc4 + 1) * 2048]
                ln_tile(ft, ap(o_nrm, 2048), EPS_LN)
                DRAIN("vector")
                ot = ap(o_outt + slot * 2048, 2048)
                if tt >= 2:
                    W("vector", S_out[slot], out_lg[tt - 2])
                else:
                    W("vector", VE, xt_slot_ve.get(slot))
                vo = OP("vector", lambda e, o=ot, a=ap(o_nrm, 2048),
                        b=xresb[:, tt * 2048:(tt + 1) * 2048]: e.tensor_tensor(o, a, b, AL.add), VE)
                W("gpsimd", VE, vo)
                out_lg[tt] = OP("gpsimd", lambda e, d=out_d[tt], s=ot: e.dma_start(out=d, in_=s), S_out[slot], 16)

        W("gpsimd", S_out[0], S_out[0].n)
        W("gpsimd", S_out[1], S_out[1].n)

    with (
        nc.sbuf_tensor("AR", [128, AWORDS], F32) as AR_t,
        nc.psum_tensor("p0", [128, 512], F32) as p0,
        nc.psum_tensor("p1", [128, 512], F32) as p1,
        nc.psum_tensor("p2", [128, 512], F32) as p2,
        nc.psum_tensor("p3", [128, 512], F32) as p3,
        nc.psum_tensor("p4", [128, 512], F32) as p4,
        nc.psum_tensor("p5", [128, 512], F32) as p5,
        nc.psum_tensor("p6", [128, 512], F32) as p6,
        nc.psum_tensor("p7", [128, 512], F32) as p7,
        nc.Block() as block,
    ):
        import contextlib
        with contextlib.ExitStack() as stack:
            for sm in SEMS:
                sm.h = stack.enter_context(nc.semaphore(sm.name))
            ARh[0] = AR_t.ap()
            PS.extend([p.ap() for p in (p0, p1, p2, p3, p4, p5, p6, p7)])
            logical()
            _emit_all(block, prog)
    return nc


def _emit_all(block, prog):
    if True:

        @block.sync
        def _(e):
            for fn in prog["sync"]:
                fn(e)

        @block.tensor
        def _(e):
            for fn in prog["tensor"]:
                fn(e)

        @block.vector
        def _(e):
            for fn in prog["vector"]:
                fn(e)

        @block.scalar
        def _(e):
            for fn in prog["scalar"]:
                fn(e)

        @block.gpsimd
        def _(e):
            for fn in prog["gpsimd"]:
                fn(e)



def _host_prep(L0_attn_x, L0_attn_Wqkv, L0_attn_Wg, L0_attn_Wout, L0_attn_norm_w,
               L0_ffn_gate_w, L0_ffn_up_w, L0_ffn_down_w):
    import ml_dtypes
    f32 = np.float32
    slopes = np.array(_slopes(), dtype=np.float64)
    arr = np.arange(1, BLK + 1, dtype=np.float64)

    qdec = np.exp(-slopes[:, None] * arr[None, :])
    kdec = np.exp(-slopes[:, None] * (BLK - arr[None, :]))
    idx = arr[:, None] - arr[None, :]
    dd = np.where(idx[None] >= 0, np.exp(-slopes[:, None, None] * idx[None]), 0.0)

    qdec_in = np.ascontiguousarray(np.broadcast_to(
        qdec.reshape(NHG, 2, 1, BLK), (NHG, 2, 128, BLK)).transpose(0, 2, 1, 3).reshape(
        NHG, 128, 512)).astype(np.float16)
    kdec_in = np.ascontiguousarray(kdec.reshape(H, 2, 128).transpose(2, 0, 1).reshape(128, 2 * H)).astype(f32)
    bd = np.exp(-slopes * BLK)
    kdp = np.zeros((128, 128))
    for h in range(H):
        for b in range(4):
            for u in range(2):
                kdp[:, h * 8 + b * 2 + u] = kdec[h, u * 128:(u + 1) * 128] * bd[h] ** (3 - b)
    kdp_in = np.ascontiguousarray(kdp).astype(f32)
    ddt = dd.transpose(0, 2, 1)
    ddt_in = np.ascontiguousarray(ddt.reshape(NHG, 2, 2, 128, BLK).transpose(0, 3, 1, 2, 4).reshape(
        NHG, 128, 1024)).astype(np.float16)

    Wqkv = np.asarray(L0_attn_Wqkv, dtype=f32)
    qk = np.empty((NHG, 4, FC, 128, 128), dtype=f32)
    vt = np.empty((NHG, FC, 128, 256), dtype=f32)
    kv = np.empty((NHG, FC, 128, 512), dtype=f32)
    Wq3 = Wqkv.reshape(FC, 128, 6144)
    for g in range(NHG):
        for hh in range(2):
            h = 2 * g + hh
            base = 384 * h
            qk[g, 2 * hh + 0] = Wq3[:, :, base:base + 128]
            qk[g, 2 * hh + 1] = Wq3[:, :, base + 128:base + 256]
            vt[g, :, :, hh * 128:(hh + 1) * 128] = Wq3[:, :, base + 256:base + 384]
            kv[g, :, :, hh * 256:hh * 256 + 128] = Wq3[:, :, base + 128:base + 256]
            kv[g, :, :, hh * 256 + 128:hh * 256 + 256] = Wq3[:, :, base + 256:base + 384]

    qk = np.ascontiguousarray(
        qk.transpose(0, 1, 3, 2, 4).reshape(NHG, 4, 128, FC * 128).transpose(0, 2, 1, 3).reshape(
            NHG, 128, 4 * FC * 128))
    kv = np.ascontiguousarray(
        kv.reshape(NHG, 4, 4, 128, 512).transpose(0, 1, 3, 2, 4).reshape(NHG, 4, 128, 2048))
    vt = np.ascontiguousarray(vt.transpose(0, 2, 1, 3).reshape(NHG, 128, FC * 256))
    Wg = np.asarray(L0_attn_Wg, dtype=f32).reshape(FC, 128, FC, 128)
    wg_in = np.ascontiguousarray(Wg.transpose(2, 1, 0, 3).reshape(FC, 128, FC * 128))
    Wo = np.asarray(L0_attn_Wout, dtype=f32).reshape(FC, 128, 4, 512)
    wo_in = np.ascontiguousarray(Wo.transpose(2, 1, 0, 3).reshape(4, 128, FC * 512))

    gT = np.asarray(L0_ffn_gate_w, dtype=f32).T.reshape(FC, 128, ICN, 128)
    gw_in = np.ascontiguousarray(gT.transpose(2, 1, 0, 3).reshape(ICN, 128, FC * 128))
    uT = np.asarray(L0_ffn_up_w, dtype=f32).T.reshape(FC, 128, ICN, 128)
    uw_in = np.ascontiguousarray(uT.transpose(2, 1, 0, 3).reshape(ICN, 128, FC * 128))
    dT = np.asarray(L0_ffn_down_w, dtype=f32).T.reshape(ICN, 128, 2, 1024)
    dw_in = np.ascontiguousarray(
        dT.transpose(2, 0, 1, 3).reshape(2, 16, 4, 128, 1024).transpose(0, 1, 3, 2, 4).reshape(
            2, 16, 128, 4096))

    nw = np.asarray(L0_attn_norm_w, dtype=f32).reshape(FC, 128)
    nw_in = np.ascontiguousarray(nw.T)

    consts = {
        "id_b": np.eye(128, dtype=f32).astype(ml_dtypes.bfloat16),
        "ones_r": np.ones((128, 128), dtype=f32).astype(ml_dtypes.bfloat16),
        "kdec": kdec_in,
        "kdecp": kdp_in,
        "normw": nw_in,
        "qdec": qdec_in,
        "ddt": ddt_in,
        "w_qk": qk, "w_v": vt, "w_kv": kv, "w_g": wg_in, "w_o": wo_in,
        "w_gate": gw_in, "w_up": uw_in, "w_down": dw_in,
    }

    x = np.asarray(L0_attn_x, dtype=f32)
    in_maps = []
    for c in range(8):
        s, half = c // 2, c % 2
        xo = np.ascontiguousarray(x[s, half * T:(half + 1) * T].reshape(TC, 128, D))
        if half == 1:
            xp = np.ascontiguousarray(x[s, 0:T].reshape(TC, 128, D))
            mk = np.ones((128, 1), dtype=f32)
        else:
            xp = xo
            mk = np.zeros((128, 1), dtype=f32)
        m = dict(consts)
        m["x_own"] = xo
        m["x_prev"] = xp
        m["maskc"] = mk
        in_maps.append(m)
    return in_maps


_NC_CACHE = [None]


def kernel(L0_attn_x, L0_attn_Wqkv, L0_attn_Wg, L0_attn_Wout, L0_attn_norm_w,
           L0_ffn_gate_w, L0_ffn_up_w, L0_ffn_down_w):
    in_maps = _host_prep(L0_attn_x, L0_attn_Wqkv, L0_attn_Wg, L0_attn_Wout, L0_attn_norm_w,
                         L0_ffn_gate_w, L0_ffn_up_w, L0_ffn_down_w)
    if _NC_CACHE[0] is None:
        _NC_CACHE[0] = build_program()
    nc = _NC_CACHE[0]
    res = run_bass_kernel_spmd(nc, in_maps, core_ids=list(range(8)))
    out = np.empty((4, 2048, D), dtype=np.float32)
    for c in range(8):
        s, half = c // 2, c % 2
        out[s, half * T:(half + 1) * T] = np.asarray(res.results[c]["out"]).reshape(T, D)
    return out


# version 23
# speedup vs baseline: 1.0611x; 1.0611x over previous
import math
import numpy as np
import concourse.bass as bass
import concourse.mybir as mybir
from concourse.bass_utils import run_bass_kernel_spmd

F32 = mybir.dt.float32
F32R = mybir.dt.float32r
BF16 = mybir.dt.bfloat16
FP16 = mybir.dt.float16
AL = mybir.AluOpType
AF = mybir.ActivationFunctionType

T = 1024
D = 2048
H = 16
BLK = 256
NB = T // BLK
TC = T // 128
FC = D // 128
ICN = 64
NHG = 8
EPS_LN = 1e-5
EPS_RMS = 1e-6

ENGS = ("sync", "tensor", "vector", "scalar", "gpsimd")


def _slopes():
    start = 2 ** (-2 ** -(math.log2(H) - 3))
    return [start * start ** i for i in range(H)]


def build_program(stop_phase=None):
    nc = bass.Bass("TRN2", target_bir_lowering=False)
    slopes = _slopes()
    bdec = [float(np.exp(np.float32(-s * BLK))) for s in slopes]

    x_own = nc.dram_tensor("x_own", [TC, 128, D], F32, kind="ExternalInput").ap()
    x_prev = nc.dram_tensor("x_prev", [TC, 128, D], F32, kind="ExternalInput").ap()
    maskc = nc.dram_tensor("maskc", [128, 1], F32, kind="ExternalInput").ap()
    id_b = nc.dram_tensor("id_b", [128, 128], BF16, kind="ExternalInput").ap()
    ones_r = nc.dram_tensor("ones_r", [128, 128], BF16, kind="ExternalInput").ap()
    kdec_d = nc.dram_tensor("kdec", [128, 2 * H], F32, kind="ExternalInput").ap()
    kdecp_d = nc.dram_tensor("kdecp", [128, 128], F32, kind="ExternalInput").ap()
    normw_d = nc.dram_tensor("normw", [128, FC], F32, kind="ExternalInput").ap()
    qdec_d = nc.dram_tensor("qdec", [NHG, 128, 512], FP16, kind="ExternalInput").ap()
    ddt_d = nc.dram_tensor("ddt", [NHG, 128, 1024], FP16, kind="ExternalInput").ap()
    w_kv = nc.dram_tensor("w_kv", [NHG, 4, 128, 2048], F32, kind="ExternalInput").ap()
    w_qk = nc.dram_tensor("w_qk", [NHG, 128, 4 * FC * 128], F32, kind="ExternalInput").ap()
    w_v = nc.dram_tensor("w_v", [NHG, 128, FC * 256], F32, kind="ExternalInput").ap()
    w_g = nc.dram_tensor("w_g", [FC, 128, FC * 128], F32, kind="ExternalInput").ap()
    w_o = nc.dram_tensor("w_o", [4, 128, FC * 512], F32, kind="ExternalInput").ap()
    w_gate = nc.dram_tensor("w_gate", [ICN, 128, FC * 128], F32, kind="ExternalInput").ap()
    w_up = nc.dram_tensor("w_up", [ICN, 128, FC * 128], F32, kind="ExternalInput").ap()
    w_down = nc.dram_tensor("w_down", [2, ICN, 128, 1024], F32, kind="ExternalInput").ap()
    out_d = nc.dram_tensor("out", [TC, 128, D], F32, kind="ExternalOutput").ap()

    AWORDS = 53120
    _ptr = [0]
    def region(words, align=128):
        p = (_ptr[0] + align - 1) // align * align
        assert p + words <= AWORDS, f"arena overflow {p}+{words}"
        _ptr[0] = p + words
        return p

    o_idb = region(64)
    o_ones = region(128)
    o_kdec = region(32)
    o_kdecp = region(128)
    o_normw = region(16)
    o_mask = region(8)
    o_stats = region(64)
    o_mv = region(16)
    o_inv = region(16)
    o_xln = region(8192)
    o_ofm = region(8192)
    o_grp = region(8192)
    o_scr = region(8192)
    o_xt = region(4096)
    o_xlntm = region(2048)
    o_nrm = region(2048)
    o_qkw = region(4096)
    o_kvw = region(2048)
    o_gu = region(2048)
    o_sil = region(2048)
    o_vw = region(512)

    o_srow = o_nrm
    o_at4 = o_ofm
    o_x2b = o_xln
    o_hsb = o_ofm
    o_ogt = o_grp
    o_scr_qdec = o_scr
    o_scr_dds = o_scr + 512
    o_scr_kvini = o_scr + 1536
    o_scr_kvpp = o_scr + 2560
    o_scr_kvppb = o_scr + 3072
    o_scr_sT = o_scr + 3328
    o_scr_ksc = o_scr + 3584
    o_scr_vsc = o_scr + 3840
    o_pro_ksc = o_scr + 4096
    o_kvp = o_nrm
    o_xres = o_scr
    o_o2 = o_xlntm
    o_fft = o_qkw
    o_gu2 = o_kvw
    o_outt = o_xt
    o_wow = o_vw

    prog = {e: [] for e in ENGS}
    last_wait = {e: {} for e in ENGS}

    class Sem:
        def __init__(self, name):
            self.name = name
            self.n = 0
            self.h = None

    TE, VE, SE = Sem("TE"), Sem("VE"), Sem("SE")
    SEMS = [TE, VE, SE]

    def mksem(name):
        sm = Sem(name)
        SEMS.append(sm)
        return sm

    CONST = mksem("cst")
    S_xt = [mksem("sxt0"), mksem("sxt1")]
    S_kv = [mksem("skv0"), mksem("skv1")]
    S_kv2 = [mksem("skv20"), mksem("skv21")]
    S_dq = [mksem("sdq0"), mksem("sdq1")]
    S_qk = [mksem("sqk0"), mksem("sqk1")]
    S_vw = [mksem("svw0"), mksem("svw1")]
    S_wo = [mksem("swo0"), mksem("swo1")]
    S_gu = [mksem("sgu0"), mksem("sgu1")]
    S_gu2 = [mksem("sgu20"), mksem("sgu21")]
    S_dw = [mksem("sdw0"), mksem("sdw1")]
    S_dw2 = [mksem("sdw20"), mksem("sdw21")]
    S_wo2 = [mksem("swo20"), mksem("swo21")]
    S_out = [mksem("sou0"), mksem("sou1")]

    def E(eng, fn):
        prog[eng].append(fn)

    def W(eng, sem, val):
        if val is None or val <= 0:
            return
        assert val <= sem.n, f"wait {sem.name}>={val} but only {sem.n} planned"
        if last_wait[eng].get(sem.name, -1) >= val:
            return
        last_wait[eng][sem.name] = val
        E(eng, lambda e, s=sem, v=val: e.wait_ge(s.h, v))

    def OP(eng, fn, sem=None, k=1):
        if sem is None:
            E(eng, lambda e, f=fn: f(e))
            return None
        sem.n += k
        def run(e, f=fn, s=sem, kk=k):
            f(e).then_inc(s.h, kk)
        E(eng, run)
        return sem.n

    def DRAIN(eng):
        E(eng, lambda e: e.drain())

    ARh = [None]
    PS = []

    def ap(off, words, dtype=F32, parts=(0, 128)):
        a = ARh[0][parts[0]:parts[1], off:off + words]
        if dtype == F32:
            return a
        return a.bitcast(dtype)

    bank_free = {}

    def bank_wait(eng, bank):
        ent = bank_free.get(bank)
        if ent is not None:
            W(eng, ent[0], ent[1])

    def logical():
        for dst, src in (
            (ap(o_idb, 64, BF16), id_b),
            (ap(o_ones, 64, BF16), ones_r),
            (ap(o_kdec, 2 * H), kdec_d),
            (ap(o_kdecp, 128), kdecp_d),
            (ap(o_normw, FC), normw_d),
            (ap(o_mask, 1), maskc),
        ):
            OP("sync", lambda e, d=dst, s=src: e.dma_start(out=d, in_=s), CONST, 16)
        const_n = CONST.n

        ln_par = [0]
        def ln_tile(src_ap, out_ap, eps):
            ln_par[0] ^= 1
            pr = ln_par[0]
            st = ap(o_stats + pr * 32, 24); mv = ap(o_mv + pr * 8, 2); inv = ap(o_inv + pr * 8, 1)
            for g in range(4):
                OP("vector", lambda e, a=st[:, g * 6:(g + 1) * 6],
                   b=src_ap[:, g * 512:(g + 1) * 512]: e.bn_stats(a, b))
            DRAIN("vector")
            OP("vector", lambda e, a=mv, b=st: e.bn_aggr(a, b))
            DRAIN("vector")
            OP("vector", lambda e, a=inv, b=mv[:, 1:2], ep=eps: e.tensor_scalar(a, b, ep, None, AL.add))
            DRAIN("vector")
            v1 = OP("vector", lambda e, a=inv: e.reciprocal(a, a), VE)
            W("scalar", VE, v1)
            s1 = OP("scalar", lambda e, a=inv: e.activation(out=a, in_=a, func=AF.Sqrt), SE)
            W("vector", SE, s1)
            return OP("vector", lambda e, o=out_ap, x=src_ap, m=mv[:, 0:1], iv=inv:
                      e.tensor_scalar(o, x, m, iv, AL.subtract, AL.mult), VE)

        def transpose_tile(src_ap, dst_ap, tcol, banks):
            W("tensor", CONST, const_n)
            d3 = dst_ap.rearrange("p (f t) -> p f t", f=16)
            for q in range(4):
                bank = banks[q % len(banks)]
                bank_wait("tensor", bank)
                te = None
                for j in range(4):
                    f = q * 4 + j
                    pdst = PS[bank][:, j * 64:(j + 1) * 64].bitcast(BF16)
                    te = OP("tensor", lambda e, p=pdst, s=src_ap[:, f * 128:(f + 1) * 128],
                            i=ap(o_idb, 64, BF16): e.transpose(p, s, i),
                            TE if j == 3 else None)
                W("scalar", TE, te)
                psrc = PS[bank][:, 0:256].bitcast(BF16).rearrange("p (f c) -> p f c", f=4)
                dsl = d3[:, q * 4:(q + 1) * 4, tcol * 128:(tcol + 1) * 128]
                ev = OP("scalar", lambda e, o=dsl, s=psrc: e.copy(o, s), SE)
                bank_free[bank] = (SE, ev)

        xt_slot_ve = {}
        def ln_transpose_stream(x_dram, dst_ap, slot_te):
            for tc in range(TC):
                slot = tc % 2
                xt = ap(o_xt + slot * 2048, 2048)
                W("sync", VE, xt_slot_ve.get(slot))
                g1 = OP("sync", lambda e, d=xt, s=x_dram[tc]: e.dma_start(out=d, in_=s), S_xt[slot], 16)
                W("vector", S_xt[slot], g1)
                W("vector", TE, slot_te.get(slot))
                xo = ap(o_xlntm + slot * 1024, 1024, BF16)
                vdone = ln_tile(xt, xo, EPS_LN)
                xt_slot_ve[slot] = vdone
                W("tensor", VE, vdone)
                transpose_tile(xo, dst_ap, tc, (4, 5, 6, 7))
                slot_te[slot] = TE.n
            return SE.n

        xf_b = ap(o_xln, 8192, BF16)
        xf = xf_b.rearrange("p (f t) -> p f t", f=16)
        xp_b = ap(o_ofm, 8192, BF16)
        xp = xp_b.rearrange("p (f t) -> p f t", f=16)
        lnt_slot_te = {}
        lnp_se = ln_transpose_stream(x_prev, xp_b, lnt_slot_te)

        if stop_phase == "lnp":
            return
        kv_slot_te = {}
        pro_ksc_te = {}
        W("tensor", SE, lnp_se)
        for hg in range(NHG):
            mm_te = None
            for fq in range(4):
                grp = fq % 2
                W("gpsimd", TE, kv_slot_te.get(grp))
                ld1 = OP("gpsimd", lambda e, d=ap(o_kvw + grp * 1024, 512, BF16),
                         s=w_kv[hg, fq, :, 0:1024]: e.dma_start(out=d, in_=s), S_kv[grp], 16)
                ld2 = OP("gpsimd", lambda e, d=ap(o_kvw + grp * 1024 + 512, 512, BF16),
                         s=w_kv[hg, fq, :, 1024:2048]: e.dma_start(out=d, in_=s), S_kv2[grp], 16)
                for jp in range(2):
                    W("tensor", (S_kv if jp == 0 else S_kv2)[grp], ld1 if jp == 0 else ld2)
                    for b in range(NB):
                        for u in range(2):
                            bank = b * 2 + u
                            if fq == 0 and jp == 0:
                                bank_wait("tensor", bank)
                            lhs0 = b * 256 + u * 128
                            for j in (jp * 2, jp * 2 + 1):
                                f = fq * 4 + j
                                is_last = (b == NB - 1 and u == 1 and j == 3)
                                mm_te = OP("tensor", lambda e, p=PS[bank][:, 0:512],
                                           l=xp[:, f, lhs0:lhs0 + 128],
                                           r=ap(o_kvw + grp * 1024, 1024, BF16)[:, j * 512:(j + 1) * 512],
                                           st=(f == 0), sp=(f == FC - 1): e.matmul(p, l, r, start=st, stop=sp),
                                           TE if is_last else None)
                kv_slot_te[grp] = mm_te
            a_mm_te = None
            for b in range(NB):
                sdone = None
                W("scalar", CONST, const_n)
                for u in range(2):
                    bank = b * 2 + u
                    W("scalar", TE, kv_slot_te.get(0))
                    W("scalar", TE, kv_slot_te.get(1))
                    W("scalar", TE, pro_ksc_te.get("t"))
                    kvsil = ap(o_scr_ksc + u * 256, 256, BF16)
                    e1 = OP("scalar", lambda e, o=kvsil, p=PS[bank][:, 0:512]:
                            e.activation(out=o, in_=p, func=AF.Silu), SE)
                    bank_free[bank] = (SE, e1)
                    W("vector", SE, e1)
                    W("vector", TE, pro_ksc_te.get("t"))
                    for hh in range(2):
                        h = hg * 2 + hh
                        ksc = ap(o_pro_ksc + (u * 2 + hh) * 64, 64, BF16)
                        kcol = ap(o_kdecp, 128)[:, (h * 8 + b * 2 + u):(h * 8 + b * 2 + u) + 1]
                        sdone = OP("vector", lambda e, o=ksc, i=kvsil[:, hh * 256:hh * 256 + 128], sc=kcol:
                                   e.tensor_scalar(o, i, sc, None, AL.mult),
                                   VE if (u == 1 and hh == 1) else None)
                for hh in range(2):
                    bank = hh
                    if b == 0:
                        bank_wait("tensor", bank)
                    W("tensor", VE, sdone)
                    for u in range(2):
                        ksc = ap(o_pro_ksc + (u * 2 + hh) * 64, 64, BF16)
                        vsl = ap(o_scr_ksc + u * 256, 256, BF16)[:, hh * 256 + 128:hh * 256 + 256]
                        a_mm_te = OP("tensor", lambda e, p=PS[bank][:, 0:128], l=ksc, r=vsl,
                                     st=(b == 0 and u == 0), sp=(b == NB - 1 and u == 1):
                                     e.matmul(p, l, r, start=st, stop=sp),
                                     TE if (u == 1 and hh == 1) else None)
                pro_ksc_te["t"] = a_mm_te
            W("scalar", TE, a_mm_te)
            for hh in range(2):
                h = hg * 2 + hh
                vv = OP("scalar", lambda e, o=ap(o_scr_kvini + h * 64, 64, BF16), i=PS[hh][:, 0:128],
                        m=ap(o_mask, 1): e.activation(out=o, in_=i, func=AF.Copy, scale=m), SE)
                bank_free[hh] = (SE, vv)
        kv_init_done_se = SE.n

        pro_sil_se = SE.n
        if stop_phase == "prologue":
            return
        ln1_se = ln_transpose_stream(x_own, xf_b, lnt_slot_te)
        if stop_phase == "ln1":
            return
        g_q, g_qs, g_kf, g_ks, g_vt = (o_grp, o_grp + 1024, o_grp + 2048, o_grp + 3072, o_grp + 4096)

        def gsl(base, hh, lo, n):
            return ap(base, 1024, BF16)[:, hh * 1024 + lo: hh * 1024 + lo + n]

        qk_slot_te = {}
        qk_res_te = {}
        vw_slot_te = {}
        dq_slot_ve = {}
        sT_rd = {0: None, 1: None}
        kv_ve = {}
        dummy_kv_guard = kv_slot_te.get(0)
        W("tensor", SE, ln1_se)
        for hg in range(NHG):
            sdq = hg % 2
            if hg == 0:
                W("sync", SE, kv_init_done_se)
            W("sync", VE, dq_slot_ve.get(sdq))
            OP("sync", lambda e, d=ap(o_scr_dds + sdq * 512, 512, FP16), s=ddt_d[hg]:
               e.dma_start(out=d, in_=s), S_dq[sdq], 16)
            dqld = OP("sync", lambda e, d=ap(o_scr_qdec + sdq * 256, 256, FP16), s=qdec_d[hg]:
                      e.dma_start(out=d, in_=s), S_dq[sdq], 16)

            W("gpsimd", TE, qk_res_te.get(0))
            if hg == 0:
                W("gpsimd", TE, kv_slot_te.get(0))
            qk_ld = OP("gpsimd", lambda e, d=ap(o_qkw, 4096, BF16), s=w_qk[hg]:
                       e.dma_start(out=d, in_=s), S_qk[0], 16)
            W("tensor", S_qk[0], qk_ld)
            t1 = None
            for cc in range(4):
                hh, isk = cc // 2, cc % 2
                for th in range(2):
                    bank = (cc % 2) * 2 + th
                    bank_wait("tensor", bank)
                    for f in range(FC):
                        lhs = ap(o_qkw, 4096, BF16)[:, cc * 2048 + f * 128: cc * 2048 + (f + 1) * 128]
                        t1 = OP("tensor", lambda e, p=PS[bank][:, 0:512], l=lhs,
                                r=xf[:, f, th * 512:(th + 1) * 512],
                                st=(f == 0), sp=(f == FC - 1): e.matmul(p, l, r, start=st, stop=sp),
                                TE if f == FC - 1 else None)
                    W("scalar", TE, t1)
                    dst = gsl(g_kf if isk else g_q, hh, th * 512, 512)
                    ev = OP("scalar", lambda e, o=dst, p=PS[bank][:, 0:512]:
                            e.activation(out=o, in_=p, func=AF.Silu), SE)
                    bank_free[bank] = (SE, ev)
            qk_res_te[0] = t1
            qk_done_se = SE.n

            W("vector", S_dq[sdq], dqld)
            W("vector", SE, qk_done_se)
            for hh in range(2):
                for b in range(NB):
                    OP("vector", lambda e, o=gsl(g_qs, hh, b * 256, 256),
                       a=gsl(g_q, hh, b * 256, 256),
                       q=ap(o_scr_qdec + sdq * 256, 256, FP16)[:, hh * 256:hh * 256 + 256]:
                       e.tensor_tensor(o, a, q, AL.mult), VE)
            qsc_done = VE.n

            for hh in range(2):
                h = hg * 2 + hh
                for q in range(2):
                    bank = 4 + (q % 2)
                    bank_wait("tensor", bank)
                    te = None
                    for j in range(4):
                        tcl = q * 4 + j
                        te = OP("tensor", lambda e, p=PS[bank][:, j * 64:(j + 1) * 64].bitcast(BF16),
                                s=gsl(g_kf, hh, tcl * 128, 128), i=ap(o_idb, 64, BF16):
                                e.transpose(p, s, i), TE if j == 3 else None)
                    W("scalar", TE, te)
                    ev = None
                    for j in range(4):
                        tcl = q * 4 + j
                        kcol = ap(o_kdec, 2 * H)[:, h * 2 + tcl % 2: h * 2 + tcl % 2 + 1]
                        ev = OP("scalar", lambda e, o=gsl(g_ks, hh, tcl * 128, 128),
                                s=PS[bank][:, j * 64:(j + 1) * 64].bitcast(BF16), sc=kcol:
                                e.activation(out=o, in_=s, func=AF.Copy, scale=sc),
                                SE if j == 3 else None)
                    bank_free[bank] = (SE, ev)
            kst_done_se = SE.n

            W("gpsimd", TE, vw_slot_te.get(0))
            v_ld = OP("gpsimd", lambda e, d=ap(o_sil, 2048, BF16), s=w_v[hg]:
                      e.dma_start(out=d, in_=s), S_vw[0], 16)
            W("tensor", S_vw[0], v_ld)
            v_last_te = None
            for tcl in range(TC):
                bank = tcl
                bank_wait("tensor", bank)
                for f in range(FC):
                    v_last_te = OP("tensor", lambda e, p=PS[bank][:, 0:256],
                                   l=xf[:, f, tcl * 128:(tcl + 1) * 128],
                                   r=ap(o_sil, 2048, BF16)[:, f * 256:(f + 1) * 256],
                                   st=(f == 0), sp=(f == FC - 1): e.matmul(p, l, r, start=st, stop=sp),
                                   TE if f == FC - 1 else None)
            vw_slot_te[0] = v_last_te
            W("scalar", TE, v_last_te)
            v_ev_se = {}
            for tcl in range(TC):
                bank = tcl
                pv = PS[bank][:, 0:256].rearrange("p (hh c) -> p hh c", hh=2)
                dv = ap(g_vt, 1024, BF16).rearrange("p (hh t) -> p hh t", hh=2)[:, :, tcl * 128:(tcl + 1) * 128]
                ev = OP("scalar", lambda e, o=dv, s=pv: e.activation(out=o, in_=s, func=AF.Silu), SE)
                bank_free[bank] = (SE, ev)
                v_ev_se[tcl] = ev
            v_done_se = SE.n

            W("vector", S_dq[sdq], dqld)
            for b in range(NB):
                sT_ve = {}
                kvmm_te = {}
                for hh in range(2):
                    h = hg * 2 + hh
                    sb = (4, 5) if hh == 0 else (0, 1)
                    for nh in range(2):
                        bank = sb[nh]
                        bank_wait("tensor", bank)
                        OP("tensor", lambda e, p=PS[bank][:, 0:256],
                           l=gsl(g_kf, hh, b * 256 + nh * 128, 128),
                           r=gsl(g_q, hh, b * 256, 256): e.matmul(p, l, r, start=True, stop=True),
                           TE if nh == 1 else None)
                    sT_ve[hh] = TE.n
                W("tensor", SE, v_ev_se[2 * b + 1])
                for hh in range(2):
                    kb = 7 if hh == 0 else 3
                    bank_wait("tensor", kb)
                    tK = None
                    for u in range(2):
                        tK = OP("tensor", lambda e, p=PS[kb][:, 0:128],
                                l=gsl(g_ks, hh, (2 * b + u) * 128, 128),
                                r=gsl(g_vt, hh, (2 * b + u) * 128, 128),
                                st=(u == 0), sp=(u == 1): e.matmul(p, l, r, start=st, stop=sp),
                                TE if u == 1 else None)
                    kvmm_te[hh] = tK
                sT_done = {}
                for hh in range(2):
                    sb = (4, 5) if hh == 0 else (0, 1)
                    W("vector", TE, sT_ve[hh])
                    W("vector", TE, sT_rd[hh])
                    for nh in range(2):
                        vv = OP("vector", lambda e, o=ap(o_scr_sT + (hh * 2 + nh) * 128, 128, BF16),
                                p=PS[sb[nh]][:, 0:256],
                                d=ap(o_scr_dds + sdq * 512, 512, FP16)[:, (hh * 2 + nh) * 256:(hh * 2 + nh + 1) * 256]:
                                e.tensor_tensor(o, p, d, AL.mult), VE)
                        bank_free[sb[nh]] = (VE, vv)
                    sT_done[hh] = VE.n
                for hh in range(2):
                    h = hg * 2 + hh
                    ob = 6 if hh == 0 else 2
                    bank_wait("tensor", ob)
                    W("tensor", VE, sT_done[hh])
                    if b == 0:
                        kv_cur_b = ap(o_scr_kvini + h * 64, 64, BF16)
                        if hg == 0:
                            W("tensor", SE, kv_init_done_se)
                    else:
                        kv_cur_b = ap(o_scr_kvppb + (hh * 2 + (b - 1) % 2) * 64, 64, BF16)
                        W("tensor", VE, kv_ve.get(hh))
                    for nh in range(2):
                        OP("tensor", lambda e, p=PS[ob][:, 0:256],
                           l=gsl(g_vt, hh, (2 * b + nh) * 128, 128),
                           r=ap(o_scr_sT + (hh * 2 + nh) * 128, 128, BF16),
                           st=(nh == 0): e.matmul(p, l, r, start=st, stop=False))
                    tO = OP("tensor", lambda e, p=PS[ob][:, 0:256], l=kv_cur_b,
                            r=gsl(g_qs, hh, b * 256, 256): e.matmul(p, l, r, start=False, stop=True), TE)
                    sT_rd[hh] = tO
                    W("scalar", TE, tO)
                    odst = ap(o_ofm, 8192, BF16).rearrange("p (hx t) -> p hx t", hx=16)[:, h, b * 256:(b + 1) * 256]
                    evO = OP("scalar", lambda e, o=odst, p=PS[ob][:, 0:256]: e.copy(o, p), SE)
                    bank_free[ob] = (SE, evO)
                for hh in range(2):
                    h = hg * 2 + hh
                    kb = 7 if hh == 0 else 3
                    W("vector", TE, kvmm_te[hh])
                    if b == 0:
                        kv_cur_x = ap(o_scr_kvini + h * 64, 64, BF16)
                    else:
                        kv_cur_x = ap(o_scr_kvpp + (hh * 2 + (b - 1) % 2) * 128, 128)
                    kv_nxt = ap(o_scr_kvpp + (hh * 2 + b % 2) * 128, 128)
                    kv_nxt_b = ap(o_scr_kvppb + (hh * 2 + b % 2) * 64, 64, BF16)
                    DRAIN("vector")
                    OP("vector", lambda e, o=kv_nxt, i=kv_cur_x, p=PS[kb][:, 0:128],
                       bd=bdec[h]: e.scalar_tensor_tensor(o, i, bd, p, AL.mult, AL.add), VE)
                    DRAIN("vector")
                    vKb = OP("vector", lambda e, o=kv_nxt_b, i=kv_nxt: e.tensor_copy(o, i), VE)
                    kv_ve[hh] = vKb
                    bank_free[kb] = (VE, vKb)
            dq_slot_ve[sdq] = VE.n
        attn_se = SE.n
        if stop_phase == "attn":
            return

        ofm_h = ap(o_ofm, 8192, BF16)
        o2_slot_te = {}
        st_te = None
        for fc in range(FC):
            slot = fc % 2
            o2 = ap(o_o2 + slot * 512, 512, BF16)
            W("scalar", TE, o2_slot_te.get(slot))
            if fc < 2:
                W("scalar", TE, lnt_slot_te.get(fc % 2))
            sq = OP("scalar", lambda e, o=o2, i=ofm_h[:, fc * 1024:(fc + 1) * 1024]:
                    e.activation(out=o, in_=i, func=AF.Square), SE)
            W("tensor", SE, sq)
            if fc == 0:
                bank_wait("tensor", 0); bank_wait("tensor", 1)
            for half in range(2):
                st_te = OP("tensor", lambda e, p=PS[half][0:1, 0:512], l=ap(o_ones, 64, BF16)[:, 0:1],
                           r=o2[:, half * 512:(half + 1) * 512], st=(fc == 0), sp=(fc == FC - 1):
                           e.matmul(p, l, r, start=st, stop=sp), TE if half == 1 else None)
            o2_slot_te[slot] = st_te
        W("scalar", TE, st_te)
        srow = ap(o_srow, 1024, parts=(0, 1))
        sc1 = None
        for half in range(2):
            sc1 = OP("scalar", lambda e, o=srow[:, half * 512:(half + 1) * 512], p=PS[half][0:1, 0:512]:
                     e.copy(o, p), SE if half == 1 else None)
        bank_free[0] = (SE, sc1); bank_free[1] = (SE, sc1)
        W("vector", SE, sc1)
        OP("vector", lambda e, o=srow, s=1.0 / D, b=EPS_RMS: e.tensor_scalar(o, o, s, b, AL.mult, AL.add))
        DRAIN("vector")
        vr = OP("vector", lambda e, o=srow: e.reciprocal(o, o), VE)
        W("scalar", VE, vr)
        srow_b = ap(o_srow + 1024, 512, BF16, parts=(0, 1))
        sq2 = OP("scalar", lambda e, o=srow_b, i=srow:
                 e.activation(out=o, in_=i, func=AF.Sqrt), SE)
        W("tensor", SE, sq2)
        bank_wait("tensor", 4); bank_wait("tensor", 5)
        bc_te = None
        for half in range(2):
            bc_te = OP("tensor", lambda e, p=PS[4 + half][:, 0:512], l=ap(o_ones, 64, BF16)[0:1, 0:128],
                       r=srow_b[:, half * 512:(half + 1) * 512]:
                       e.matmul(p, l, r, start=True, stop=True), TE if half == 1 else None)
        W("vector", TE, bc_te)

        ogt_h = ap(o_ogt, 8192, BF16)
        wg_slot_te = {0: qk_res_te.get(0), 1: qk_res_te.get(0)}
        gv_prev = None
        for fc in range(FC):
            slot = fc % 2
            W("gpsimd", TE, wg_slot_te.get(slot))
            ld = OP("gpsimd", lambda e, d=ap(o_qkw + slot * 1024, 1024, BF16), s=w_g[fc]:
                    e.dma_start(out=d, in_=s), S_qk[slot], 16)
            W("tensor", S_qk[slot], ld)
            t1 = None
            for th in range(2):
                bank = slot * 2 + th
                bank_wait("tensor", bank)
                for f in range(FC):
                    lhs = ap(o_qkw + slot * 1024, 1024, BF16)[:, f * 128:(f + 1) * 128]
                    t1 = OP("tensor", lambda e, p=PS[bank][:, 0:512], l=lhs,
                            r=xf[:, f, th * 512:(th + 1) * 512],
                            st=(f == 0), sp=(f == FC - 1): e.matmul(p, l, r, start=st, stop=sp),
                            TE if (th == 1 and f == FC - 1) else None)
            wg_slot_te[slot] = t1
            W("scalar", TE, t1)
            W("scalar", VE, gv_prev)
            sg = None
            for th in range(2):
                sg = OP("scalar", lambda e, o=ap(o_sil, 1024)[:, th * 512:(th + 1) * 512],
                        p=PS[slot * 2 + th][:, 0:512]: e.activation(out=o, in_=p, func=AF.Sigmoid),
                        SE if th == 1 else None)
            bank_free[slot * 2] = (SE, sg); bank_free[slot * 2 + 1] = (SE, sg)
            W("vector", SE, sg)
            W("vector", CONST, const_n)
            for th in range(2):
                OP("vector", lambda e, o=ap(o_sil + 1024, 1024)[:, th * 512:(th + 1) * 512],
                   i=ofm_h[:, fc * 1024 + th * 512: fc * 1024 + (th + 1) * 512],
                   nw=ap(o_normw, FC)[:, fc:fc + 1], p=PS[4 + th][:, 0:512]:
                   e.scalar_tensor_tensor(o, i, nw, p, AL.mult, AL.mult))
            DRAIN("vector")
            gv = None
            for th in range(2):
                gv = OP("vector", lambda e, o=ogt_h[:, fc * 1024 + th * 512: fc * 1024 + (th + 1) * 512],
                        a=ap(o_sil + 1024, 1024)[:, th * 512:(th + 1) * 512],
                        b=ap(o_sil, 1024)[:, th * 512:(th + 1) * 512]: e.tensor_tensor(o, a, b, AL.mult),
                        VE if th == 1 else None)
            gv_prev = gv
        ogt_done_ve = VE.n
        bank_free[4] = (VE, ogt_done_ve); bank_free[5] = (VE, ogt_done_ve)
        if stop_phase == "c4":
            return

        x2b = ap(o_x2b, 8192, BF16)
        x2f = x2b.rearrange("p (f t) -> p f t", f=16)
        d0_slot_te = {0: lnt_slot_te.get(0), 1: lnt_slot_te.get(1)}
        W("tensor", VE, ogt_done_ve)
        wow_slot_te = {}
        at4_ve = None
        xresb = ap(o_xres, 8192, BF16)
        for tg in range(2):
            for oc in range(4):
                par = oc % 2
                wbase = o_qkw if par == 0 else o_kvw
                W("gpsimd", TE, wow_slot_te.get(par))
                if tg == 0 and oc == 0:
                    W("gpsimd", TE, wg_slot_te.get(0))
                    W("gpsimd", TE, wg_slot_te.get(1))
                if tg == 0 and oc == 1:
                    W("gpsimd", TE, dummy_kv_guard)
                wld1 = OP("gpsimd", lambda e, d=ap(wbase, 2048, BF16), s=w_o[oc, :, 0:4096]:
                          e.dma_start(out=d, in_=s), S_wo[par], 16)
                wld2 = OP("gpsimd", lambda e, d=ap(wbase + 2048, 2048, BF16), s=w_o[oc, :, 4096:8192]:
                          e.dma_start(out=d, in_=s), S_wo2[par], 16)
                te_oc = None
                for hp in range(2):
                    W("tensor", (S_wo if hp == 0 else S_wo2)[par], wld1 if hp == 0 else wld2)
                    for t in range(4):
                        bank = 4 + t
                        if hp == 0:
                            bank_wait("tensor", bank)
                        for hd in range(hp * 8, hp * 8 + 8):
                            lhs = ogt_h[:, hd * 1024 + (tg * 4 + t) * 128: hd * 1024 + (tg * 4 + t) * 128 + 128]
                            te_oc = OP("tensor", lambda e, p=PS[bank][:, 0:512], l=lhs,
                                       r=ap(wbase, 4096, BF16)[:, hd * 512:(hd + 1) * 512],
                                       st=(hd == 0), sp=(hd == FC - 1): e.matmul(p, l, r, start=st, stop=sp),
                                       TE if hd == hp * 8 + 7 else None)
                wow_slot_te[par] = te_oc
                W("scalar", TE, te_oc)
                if tg == 1 and oc == 0:
                    W("scalar", VE, at4_ve)
                if tg == 0 and oc == 0:
                    W("scalar", VE, ogt_done_ve)
                ev = None
                for t in range(4):
                    bank = 4 + t
                    dst = ap(o_at4 + tg * 4096, 4096, BF16)[:, t * 2048 + oc * 512: t * 2048 + (oc + 1) * 512]
                    ev = OP("scalar", lambda e, o=dst, p=PS[bank][:, 0:512]: e.copy(o, p),
                            SE if t == 3 else None)
                for t in range(4):
                    bank_free[4 + t] = (SE, ev)
            at4_se = SE.n
            for t in range(4):
                tt = tg * 4 + t
                slot = tt % 2
                xt = ap(o_xt + slot * 2048, 2048)
                W("sync", VE, xt_slot_ve.get(slot))
                gx = OP("sync", lambda e, d=xt, s=x_own[tt]: e.dma_start(out=d, in_=s), S_xt[slot], 16)
                W("vector", SE, at4_se)
                at = ap(o_at4 + tg * 4096, 4096, BF16)[:, t * 2048:(t + 1) * 2048]
                ln_tile(at, ap(o_nrm, 2048), EPS_LN)
                W("vector", S_xt[slot], gx)
                DRAIN("vector")
                vres = OP("vector", lambda e, o=xresb[:, tt * 2048:(tt + 1) * 2048],
                          a=ap(o_nrm, 2048), b=xt: e.tensor_tensor(o, a, b, AL.add), VE)
                xt_slot_ve[slot] = vres
            at4_ve = VE.n
            if tg == 0:
                W("scalar", TE, wg_slot_te[(FC - 1) % 2])
                for t in range(4):
                    tt = t
                    slot = tt % 2
                    xo = ap(o_xlntm + slot * 1024, 1024, BF16)
                    W("vector", TE, d0_slot_te.get(slot))
                    if t < 2:
                        W("vector", TE, o2_slot_te.get(slot))
                    vdone = ln_tile(xresb[:, tt * 2048:(tt + 1) * 2048], xo, EPS_LN)
                    W("tensor", VE, vdone)
                    transpose_tile(xo, x2b, tt, (2, 3))
                    d0_slot_te[slot] = TE.n
                ln2_half_se = SE.n
        xres_done_ve = VE.n

        def emit_d0_second_half():
            for t in range(4):
                tt = 4 + t
                slot = tt % 2
                xo = ap(o_xlntm + slot * 1024, 1024, BF16)
                W("vector", TE, d0_slot_te.get(slot))
                vdone = ln_tile(xresb[:, tt * 2048:(tt + 1) * 2048], xo, EPS_LN)
                W("tensor", VE, vdone)
                transpose_tile(xo, x2b, tt, (6, 7))
                d0_slot_te[slot] = TE.n
        if stop_phase == "c5":
            return

        if stop_phase == "d0":
            return
        hsb = ap(o_hsb, 16384, BF16)
        fft_tc_se = {}
        gu_slot_te = {}
        gu_gate_te = {}
        dw_slot_te = {}
        sil_slot_ve = {}
        out_lg = {}
        for th in range(2):
            for ic in range(ICN):
                slot = ic % 2
                W("gpsimd", TE, gu_gate_te.get(slot))
                if th == 1 and ic < 2:
                    W("gpsimd", TE, dw_slot_te.get(0))
                    W("gpsimd", TE, dw_slot_te.get(1))
                if th == 0 and ic < 2:
                    W("gpsimd", TE, wow_slot_te.get(0))
                    W("gpsimd", TE, wow_slot_te.get(1))
                ldg = OP("gpsimd", lambda e, d=ap(o_gu + slot * 1024, 1024, BF16), s=w_gate[ic]:
                         e.dma_start(out=d, in_=s), S_gu[slot], 16)
                W("gpsimd", TE, gu_slot_te.get(slot))
                ldu = OP("gpsimd", lambda e, d=ap(o_gu2 + slot * 1024, 1024, BF16), s=w_up[ic]:
                         e.dma_start(out=d, in_=s), S_gu2[slot], 16)
                W("tensor", S_gu[slot], ldg)
                W("tensor", S_gu2[slot], ldu)
                if th == 0 and ic == 0:
                    W("tensor", SE, ln2_half_se)
                if th == 1 and ic == 0:
                    W("tensor", SE, ln2_se)
                bga = slot * 2
                bgb = 1 + slot * 2
                bank_wait("tensor", bga)
                tg1 = None
                for f in range(FC):
                    lhs = ap(o_gu + slot * 1024, 1024, BF16)[:, f * 128:(f + 1) * 128]
                    tg1 = OP("tensor", lambda e, p=PS[bga][:, 0:512], l=lhs,
                             r=x2f[:, f, th * 512:(th + 1) * 512],
                             st=(f == 0), sp=(f == FC - 1): e.matmul(p, l, r, start=st, stop=sp),
                             TE if f == FC - 1 else None)
                bank_wait("tensor", bgb)
                tg2 = None
                for f in range(FC):
                    lhs = ap(o_gu2 + slot * 1024, 1024, BF16)[:, f * 128:(f + 1) * 128]
                    tg2 = OP("tensor", lambda e, p=PS[bgb][:, 0:512], l=lhs,
                             r=x2f[:, f, th * 512:(th + 1) * 512],
                             st=(f == 0), sp=(f == FC - 1): e.matmul(p, l, r, start=st, stop=sp),
                             TE if f == FC - 1 else None)
                gu_slot_te[slot] = tg2
                gu_gate_te[slot] = tg1
                if th == 0 and ic == 3:
                    emit_d0_second_half()
                    ln2_se = SE.n
                W("scalar", TE, tg1)
                W("scalar", VE, sil_slot_ve.get(slot))
                sil = ap(o_sil + slot * 512, 512)
                sev = OP("scalar", lambda e, o=sil, p=PS[bga][:, 0:512]:
                         e.activation(out=o, in_=p, func=AF.Silu), SE)
                bank_free[bga] = (SE, sev)
                W("vector", TE, tg2)
                W("vector", SE, sev)
                hv = OP("vector", lambda e, o=hsb[:, ic * 512:(ic + 1) * 512], p=PS[bgb][:, 0:512], s=sil:
                        e.tensor_tensor(o, p, s, AL.mult), VE)
                sil_slot_ve[slot] = hv
                bank_free[bgb] = (VE, hv)
            h_done_ve = VE.n
            W("tensor", VE, h_done_ve)
            for oh in range(2):
                dte = None
                for icq in range(16):
                    grp = icq % 2
                    doff = o_kvw if grp == 0 else o_gu
                    lds = []
                    W("gpsimd", TE, dw_slot_te.get(grp))
                    if oh == 0 and icq < 2:
                        W("gpsimd", TE, gu_slot_te.get(0))
                        W("gpsimd", TE, gu_slot_te.get(1))
                    for j in range(4):
                        ic = icq * 4 + j
                        lds.append(OP("gpsimd", lambda e, d=ap(doff + j * 512, 512, BF16),
                                      s=w_down[oh, ic]: e.dma_start(out=d, in_=s),
                                      S_dw[grp] if j < 2 else S_dw2[grp], 16))
                    for jp in range(2):
                        W("tensor", (S_dw if jp == 0 else S_dw2)[grp], lds[jp * 2 + 1])
                        for b in range(8):
                            tc4, oc = b // 2, b % 2
                            if icq == 0 and jp == 0:
                                bank_wait("tensor", b)
                            for j in (jp * 2, jp * 2 + 1):
                                ic = icq * 4 + j
                                lhs = hsb[:, ic * 512 + tc4 * 128: ic * 512 + tc4 * 128 + 128]
                                rhs = ap(doff + j * 512, 512, BF16)[:, oc * 512:(oc + 1) * 512]
                                dte = OP("tensor", lambda e, p=PS[b][:, 0:512], l=lhs, r=rhs,
                                         st=(ic == 0), sp=(ic == ICN - 1): e.matmul(p, l, r, start=st, stop=sp),
                                         TE if (b == 7 and j == jp * 2 + 1) else None)
                    dw_slot_te[grp] = dte
                W("scalar", TE, dte)
                if th == 0 and oh == 0:
                    W("scalar", TE, wg_slot_te[(FC - 1) % 2])
                ev = None
                for b in range(8):
                    tc4, oc = b // 2, b % 2
                    dst = ap(o_fft, 4096, BF16)[:, tc4 * 2048 + oh * 1024 + oc * 512:
                                                tc4 * 2048 + oh * 1024 + (oc + 1) * 512]
                    ev = OP("scalar", lambda e, o=dst, p=PS[b][:, 0:512]: e.copy(o, p), SE)
                    if oh == 1:
                        fft_tc_se[tc4] = ev
                for b in range(8):
                    bank_free[b] = (SE, ev)
            for tc4 in range(4):
                tt = th * 4 + tc4
                slot = tt % 2
                W("vector", SE, fft_tc_se[tc4])
                ft = ap(o_fft, 4096, BF16)[:, tc4 * 2048:(tc4 + 1) * 2048]
                ln_tile(ft, ap(o_nrm, 2048), EPS_LN)
                DRAIN("vector")
                ot = ap(o_outt + slot * 2048, 2048)
                if tt >= 2:
                    W("vector", S_out[slot], out_lg[tt - 2])
                else:
                    W("vector", VE, xt_slot_ve.get(slot))
                vo = OP("vector", lambda e, o=ot, a=ap(o_nrm, 2048),
                        b=xresb[:, tt * 2048:(tt + 1) * 2048]: e.tensor_tensor(o, a, b, AL.add), VE)
                W("gpsimd", VE, vo)
                out_lg[tt] = OP("gpsimd", lambda e, d=out_d[tt], s=ot: e.dma_start(out=d, in_=s), S_out[slot], 16)

        W("gpsimd", S_out[0], S_out[0].n)
        W("gpsimd", S_out[1], S_out[1].n)

    with (
        nc.sbuf_tensor("AR", [128, AWORDS], F32) as AR_t,
        nc.psum_tensor("p0", [128, 512], F32) as p0,
        nc.psum_tensor("p1", [128, 512], F32) as p1,
        nc.psum_tensor("p2", [128, 512], F32) as p2,
        nc.psum_tensor("p3", [128, 512], F32) as p3,
        nc.psum_tensor("p4", [128, 512], F32) as p4,
        nc.psum_tensor("p5", [128, 512], F32) as p5,
        nc.psum_tensor("p6", [128, 512], F32) as p6,
        nc.psum_tensor("p7", [128, 512], F32) as p7,
        nc.Block() as block,
    ):
        import contextlib
        with contextlib.ExitStack() as stack:
            for sm in SEMS:
                sm.h = stack.enter_context(nc.semaphore(sm.name))
            ARh[0] = AR_t.ap()
            PS.extend([p.ap() for p in (p0, p1, p2, p3, p4, p5, p6, p7)])
            logical()
            _emit_all(block, prog)
    return nc


def _emit_all(block, prog):
    if True:

        @block.sync
        def _(e):
            for fn in prog["sync"]:
                fn(e)

        @block.tensor
        def _(e):
            for fn in prog["tensor"]:
                fn(e)

        @block.vector
        def _(e):
            for fn in prog["vector"]:
                fn(e)

        @block.scalar
        def _(e):
            for fn in prog["scalar"]:
                fn(e)

        @block.gpsimd
        def _(e):
            for fn in prog["gpsimd"]:
                fn(e)



def _host_prep(L0_attn_x, L0_attn_Wqkv, L0_attn_Wg, L0_attn_Wout, L0_attn_norm_w,
               L0_ffn_gate_w, L0_ffn_up_w, L0_ffn_down_w):
    import ml_dtypes
    f32 = np.float32
    slopes = np.array(_slopes(), dtype=np.float64)
    arr = np.arange(1, BLK + 1, dtype=np.float64)

    qdec = np.exp(-slopes[:, None] * arr[None, :])
    kdec = np.exp(-slopes[:, None] * (BLK - arr[None, :]))
    idx = arr[:, None] - arr[None, :]
    dd = np.where(idx[None] >= 0, np.exp(-slopes[:, None, None] * idx[None]), 0.0)

    qdec_in = np.ascontiguousarray(np.broadcast_to(
        qdec.reshape(NHG, 2, 1, BLK), (NHG, 2, 128, BLK)).transpose(0, 2, 1, 3).reshape(
        NHG, 128, 512)).astype(np.float16)
    kdec_in = np.ascontiguousarray(kdec.reshape(H, 2, 128).transpose(2, 0, 1).reshape(128, 2 * H)).astype(f32)
    bd = np.exp(-slopes * BLK)
    kdp = np.zeros((128, 128))
    for h in range(H):
        for b in range(4):
            for u in range(2):
                kdp[:, h * 8 + b * 2 + u] = kdec[h, u * 128:(u + 1) * 128] * bd[h] ** (3 - b)
    kdp_in = np.ascontiguousarray(kdp).astype(f32)
    ddt = dd.transpose(0, 2, 1)
    ddt_in = np.ascontiguousarray(ddt.reshape(NHG, 2, 2, 128, BLK).transpose(0, 3, 1, 2, 4).reshape(
        NHG, 128, 1024)).astype(np.float16)

    Wqkv = np.asarray(L0_attn_Wqkv, dtype=f32)
    qk = np.empty((NHG, 4, FC, 128, 128), dtype=f32)
    vt = np.empty((NHG, FC, 128, 256), dtype=f32)
    kv = np.empty((NHG, FC, 128, 512), dtype=f32)
    Wq3 = Wqkv.reshape(FC, 128, 6144)
    for g in range(NHG):
        for hh in range(2):
            h = 2 * g + hh
            base = 384 * h
            qk[g, 2 * hh + 0] = Wq3[:, :, base:base + 128]
            qk[g, 2 * hh + 1] = Wq3[:, :, base + 128:base + 256]
            vt[g, :, :, hh * 128:(hh + 1) * 128] = Wq3[:, :, base + 256:base + 384]
            kv[g, :, :, hh * 256:hh * 256 + 128] = Wq3[:, :, base + 128:base + 256]
            kv[g, :, :, hh * 256 + 128:hh * 256 + 256] = Wq3[:, :, base + 256:base + 384]

    qk = np.ascontiguousarray(
        qk.transpose(0, 1, 3, 2, 4).reshape(NHG, 4, 128, FC * 128).transpose(0, 2, 1, 3).reshape(
            NHG, 128, 4 * FC * 128))
    kv = np.ascontiguousarray(
        kv.reshape(NHG, 4, 4, 128, 512).transpose(0, 1, 3, 2, 4).reshape(NHG, 4, 128, 2048))
    vt = np.ascontiguousarray(vt.transpose(0, 2, 1, 3).reshape(NHG, 128, FC * 256))
    Wg = np.asarray(L0_attn_Wg, dtype=f32).reshape(FC, 128, FC, 128)
    wg_in = np.ascontiguousarray(Wg.transpose(2, 1, 0, 3).reshape(FC, 128, FC * 128))
    Wo = np.asarray(L0_attn_Wout, dtype=f32).reshape(FC, 128, 4, 512)
    wo_in = np.ascontiguousarray(Wo.transpose(2, 1, 0, 3).reshape(4, 128, FC * 512))

    gT = np.asarray(L0_ffn_gate_w, dtype=f32).T.reshape(FC, 128, ICN, 128)
    gw_in = np.ascontiguousarray(gT.transpose(2, 1, 0, 3).reshape(ICN, 128, FC * 128))
    uT = np.asarray(L0_ffn_up_w, dtype=f32).T.reshape(FC, 128, ICN, 128)
    uw_in = np.ascontiguousarray(uT.transpose(2, 1, 0, 3).reshape(ICN, 128, FC * 128))
    dT = np.asarray(L0_ffn_down_w, dtype=f32).T.reshape(ICN, 128, 2, 1024)
    dw_in = np.ascontiguousarray(dT.transpose(2, 0, 1, 3))

    nw = np.asarray(L0_attn_norm_w, dtype=f32).reshape(FC, 128)
    nw_in = np.ascontiguousarray(nw.T)

    consts = {
        "id_b": np.eye(128, dtype=f32).astype(ml_dtypes.bfloat16),
        "ones_r": np.ones((128, 128), dtype=f32).astype(ml_dtypes.bfloat16),
        "kdec": kdec_in,
        "kdecp": kdp_in,
        "normw": nw_in,
        "qdec": qdec_in,
        "ddt": ddt_in,
        "w_qk": qk, "w_v": vt, "w_kv": kv, "w_g": wg_in, "w_o": wo_in,
        "w_gate": gw_in, "w_up": uw_in, "w_down": dw_in,
    }

    x = np.asarray(L0_attn_x, dtype=f32)
    in_maps = []
    for c in range(8):
        s, half = c // 2, c % 2
        xo = np.ascontiguousarray(x[s, half * T:(half + 1) * T].reshape(TC, 128, D))
        if half == 1:
            xp = np.ascontiguousarray(x[s, 0:T].reshape(TC, 128, D))
            mk = np.ones((128, 1), dtype=f32)
        else:
            xp = xo
            mk = np.zeros((128, 1), dtype=f32)
        m = dict(consts)
        m["x_own"] = xo
        m["x_prev"] = xp
        m["maskc"] = mk
        in_maps.append(m)
    return in_maps


_NC_CACHE = [None]


def kernel(L0_attn_x, L0_attn_Wqkv, L0_attn_Wg, L0_attn_Wout, L0_attn_norm_w,
           L0_ffn_gate_w, L0_ffn_up_w, L0_ffn_down_w):
    in_maps = _host_prep(L0_attn_x, L0_attn_Wqkv, L0_attn_Wg, L0_attn_Wout, L0_attn_norm_w,
                         L0_ffn_gate_w, L0_ffn_up_w, L0_ffn_down_w)
    if _NC_CACHE[0] is None:
        _NC_CACHE[0] = build_program()
    nc = _NC_CACHE[0]
    res = run_bass_kernel_spmd(nc, in_maps, core_ids=list(range(8)))
    out = np.empty((4, 2048, D), dtype=np.float32)
    for c in range(8):
        s, half = c // 2, c % 2
        out[s, half * T:(half + 1) * T] = np.asarray(res.results[c]["out"]).reshape(T, D)
    return out


# version 24 (final)
# speedup vs baseline: 1.0694x; 1.0078x over previous
import math
import numpy as np
import concourse.bass as bass
import concourse.mybir as mybir
from concourse.bass_utils import run_bass_kernel_spmd

F32 = mybir.dt.float32
F32R = mybir.dt.float32r
BF16 = mybir.dt.bfloat16
FP16 = mybir.dt.float16
AL = mybir.AluOpType
AF = mybir.ActivationFunctionType

T = 1024
D = 2048
H = 16
BLK = 256
NB = T // BLK
TC = T // 128
FC = D // 128
ICN = 64
NHG = 8
EPS_LN = 1e-5
EPS_RMS = 1e-6

ENGS = ("sync", "tensor", "vector", "scalar", "gpsimd")


def _slopes():
    start = 2 ** (-2 ** -(math.log2(H) - 3))
    return [start * start ** i for i in range(H)]


def build_program(stop_phase=None):
    nc = bass.Bass("TRN2", target_bir_lowering=False)
    slopes = _slopes()
    bdec = [float(np.exp(np.float32(-s * BLK))) for s in slopes]

    x_own = nc.dram_tensor("x_own", [TC, 128, D], F32, kind="ExternalInput").ap()
    x_prev = nc.dram_tensor("x_prev", [TC, 128, D], F32, kind="ExternalInput").ap()
    maskc = nc.dram_tensor("maskc", [128, 1], F32, kind="ExternalInput").ap()
    id_b = nc.dram_tensor("id_b", [128, 128], BF16, kind="ExternalInput").ap()
    ones_r = nc.dram_tensor("ones_r", [128, 128], BF16, kind="ExternalInput").ap()
    kdec_d = nc.dram_tensor("kdec", [128, 2 * H], F32, kind="ExternalInput").ap()
    kdecp_d = nc.dram_tensor("kdecp", [128, 128], F32, kind="ExternalInput").ap()
    normw_d = nc.dram_tensor("normw", [128, FC], F32, kind="ExternalInput").ap()
    qdec_d = nc.dram_tensor("qdec", [NHG, 128, 512], FP16, kind="ExternalInput").ap()
    ddt_d = nc.dram_tensor("ddt", [NHG, 128, 1024], FP16, kind="ExternalInput").ap()
    w_kv = nc.dram_tensor("w_kv", [NHG, 4, 128, 2048], F32, kind="ExternalInput").ap()
    w_qk = nc.dram_tensor("w_qk", [NHG, 128, 4 * FC * 128], F32, kind="ExternalInput").ap()
    w_v = nc.dram_tensor("w_v", [NHG, 128, FC * 256], F32, kind="ExternalInput").ap()
    w_g = nc.dram_tensor("w_g", [FC, 128, FC * 128], F32, kind="ExternalInput").ap()
    w_o = nc.dram_tensor("w_o", [4, 128, FC * 512], F32, kind="ExternalInput").ap()
    w_gate = nc.dram_tensor("w_gate", [ICN, 128, FC * 128], F32, kind="ExternalInput").ap()
    w_up = nc.dram_tensor("w_up", [ICN, 128, FC * 128], F32, kind="ExternalInput").ap()
    w_down = nc.dram_tensor("w_down", [2, ICN, 128, 1024], F32, kind="ExternalInput").ap()
    out_d = nc.dram_tensor("out", [TC, 128, D], F32, kind="ExternalOutput").ap()

    AWORDS = 53120
    _ptr = [0]
    def region(words, align=128):
        p = (_ptr[0] + align - 1) // align * align
        assert p + words <= AWORDS, f"arena overflow {p}+{words}"
        _ptr[0] = p + words
        return p

    o_idb = region(64)
    o_ones = region(128)
    o_kdec = region(32)
    o_kdecp = region(128)
    o_normw = region(16)
    o_mask = region(8)
    o_stats = region(64)
    o_mv = region(16)
    o_inv = region(16)
    o_xln = region(8192)
    o_ofm = region(8192)
    o_grp = region(8192)
    o_scr = region(8192)
    o_xt = region(4096)
    o_xlntm = region(2048)
    o_nrm = region(2048)
    o_qkw = region(4096)
    o_kvw = region(2048)
    o_gu = region(2048)
    o_sil = region(2048)
    o_vw = region(512)

    o_srow = o_nrm
    o_at4 = o_ofm
    o_x2b = o_xln
    o_hsb = o_ofm
    o_ogt = o_grp
    o_scr_qdec = o_scr
    o_scr_dds = o_scr + 512
    o_scr_kvini = o_scr + 1536
    o_scr_kvpp = o_scr + 2560
    o_scr_kvppb = o_scr + 3072
    o_scr_sT = o_scr + 3328
    o_scr_ksc = o_scr + 3584
    o_scr_vsc = o_scr + 3840
    o_pro_ksc = o_scr + 4096
    o_kvp = o_nrm
    o_xres = o_scr
    o_o2 = o_xlntm
    o_fft = o_qkw
    o_gu2 = o_kvw
    o_outt = o_xt
    o_wow = o_vw

    prog = {e: [] for e in ENGS}
    last_wait = {e: {} for e in ENGS}

    class Sem:
        def __init__(self, name):
            self.name = name
            self.n = 0
            self.h = None

    TE, VE, SE = Sem("TE"), Sem("VE"), Sem("SE")
    SEMS = [TE, VE, SE]

    def mksem(name):
        sm = Sem(name)
        SEMS.append(sm)
        return sm

    CONST = mksem("cst")
    S_xt = [mksem("sxt0"), mksem("sxt1")]
    S_kv = [mksem("skv0"), mksem("skv1")]
    S_kv2 = [mksem("skv20"), mksem("skv21")]
    S_dq = [mksem("sdq0"), mksem("sdq1")]
    S_qk = [mksem("sqk0"), mksem("sqk1")]
    S_vw = [mksem("svw0"), mksem("svw1")]
    S_wo = [mksem("swo0"), mksem("swo1")]
    S_gu = [mksem("sgu0"), mksem("sgu1")]
    S_gu2 = [mksem("sgu20"), mksem("sgu21")]
    S_dw = [mksem("sdw0"), mksem("sdw1")]
    S_dw2 = [mksem("sdw20"), mksem("sdw21")]
    S_wo2 = [mksem("swo20"), mksem("swo21")]
    S_out = [mksem("sou0"), mksem("sou1")]

    def E(eng, fn):
        prog[eng].append(fn)

    def W(eng, sem, val):
        if val is None or val <= 0:
            return
        assert val <= sem.n, f"wait {sem.name}>={val} but only {sem.n} planned"
        if last_wait[eng].get(sem.name, -1) >= val:
            return
        last_wait[eng][sem.name] = val
        E(eng, lambda e, s=sem, v=val: e.wait_ge(s.h, v))

    def OP(eng, fn, sem=None, k=1):
        if sem is None:
            E(eng, lambda e, f=fn: f(e))
            return None
        sem.n += k
        def run(e, f=fn, s=sem, kk=k):
            f(e).then_inc(s.h, kk)
        E(eng, run)
        return sem.n

    def DRAIN(eng):
        E(eng, lambda e: e.drain())

    ARh = [None]
    PS = []

    def ap(off, words, dtype=F32, parts=(0, 128)):
        a = ARh[0][parts[0]:parts[1], off:off + words]
        if dtype == F32:
            return a
        return a.bitcast(dtype)

    bank_free = {}

    def bank_wait(eng, bank):
        ent = bank_free.get(bank)
        if ent is not None:
            W(eng, ent[0], ent[1])

    def logical():
        for dst, src in (
            (ap(o_idb, 64, BF16), id_b),
            (ap(o_ones, 64, BF16), ones_r),
            (ap(o_kdec, 2 * H), kdec_d),
            (ap(o_kdecp, 128), kdecp_d),
            (ap(o_normw, FC), normw_d),
            (ap(o_mask, 1), maskc),
        ):
            OP("sync", lambda e, d=dst, s=src: e.dma_start(out=d, in_=s), CONST, 16)
        const_n = CONST.n

        ln_par = [0]
        def ln_tile(src_ap, out_ap, eps):
            ln_par[0] ^= 1
            pr = ln_par[0]
            st = ap(o_stats + pr * 32, 24); mv = ap(o_mv + pr * 8, 2); inv = ap(o_inv + pr * 8, 1)
            for g in range(4):
                OP("vector", lambda e, a=st[:, g * 6:(g + 1) * 6],
                   b=src_ap[:, g * 512:(g + 1) * 512]: e.bn_stats(a, b))
            DRAIN("vector")
            OP("vector", lambda e, a=mv, b=st: e.bn_aggr(a, b))
            DRAIN("vector")
            OP("vector", lambda e, a=inv, b=mv[:, 1:2], ep=eps: e.tensor_scalar(a, b, ep, None, AL.add))
            DRAIN("vector")
            v1 = OP("vector", lambda e, a=inv: e.reciprocal(a, a), VE)
            W("scalar", VE, v1)
            s1 = OP("scalar", lambda e, a=inv: e.activation(out=a, in_=a, func=AF.Sqrt), SE)
            W("vector", SE, s1)
            return OP("vector", lambda e, o=out_ap, x=src_ap, m=mv[:, 0:1], iv=inv:
                      e.tensor_scalar(o, x, m, iv, AL.subtract, AL.mult), VE)

        def transpose_tile(src_ap, dst_ap, tcol, banks):
            W("tensor", CONST, const_n)
            d3 = dst_ap.rearrange("p (f t) -> p f t", f=16)
            for q in range(4):
                bank = banks[q % len(banks)]
                bank_wait("tensor", bank)
                te = None
                for j in range(4):
                    f = q * 4 + j
                    pdst = PS[bank][:, j * 64:(j + 1) * 64].bitcast(BF16)
                    te = OP("tensor", lambda e, p=pdst, s=src_ap[:, f * 128:(f + 1) * 128],
                            i=ap(o_idb, 64, BF16): e.transpose(p, s, i),
                            TE if j == 3 else None)
                W("scalar", TE, te)
                psrc = PS[bank][:, 0:256].bitcast(BF16).rearrange("p (f c) -> p f c", f=4)
                dsl = d3[:, q * 4:(q + 1) * 4, tcol * 128:(tcol + 1) * 128]
                ev = OP("scalar", lambda e, o=dsl, s=psrc: e.copy(o, s), SE)
                bank_free[bank] = (SE, ev)

        xt_slot_ve = {}
        def ln_transpose_stream(x_dram, dst_ap, slot_te):
            for tc in range(TC):
                slot = tc % 2
                xt = ap(o_xt + slot * 2048, 2048)
                W("sync", VE, xt_slot_ve.get(slot))
                g1 = OP("sync", lambda e, d=xt, s=x_dram[tc]: e.dma_start(out=d, in_=s), S_xt[slot], 16)
                W("vector", S_xt[slot], g1)
                W("vector", TE, slot_te.get(slot))
                xo = ap(o_xlntm + slot * 1024, 1024, BF16)
                vdone = ln_tile(xt, xo, EPS_LN)
                xt_slot_ve[slot] = vdone
                W("tensor", VE, vdone)
                transpose_tile(xo, dst_ap, tc, (4, 5, 6, 7))
                slot_te[slot] = TE.n
            return SE.n

        xf_b = ap(o_xln, 8192, BF16)
        xf = xf_b.rearrange("p (f t) -> p f t", f=16)
        xp_b = ap(o_ofm, 8192, BF16)
        xp = xp_b.rearrange("p (f t) -> p f t", f=16)
        lnt_slot_te = {}
        lnp_se = ln_transpose_stream(x_prev, xp_b, lnt_slot_te)

        if stop_phase == "lnp":
            return
        kv_slot_te = {}
        pro_ksc_te = {}
        W("tensor", SE, lnp_se)
        for hg in range(NHG):
            mm_te = None
            for fq in range(4):
                grp = fq % 2
                W("gpsimd", TE, kv_slot_te.get(grp))
                ld1 = OP("gpsimd", lambda e, d=ap(o_kvw + grp * 1024, 512, BF16),
                         s=w_kv[hg, fq, :, 0:1024]: e.dma_start(out=d, in_=s), S_kv[grp], 16)
                ld2 = OP("gpsimd", lambda e, d=ap(o_kvw + grp * 1024 + 512, 512, BF16),
                         s=w_kv[hg, fq, :, 1024:2048]: e.dma_start(out=d, in_=s), S_kv2[grp], 16)
                for jp in range(2):
                    W("tensor", (S_kv if jp == 0 else S_kv2)[grp], ld1 if jp == 0 else ld2)
                    for b in range(NB):
                        for u in range(2):
                            bank = b * 2 + u
                            if fq == 0 and jp == 0:
                                bank_wait("tensor", bank)
                            lhs0 = b * 256 + u * 128
                            for j in (jp * 2, jp * 2 + 1):
                                f = fq * 4 + j
                                is_last = (b == NB - 1 and u == 1 and j == 3)
                                mm_te = OP("tensor", lambda e, p=PS[bank][:, 0:512],
                                           l=xp[:, f, lhs0:lhs0 + 128],
                                           r=ap(o_kvw + grp * 1024, 1024, BF16)[:, j * 512:(j + 1) * 512],
                                           st=(f == 0), sp=(f == FC - 1): e.matmul(p, l, r, start=st, stop=sp),
                                           TE if is_last else None)
                kv_slot_te[grp] = mm_te
            a_mm_te = None
            for b in range(NB):
                sdone = None
                W("scalar", CONST, const_n)
                for u in range(2):
                    bank = b * 2 + u
                    W("scalar", TE, kv_slot_te.get(0))
                    W("scalar", TE, kv_slot_te.get(1))
                    W("scalar", TE, pro_ksc_te.get("t"))
                    kvsil = ap(o_scr_ksc + u * 256, 256, BF16)
                    e1 = OP("scalar", lambda e, o=kvsil, p=PS[bank][:, 0:512]:
                            e.activation(out=o, in_=p, func=AF.Silu), SE)
                    bank_free[bank] = (SE, e1)
                    W("vector", SE, e1)
                    W("vector", TE, pro_ksc_te.get("t"))
                    for hh in range(2):
                        h = hg * 2 + hh
                        ksc = ap(o_pro_ksc + (u * 2 + hh) * 64, 64, BF16)
                        kcol = ap(o_kdecp, 128)[:, (h * 8 + b * 2 + u):(h * 8 + b * 2 + u) + 1]
                        sdone = OP("vector", lambda e, o=ksc, i=kvsil[:, hh * 256:hh * 256 + 128], sc=kcol:
                                   e.tensor_scalar(o, i, sc, None, AL.mult),
                                   VE if (u == 1 and hh == 1) else None)
                for hh in range(2):
                    bank = hh
                    if b == 0:
                        bank_wait("tensor", bank)
                    W("tensor", VE, sdone)
                    for u in range(2):
                        ksc = ap(o_pro_ksc + (u * 2 + hh) * 64, 64, BF16)
                        vsl = ap(o_scr_ksc + u * 256, 256, BF16)[:, hh * 256 + 128:hh * 256 + 256]
                        a_mm_te = OP("tensor", lambda e, p=PS[bank][:, 0:128], l=ksc, r=vsl,
                                     st=(b == 0 and u == 0), sp=(b == NB - 1 and u == 1):
                                     e.matmul(p, l, r, start=st, stop=sp),
                                     TE if (u == 1 and hh == 1) else None)
                pro_ksc_te["t"] = a_mm_te
            W("scalar", TE, a_mm_te)
            for hh in range(2):
                h = hg * 2 + hh
                vv = OP("scalar", lambda e, o=ap(o_scr_kvini + h * 64, 64, BF16), i=PS[hh][:, 0:128],
                        m=ap(o_mask, 1): e.activation(out=o, in_=i, func=AF.Copy, scale=m), SE)
                bank_free[hh] = (SE, vv)
        kv_init_done_se = SE.n

        pro_sil_se = SE.n
        if stop_phase == "prologue":
            return
        ln1_se = ln_transpose_stream(x_own, xf_b, lnt_slot_te)
        if stop_phase == "ln1":
            return
        g_q, g_qs, g_kf, g_ks, g_vt = (o_grp, o_grp + 1024, o_grp + 2048, o_grp + 3072, o_grp + 4096)

        def gsl(base, hh, lo, n):
            return ap(base, 1024, BF16)[:, hh * 1024 + lo: hh * 1024 + lo + n]

        qk_slot_te = {}
        qk_res_te = {}
        vw_slot_te = {}
        dq_slot_ve = {}
        sT_rd = {0: None, 1: None}
        kv_ve = {}
        dummy_kv_guard = kv_slot_te.get(0)
        W("tensor", SE, ln1_se)
        for hg in range(NHG):
            sdq = hg % 2
            if hg == 0:
                W("sync", SE, kv_init_done_se)
            W("sync", VE, dq_slot_ve.get(sdq))
            OP("sync", lambda e, d=ap(o_scr_dds + sdq * 512, 512, FP16), s=ddt_d[hg]:
               e.dma_start(out=d, in_=s), S_dq[sdq], 16)
            dqld = OP("sync", lambda e, d=ap(o_scr_qdec + sdq * 256, 256, FP16), s=qdec_d[hg]:
                      e.dma_start(out=d, in_=s), S_dq[sdq], 16)

            W("gpsimd", TE, qk_res_te.get(0))
            if hg == 0:
                W("gpsimd", TE, kv_slot_te.get(0))
            qk_ld = OP("gpsimd", lambda e, d=ap(o_qkw, 4096, BF16), s=w_qk[hg]:
                       e.dma_start(out=d, in_=s), S_qk[0], 16)
            W("tensor", S_qk[0], qk_ld)
            t1 = None
            for cc in range(4):
                hh, isk = cc // 2, cc % 2
                for th in range(2):
                    bank = (cc % 2) * 2 + th
                    bank_wait("tensor", bank)
                    for f in range(FC):
                        lhs = ap(o_qkw, 4096, BF16)[:, cc * 2048 + f * 128: cc * 2048 + (f + 1) * 128]
                        t1 = OP("tensor", lambda e, p=PS[bank][:, 0:512], l=lhs,
                                r=xf[:, f, th * 512:(th + 1) * 512],
                                st=(f == 0), sp=(f == FC - 1): e.matmul(p, l, r, start=st, stop=sp),
                                TE if f == FC - 1 else None)
                    W("scalar", TE, t1)
                    dst = gsl(g_kf if isk else g_q, hh, th * 512, 512)
                    ev = OP("scalar", lambda e, o=dst, p=PS[bank][:, 0:512]:
                            e.activation(out=o, in_=p, func=AF.Silu), SE)
                    bank_free[bank] = (SE, ev)
            qk_res_te[0] = t1
            qk_done_se = SE.n

            W("vector", S_dq[sdq], dqld)
            W("vector", SE, qk_done_se)
            for hh in range(2):
                for b in range(NB):
                    OP("vector", lambda e, o=gsl(g_qs, hh, b * 256, 256),
                       a=gsl(g_q, hh, b * 256, 256),
                       q=ap(o_scr_qdec + sdq * 256, 256, FP16)[:, hh * 256:hh * 256 + 256]:
                       e.tensor_tensor(o, a, q, AL.mult), VE)
            qsc_done = VE.n

            for hh in range(2):
                h = hg * 2 + hh
                for q in range(2):
                    bank = 4 + (q % 2)
                    bank_wait("tensor", bank)
                    te = None
                    for j in range(4):
                        tcl = q * 4 + j
                        te = OP("tensor", lambda e, p=PS[bank][:, j * 64:(j + 1) * 64].bitcast(BF16),
                                s=gsl(g_kf, hh, tcl * 128, 128), i=ap(o_idb, 64, BF16):
                                e.transpose(p, s, i), TE if j == 3 else None)
                    W("scalar", TE, te)
                    ev = None
                    for j in range(4):
                        tcl = q * 4 + j
                        kcol = ap(o_kdec, 2 * H)[:, h * 2 + tcl % 2: h * 2 + tcl % 2 + 1]
                        ev = OP("scalar", lambda e, o=gsl(g_ks, hh, tcl * 128, 128),
                                s=PS[bank][:, j * 64:(j + 1) * 64].bitcast(BF16), sc=kcol:
                                e.activation(out=o, in_=s, func=AF.Copy, scale=sc),
                                SE if j == 3 else None)
                    bank_free[bank] = (SE, ev)
            kst_done_se = SE.n

            W("gpsimd", TE, vw_slot_te.get(0))
            v_ld = OP("gpsimd", lambda e, d=ap(o_sil, 2048, BF16), s=w_v[hg]:
                      e.dma_start(out=d, in_=s), S_vw[0], 16)
            W("tensor", S_vw[0], v_ld)
            v_last_te = None
            for tcl in range(TC):
                bank = tcl
                bank_wait("tensor", bank)
                for f in range(FC):
                    v_last_te = OP("tensor", lambda e, p=PS[bank][:, 0:256],
                                   l=xf[:, f, tcl * 128:(tcl + 1) * 128],
                                   r=ap(o_sil, 2048, BF16)[:, f * 256:(f + 1) * 256],
                                   st=(f == 0), sp=(f == FC - 1): e.matmul(p, l, r, start=st, stop=sp),
                                   TE if f == FC - 1 else None)
            vw_slot_te[0] = v_last_te
            W("scalar", TE, v_last_te)
            v_ev_se = {}
            for tcl in range(TC):
                bank = tcl
                pv = PS[bank][:, 0:256].rearrange("p (hh c) -> p hh c", hh=2)
                dv = ap(g_vt, 1024, BF16).rearrange("p (hh t) -> p hh t", hh=2)[:, :, tcl * 128:(tcl + 1) * 128]
                ev = OP("scalar", lambda e, o=dv, s=pv: e.activation(out=o, in_=s, func=AF.Silu), SE)
                bank_free[bank] = (SE, ev)
                v_ev_se[tcl] = ev
            v_done_se = SE.n

            W("vector", S_dq[sdq], dqld)
            for b in range(NB):
                sT_ve = {}
                kvmm_te = {}
                for hh in range(2):
                    h = hg * 2 + hh
                    sb = (4, 5) if hh == 0 else (0, 1)
                    for nh in range(2):
                        bank = sb[nh]
                        bank_wait("tensor", bank)
                        OP("tensor", lambda e, p=PS[bank][:, 0:256],
                           l=gsl(g_kf, hh, b * 256 + nh * 128, 128),
                           r=gsl(g_q, hh, b * 256, 256): e.matmul(p, l, r, start=True, stop=True),
                           TE if nh == 1 else None)
                    sT_ve[hh] = TE.n
                W("tensor", SE, v_ev_se[2 * b + 1])
                for hh in range(2):
                    kb = 7 if hh == 0 else 3
                    bank_wait("tensor", kb)
                    tK = None
                    for u in range(2):
                        tK = OP("tensor", lambda e, p=PS[kb][:, 0:128],
                                l=gsl(g_ks, hh, (2 * b + u) * 128, 128),
                                r=gsl(g_vt, hh, (2 * b + u) * 128, 128),
                                st=(u == 0), sp=(u == 1): e.matmul(p, l, r, start=st, stop=sp),
                                TE if u == 1 else None)
                    kvmm_te[hh] = tK
                sT_done = {}
                for hh in range(2):
                    sb = (4, 5) if hh == 0 else (0, 1)
                    W("vector", TE, sT_ve[hh])
                    W("vector", TE, sT_rd[hh])
                    for nh in range(2):
                        vv = OP("vector", lambda e, o=ap(o_scr_sT + (hh * 2 + nh) * 128, 128, BF16),
                                p=PS[sb[nh]][:, 0:256],
                                d=ap(o_scr_dds + sdq * 512, 512, FP16)[:, (hh * 2 + nh) * 256:(hh * 2 + nh + 1) * 256]:
                                e.tensor_tensor(o, p, d, AL.mult), VE)
                        bank_free[sb[nh]] = (VE, vv)
                    sT_done[hh] = VE.n
                for hh in range(2):
                    h = hg * 2 + hh
                    ob = 6 if hh == 0 else 2
                    bank_wait("tensor", ob)
                    W("tensor", VE, sT_done[hh])
                    if b == 0:
                        kv_cur_b = ap(o_scr_kvini + h * 64, 64, BF16)
                        if hg == 0:
                            W("tensor", SE, kv_init_done_se)
                    else:
                        kv_cur_b = ap(o_scr_kvppb + (hh * 2 + (b - 1) % 2) * 64, 64, BF16)
                        W("tensor", VE, kv_ve.get(hh))
                    for nh in range(2):
                        OP("tensor", lambda e, p=PS[ob][:, 0:256],
                           l=gsl(g_vt, hh, (2 * b + nh) * 128, 128),
                           r=ap(o_scr_sT + (hh * 2 + nh) * 128, 128, BF16),
                           st=(nh == 0): e.matmul(p, l, r, start=st, stop=False))
                    tO = OP("tensor", lambda e, p=PS[ob][:, 0:256], l=kv_cur_b,
                            r=gsl(g_qs, hh, b * 256, 256): e.matmul(p, l, r, start=False, stop=True), TE)
                    sT_rd[hh] = tO
                    W("scalar", TE, tO)
                    odst = ap(o_ofm, 8192, BF16).rearrange("p (hx t) -> p hx t", hx=16)[:, h, b * 256:(b + 1) * 256]
                    evO = OP("scalar", lambda e, o=odst, p=PS[ob][:, 0:256]: e.copy(o, p), SE)
                    bank_free[ob] = (SE, evO)
                for hh in range(2):
                    h = hg * 2 + hh
                    kb = 7 if hh == 0 else 3
                    W("vector", TE, kvmm_te[hh])
                    if b == 0:
                        kv_cur_x = ap(o_scr_kvini + h * 64, 64, BF16)
                    else:
                        kv_cur_x = ap(o_scr_kvpp + (hh * 2 + (b - 1) % 2) * 128, 128)
                    kv_nxt = ap(o_scr_kvpp + (hh * 2 + b % 2) * 128, 128)
                    kv_nxt_b = ap(o_scr_kvppb + (hh * 2 + b % 2) * 64, 64, BF16)
                    DRAIN("vector")
                    OP("vector", lambda e, o=kv_nxt, i=kv_cur_x, p=PS[kb][:, 0:128],
                       bd=bdec[h]: e.scalar_tensor_tensor(o, i, bd, p, AL.mult, AL.add), VE)
                    DRAIN("vector")
                    vKb = OP("vector", lambda e, o=kv_nxt_b, i=kv_nxt: e.tensor_copy(o, i), VE)
                    kv_ve[hh] = vKb
                    bank_free[kb] = (VE, vKb)
            dq_slot_ve[sdq] = VE.n
        attn_se = SE.n
        if stop_phase == "attn":
            return

        ofm_h = ap(o_ofm, 8192, BF16)
        o2_slot_te = {}
        st_te = None
        for fc in range(FC):
            slot = fc % 2
            o2 = ap(o_o2 + slot * 512, 512, BF16)
            W("scalar", TE, o2_slot_te.get(slot))
            if fc < 2:
                W("scalar", TE, lnt_slot_te.get(fc % 2))
            sq = OP("scalar", lambda e, o=o2, i=ofm_h[:, fc * 1024:(fc + 1) * 1024]:
                    e.activation(out=o, in_=i, func=AF.Square), SE)
            W("tensor", SE, sq)
            if fc == 0:
                bank_wait("tensor", 0); bank_wait("tensor", 1)
            for half in range(2):
                st_te = OP("tensor", lambda e, p=PS[half][0:1, 0:512], l=ap(o_ones, 64, BF16)[:, 0:1],
                           r=o2[:, half * 512:(half + 1) * 512], st=(fc == 0), sp=(fc == FC - 1):
                           e.matmul(p, l, r, start=st, stop=sp), TE if half == 1 else None)
            o2_slot_te[slot] = st_te
        W("scalar", TE, st_te)
        srow = ap(o_srow, 1024, parts=(0, 1))
        sc1 = None
        for half in range(2):
            sc1 = OP("scalar", lambda e, o=srow[:, half * 512:(half + 1) * 512], p=PS[half][0:1, 0:512]:
                     e.copy(o, p), SE if half == 1 else None)
        bank_free[0] = (SE, sc1); bank_free[1] = (SE, sc1)
        W("vector", SE, sc1)
        OP("vector", lambda e, o=srow, s=1.0 / D, b=EPS_RMS: e.tensor_scalar(o, o, s, b, AL.mult, AL.add))
        DRAIN("vector")
        vr = OP("vector", lambda e, o=srow: e.reciprocal(o, o), VE)
        W("scalar", VE, vr)
        srow_b = ap(o_srow + 1024, 512, BF16, parts=(0, 1))
        sq2 = OP("scalar", lambda e, o=srow_b, i=srow:
                 e.activation(out=o, in_=i, func=AF.Sqrt), SE)
        W("tensor", SE, sq2)
        bank_wait("tensor", 4); bank_wait("tensor", 5)
        bc_te = None
        for half in range(2):
            bc_te = OP("tensor", lambda e, p=PS[4 + half][:, 0:512], l=ap(o_ones, 64, BF16)[0:1, 0:128],
                       r=srow_b[:, half * 512:(half + 1) * 512]:
                       e.matmul(p, l, r, start=True, stop=True), TE if half == 1 else None)
        W("vector", TE, bc_te)

        ogt_h = ap(o_ogt, 8192, BF16)
        wg_slot_te = {0: qk_res_te.get(0), 1: qk_res_te.get(0)}
        gv_prev = None
        for fc in range(FC):
            slot = fc % 2
            W("gpsimd", TE, wg_slot_te.get(slot))
            ld = OP("gpsimd", lambda e, d=ap(o_qkw + slot * 1024, 1024, BF16), s=w_g[fc]:
                    e.dma_start(out=d, in_=s), S_qk[slot], 16)
            W("tensor", S_qk[slot], ld)
            t1 = None
            for th in range(2):
                bank = slot * 2 + th
                bank_wait("tensor", bank)
                for f in range(FC):
                    lhs = ap(o_qkw + slot * 1024, 1024, BF16)[:, f * 128:(f + 1) * 128]
                    t1 = OP("tensor", lambda e, p=PS[bank][:, 0:512], l=lhs,
                            r=xf[:, f, th * 512:(th + 1) * 512],
                            st=(f == 0), sp=(f == FC - 1): e.matmul(p, l, r, start=st, stop=sp),
                            TE if (th == 1 and f == FC - 1) else None)
            wg_slot_te[slot] = t1
            W("scalar", TE, t1)
            W("scalar", VE, gv_prev)
            sg = None
            for th in range(2):
                sg = OP("scalar", lambda e, o=ap(o_sil, 1024)[:, th * 512:(th + 1) * 512],
                        p=PS[slot * 2 + th][:, 0:512]: e.activation(out=o, in_=p, func=AF.Sigmoid),
                        SE if th == 1 else None)
            bank_free[slot * 2] = (SE, sg); bank_free[slot * 2 + 1] = (SE, sg)
            W("vector", SE, sg)
            W("vector", CONST, const_n)
            for th in range(2):
                OP("vector", lambda e, o=ap(o_sil + 1024, 1024)[:, th * 512:(th + 1) * 512],
                   i=ofm_h[:, fc * 1024 + th * 512: fc * 1024 + (th + 1) * 512],
                   nw=ap(o_normw, FC)[:, fc:fc + 1], p=PS[4 + th][:, 0:512]:
                   e.scalar_tensor_tensor(o, i, nw, p, AL.mult, AL.mult))
            DRAIN("vector")
            gv = None
            for th in range(2):
                gv = OP("vector", lambda e, o=ogt_h[:, fc * 1024 + th * 512: fc * 1024 + (th + 1) * 512],
                        a=ap(o_sil + 1024, 1024)[:, th * 512:(th + 1) * 512],
                        b=ap(o_sil, 1024)[:, th * 512:(th + 1) * 512]: e.tensor_tensor(o, a, b, AL.mult),
                        VE if th == 1 else None)
            gv_prev = gv
        ogt_done_ve = VE.n
        bank_free[4] = (VE, ogt_done_ve); bank_free[5] = (VE, ogt_done_ve)
        if stop_phase == "c4":
            return

        x2b = ap(o_x2b, 8192, BF16)
        x2f = x2b.rearrange("p (f t) -> p f t", f=16)
        d0_slot_te = {0: lnt_slot_te.get(0), 1: lnt_slot_te.get(1)}
        W("tensor", VE, ogt_done_ve)
        wow_slot_te = {}
        wow_half_te = {}
        at4_ve = None
        xresb = ap(o_xres, 8192, BF16)
        for tg in range(2):
            for oc in range(4):
                par = oc % 2
                wbase = o_qkw if par == 0 else o_kvw
                W("gpsimd", TE, wow_half_te.get(par))
                if tg == 0 and oc == 0:
                    W("gpsimd", TE, wg_slot_te.get(0))
                    W("gpsimd", TE, wg_slot_te.get(1))
                if tg == 0 and oc == 1:
                    W("gpsimd", TE, dummy_kv_guard)
                wld1 = OP("gpsimd", lambda e, d=ap(wbase, 2048, BF16), s=w_o[oc, :, 0:4096]:
                          e.dma_start(out=d, in_=s), S_wo[par], 16)
                W("gpsimd", TE, wow_slot_te.get(par))
                wld2 = OP("gpsimd", lambda e, d=ap(wbase + 2048, 2048, BF16), s=w_o[oc, :, 4096:8192]:
                          e.dma_start(out=d, in_=s), S_wo2[par], 16)
                te_oc = None
                for hp in range(2):
                    W("tensor", (S_wo if hp == 0 else S_wo2)[par], wld1 if hp == 0 else wld2)
                    for t in range(4):
                        bank = 4 + t
                        if hp == 0:
                            bank_wait("tensor", bank)
                        for hd in range(hp * 8, hp * 8 + 8):
                            lhs = ogt_h[:, hd * 1024 + (tg * 4 + t) * 128: hd * 1024 + (tg * 4 + t) * 128 + 128]
                            te_oc = OP("tensor", lambda e, p=PS[bank][:, 0:512], l=lhs,
                                       r=ap(wbase, 4096, BF16)[:, hd * 512:(hd + 1) * 512],
                                       st=(hd == 0), sp=(hd == FC - 1): e.matmul(p, l, r, start=st, stop=sp),
                                       TE if hd == hp * 8 + 7 else None)
                    if hp == 0:
                        wow_half_te[par] = te_oc
                wow_slot_te[par] = te_oc
                W("scalar", TE, te_oc)
                if tg == 1 and oc == 0:
                    W("scalar", VE, at4_ve)
                if tg == 0 and oc == 0:
                    W("scalar", VE, ogt_done_ve)
                ev = None
                for t in range(4):
                    bank = 4 + t
                    dst = ap(o_at4 + tg * 4096, 4096, BF16)[:, t * 2048 + oc * 512: t * 2048 + (oc + 1) * 512]
                    ev = OP("scalar", lambda e, o=dst, p=PS[bank][:, 0:512]: e.copy(o, p),
                            SE if t == 3 else None)
                for t in range(4):
                    bank_free[4 + t] = (SE, ev)
            at4_se = SE.n
            for t in range(4):
                tt = tg * 4 + t
                slot = tt % 2
                xt = ap(o_xt + slot * 2048, 2048)
                W("sync", VE, xt_slot_ve.get(slot))
                gx = OP("sync", lambda e, d=xt, s=x_own[tt]: e.dma_start(out=d, in_=s), S_xt[slot], 16)
                W("vector", SE, at4_se)
                at = ap(o_at4 + tg * 4096, 4096, BF16)[:, t * 2048:(t + 1) * 2048]
                ln_tile(at, ap(o_nrm, 2048), EPS_LN)
                W("vector", S_xt[slot], gx)
                DRAIN("vector")
                vres = OP("vector", lambda e, o=xresb[:, tt * 2048:(tt + 1) * 2048],
                          a=ap(o_nrm, 2048), b=xt: e.tensor_tensor(o, a, b, AL.add), VE)
                xt_slot_ve[slot] = vres
            at4_ve = VE.n
            if tg == 0:
                W("scalar", TE, wg_slot_te[(FC - 1) % 2])
                for t in range(4):
                    tt = t
                    slot = tt % 2
                    xo = ap(o_xlntm + slot * 1024, 1024, BF16)
                    W("vector", TE, d0_slot_te.get(slot))
                    if t < 2:
                        W("vector", TE, o2_slot_te.get(slot))
                    vdone = ln_tile(xresb[:, tt * 2048:(tt + 1) * 2048], xo, EPS_LN)
                    W("tensor", VE, vdone)
                    transpose_tile(xo, x2b, tt, (2, 3))
                    d0_slot_te[slot] = TE.n
                ln2_half_se = SE.n
        xres_done_ve = VE.n

        def emit_d0_second_half():
            for t in range(4):
                tt = 4 + t
                slot = tt % 2
                xo = ap(o_xlntm + slot * 1024, 1024, BF16)
                W("vector", TE, d0_slot_te.get(slot))
                vdone = ln_tile(xresb[:, tt * 2048:(tt + 1) * 2048], xo, EPS_LN)
                W("tensor", VE, vdone)
                transpose_tile(xo, x2b, tt, (6, 7))
                d0_slot_te[slot] = TE.n
        if stop_phase == "c5":
            return

        if stop_phase == "d0":
            return
        hsb = ap(o_hsb, 16384, BF16)
        fft_tc_se = {}
        dw_half_te = {}
        gu_slot_te = {}
        gu_gate_te = {}
        dw_slot_te = {}
        sil_slot_ve = {}
        out_lg = {}
        for th in range(2):
            for ic in range(ICN):
                slot = ic % 2
                W("gpsimd", TE, gu_gate_te.get(slot))
                if th == 1 and ic < 2:
                    W("gpsimd", TE, dw_slot_te.get(0))
                    W("gpsimd", TE, dw_slot_te.get(1))
                if th == 0 and ic < 2:
                    W("gpsimd", TE, wow_slot_te.get(0))
                    W("gpsimd", TE, wow_slot_te.get(1))
                ldg = OP("gpsimd", lambda e, d=ap(o_gu + slot * 1024, 1024, BF16), s=w_gate[ic]:
                         e.dma_start(out=d, in_=s), S_gu[slot], 16)
                W("gpsimd", TE, gu_slot_te.get(slot))
                ldu = OP("gpsimd", lambda e, d=ap(o_gu2 + slot * 1024, 1024, BF16), s=w_up[ic]:
                         e.dma_start(out=d, in_=s), S_gu2[slot], 16)
                W("tensor", S_gu[slot], ldg)
                W("tensor", S_gu2[slot], ldu)
                if th == 0 and ic == 0:
                    W("tensor", SE, ln2_half_se)
                if th == 1 and ic == 0:
                    W("tensor", SE, ln2_se)
                bga = slot * 2
                bgb = 1 + slot * 2
                bank_wait("tensor", bga)
                tg1 = None
                for f in range(FC):
                    lhs = ap(o_gu + slot * 1024, 1024, BF16)[:, f * 128:(f + 1) * 128]
                    tg1 = OP("tensor", lambda e, p=PS[bga][:, 0:512], l=lhs,
                             r=x2f[:, f, th * 512:(th + 1) * 512],
                             st=(f == 0), sp=(f == FC - 1): e.matmul(p, l, r, start=st, stop=sp),
                             TE if f == FC - 1 else None)
                bank_wait("tensor", bgb)
                tg2 = None
                for f in range(FC):
                    lhs = ap(o_gu2 + slot * 1024, 1024, BF16)[:, f * 128:(f + 1) * 128]
                    tg2 = OP("tensor", lambda e, p=PS[bgb][:, 0:512], l=lhs,
                             r=x2f[:, f, th * 512:(th + 1) * 512],
                             st=(f == 0), sp=(f == FC - 1): e.matmul(p, l, r, start=st, stop=sp),
                             TE if f == FC - 1 else None)
                gu_slot_te[slot] = tg2
                gu_gate_te[slot] = tg1
                if th == 0 and ic == 3:
                    emit_d0_second_half()
                    ln2_se = SE.n
                W("scalar", TE, tg1)
                W("scalar", VE, sil_slot_ve.get(slot))
                sil = ap(o_sil + slot * 512, 512)
                sev = OP("scalar", lambda e, o=sil, p=PS[bga][:, 0:512]:
                         e.activation(out=o, in_=p, func=AF.Silu), SE)
                bank_free[bga] = (SE, sev)
                W("vector", TE, tg2)
                W("vector", SE, sev)
                hv = OP("vector", lambda e, o=hsb[:, ic * 512:(ic + 1) * 512], p=PS[bgb][:, 0:512], s=sil:
                        e.tensor_tensor(o, p, s, AL.mult), VE)
                sil_slot_ve[slot] = hv
                bank_free[bgb] = (VE, hv)
            h_done_ve = VE.n
            W("tensor", VE, h_done_ve)
            for oh in range(2):
                dte = None
                for icq in range(16):
                    grp = icq % 2
                    doff = o_kvw if grp == 0 else o_gu
                    lds = []
                    W("gpsimd", TE, dw_half_te.get(grp))
                    if oh == 0 and icq < 2:
                        W("gpsimd", TE, gu_slot_te.get(0))
                        W("gpsimd", TE, gu_slot_te.get(1))
                    for j in range(2):
                        ic = icq * 4 + j
                        lds.append(OP("gpsimd", lambda e, d=ap(doff + j * 512, 512, BF16),
                                      s=w_down[oh, ic]: e.dma_start(out=d, in_=s), S_dw[grp], 16))
                    W("gpsimd", TE, dw_slot_te.get(grp))
                    for j in range(2, 4):
                        ic = icq * 4 + j
                        lds.append(OP("gpsimd", lambda e, d=ap(doff + j * 512, 512, BF16),
                                      s=w_down[oh, ic]: e.dma_start(out=d, in_=s), S_dw2[grp], 16))
                    for jp in range(2):
                        W("tensor", (S_dw if jp == 0 else S_dw2)[grp], lds[jp * 2 + 1])
                        for b in range(8):
                            tc4, oc = b // 2, b % 2
                            if icq == 0 and jp == 0:
                                bank_wait("tensor", b)
                            for j in (jp * 2, jp * 2 + 1):
                                ic = icq * 4 + j
                                lhs = hsb[:, ic * 512 + tc4 * 128: ic * 512 + tc4 * 128 + 128]
                                rhs = ap(doff + j * 512, 512, BF16)[:, oc * 512:(oc + 1) * 512]
                                dte = OP("tensor", lambda e, p=PS[b][:, 0:512], l=lhs, r=rhs,
                                         st=(ic == 0), sp=(ic == ICN - 1): e.matmul(p, l, r, start=st, stop=sp),
                                         TE if (b == 7 and j == jp * 2 + 1) else None)
                        if jp == 0:
                            dw_half_te[grp] = dte
                    dw_slot_te[grp] = dte
                W("scalar", TE, dte)
                if th == 0 and oh == 0:
                    W("scalar", TE, wg_slot_te[(FC - 1) % 2])
                ev = None
                for b in range(8):
                    tc4, oc = b // 2, b % 2
                    dst = ap(o_fft, 4096, BF16)[:, tc4 * 2048 + oh * 1024 + oc * 512:
                                                tc4 * 2048 + oh * 1024 + (oc + 1) * 512]
                    ev = OP("scalar", lambda e, o=dst, p=PS[b][:, 0:512]: e.copy(o, p), SE)
                    if oh == 1:
                        fft_tc_se[tc4] = ev
                for b in range(8):
                    bank_free[b] = (SE, ev)
            for tc4 in range(4):
                tt = th * 4 + tc4
                slot = tt % 2
                W("vector", SE, fft_tc_se[tc4])
                ft = ap(o_fft, 4096, BF16)[:, tc4 * 2048:(tc4 + 1) * 2048]
                ln_tile(ft, ap(o_nrm, 2048), EPS_LN)
                DRAIN("vector")
                ot = ap(o_outt + slot * 2048, 2048)
                if tt >= 2:
                    W("vector", S_out[slot], out_lg[tt - 2])
                else:
                    W("vector", VE, xt_slot_ve.get(slot))
                vo = OP("vector", lambda e, o=ot, a=ap(o_nrm, 2048),
                        b=xresb[:, tt * 2048:(tt + 1) * 2048]: e.tensor_tensor(o, a, b, AL.add), VE)
                W("gpsimd", VE, vo)
                out_lg[tt] = OP("gpsimd", lambda e, d=out_d[tt], s=ot: e.dma_start(out=d, in_=s), S_out[slot], 16)

        W("gpsimd", S_out[0], S_out[0].n)
        W("gpsimd", S_out[1], S_out[1].n)

    with (
        nc.sbuf_tensor("AR", [128, AWORDS], F32) as AR_t,
        nc.psum_tensor("p0", [128, 512], F32) as p0,
        nc.psum_tensor("p1", [128, 512], F32) as p1,
        nc.psum_tensor("p2", [128, 512], F32) as p2,
        nc.psum_tensor("p3", [128, 512], F32) as p3,
        nc.psum_tensor("p4", [128, 512], F32) as p4,
        nc.psum_tensor("p5", [128, 512], F32) as p5,
        nc.psum_tensor("p6", [128, 512], F32) as p6,
        nc.psum_tensor("p7", [128, 512], F32) as p7,
        nc.Block() as block,
    ):
        import contextlib
        with contextlib.ExitStack() as stack:
            for sm in SEMS:
                sm.h = stack.enter_context(nc.semaphore(sm.name))
            ARh[0] = AR_t.ap()
            PS.extend([p.ap() for p in (p0, p1, p2, p3, p4, p5, p6, p7)])
            logical()
            _emit_all(block, prog)
    return nc


def _emit_all(block, prog):
    if True:

        @block.sync
        def _(e):
            for fn in prog["sync"]:
                fn(e)

        @block.tensor
        def _(e):
            for fn in prog["tensor"]:
                fn(e)

        @block.vector
        def _(e):
            for fn in prog["vector"]:
                fn(e)

        @block.scalar
        def _(e):
            for fn in prog["scalar"]:
                fn(e)

        @block.gpsimd
        def _(e):
            for fn in prog["gpsimd"]:
                fn(e)



def _host_prep(L0_attn_x, L0_attn_Wqkv, L0_attn_Wg, L0_attn_Wout, L0_attn_norm_w,
               L0_ffn_gate_w, L0_ffn_up_w, L0_ffn_down_w):
    import ml_dtypes
    f32 = np.float32
    slopes = np.array(_slopes(), dtype=np.float64)
    arr = np.arange(1, BLK + 1, dtype=np.float64)

    qdec = np.exp(-slopes[:, None] * arr[None, :])
    kdec = np.exp(-slopes[:, None] * (BLK - arr[None, :]))
    idx = arr[:, None] - arr[None, :]
    dd = np.where(idx[None] >= 0, np.exp(-slopes[:, None, None] * idx[None]), 0.0)

    qdec_in = np.ascontiguousarray(np.broadcast_to(
        qdec.reshape(NHG, 2, 1, BLK), (NHG, 2, 128, BLK)).transpose(0, 2, 1, 3).reshape(
        NHG, 128, 512)).astype(np.float16)
    kdec_in = np.ascontiguousarray(kdec.reshape(H, 2, 128).transpose(2, 0, 1).reshape(128, 2 * H)).astype(f32)
    bd = np.exp(-slopes * BLK)
    kdp = np.zeros((128, 128))
    for h in range(H):
        for b in range(4):
            for u in range(2):
                kdp[:, h * 8 + b * 2 + u] = kdec[h, u * 128:(u + 1) * 128] * bd[h] ** (3 - b)
    kdp_in = np.ascontiguousarray(kdp).astype(f32)
    ddt = dd.transpose(0, 2, 1)
    ddt_in = np.ascontiguousarray(ddt.reshape(NHG, 2, 2, 128, BLK).transpose(0, 3, 1, 2, 4).reshape(
        NHG, 128, 1024)).astype(np.float16)

    Wqkv = np.asarray(L0_attn_Wqkv, dtype=f32)
    qk = np.empty((NHG, 4, FC, 128, 128), dtype=f32)
    vt = np.empty((NHG, FC, 128, 256), dtype=f32)
    kv = np.empty((NHG, FC, 128, 512), dtype=f32)
    Wq3 = Wqkv.reshape(FC, 128, 6144)
    for g in range(NHG):
        for hh in range(2):
            h = 2 * g + hh
            base = 384 * h
            qk[g, 2 * hh + 0] = Wq3[:, :, base:base + 128]
            qk[g, 2 * hh + 1] = Wq3[:, :, base + 128:base + 256]
            vt[g, :, :, hh * 128:(hh + 1) * 128] = Wq3[:, :, base + 256:base + 384]
            kv[g, :, :, hh * 256:hh * 256 + 128] = Wq3[:, :, base + 128:base + 256]
            kv[g, :, :, hh * 256 + 128:hh * 256 + 256] = Wq3[:, :, base + 256:base + 384]

    qk = np.ascontiguousarray(
        qk.transpose(0, 1, 3, 2, 4).reshape(NHG, 4, 128, FC * 128).transpose(0, 2, 1, 3).reshape(
            NHG, 128, 4 * FC * 128))
    kv = np.ascontiguousarray(
        kv.reshape(NHG, 4, 4, 128, 512).transpose(0, 1, 3, 2, 4).reshape(NHG, 4, 128, 2048))
    vt = np.ascontiguousarray(vt.transpose(0, 2, 1, 3).reshape(NHG, 128, FC * 256))
    Wg = np.asarray(L0_attn_Wg, dtype=f32).reshape(FC, 128, FC, 128)
    wg_in = np.ascontiguousarray(Wg.transpose(2, 1, 0, 3).reshape(FC, 128, FC * 128))
    Wo = np.asarray(L0_attn_Wout, dtype=f32).reshape(FC, 128, 4, 512)
    wo_in = np.ascontiguousarray(Wo.transpose(2, 1, 0, 3).reshape(4, 128, FC * 512))

    gT = np.asarray(L0_ffn_gate_w, dtype=f32).T.reshape(FC, 128, ICN, 128)
    gw_in = np.ascontiguousarray(gT.transpose(2, 1, 0, 3).reshape(ICN, 128, FC * 128))
    uT = np.asarray(L0_ffn_up_w, dtype=f32).T.reshape(FC, 128, ICN, 128)
    uw_in = np.ascontiguousarray(uT.transpose(2, 1, 0, 3).reshape(ICN, 128, FC * 128))
    dT = np.asarray(L0_ffn_down_w, dtype=f32).T.reshape(ICN, 128, 2, 1024)
    dw_in = np.ascontiguousarray(dT.transpose(2, 0, 1, 3))

    nw = np.asarray(L0_attn_norm_w, dtype=f32).reshape(FC, 128)
    nw_in = np.ascontiguousarray(nw.T)

    consts = {
        "id_b": np.eye(128, dtype=f32).astype(ml_dtypes.bfloat16),
        "ones_r": np.ones((128, 128), dtype=f32).astype(ml_dtypes.bfloat16),
        "kdec": kdec_in,
        "kdecp": kdp_in,
        "normw": nw_in,
        "qdec": qdec_in,
        "ddt": ddt_in,
        "w_qk": qk, "w_v": vt, "w_kv": kv, "w_g": wg_in, "w_o": wo_in,
        "w_gate": gw_in, "w_up": uw_in, "w_down": dw_in,
    }

    x = np.asarray(L0_attn_x, dtype=f32)
    in_maps = []
    for c in range(8):
        s, half = c // 2, c % 2
        xo = np.ascontiguousarray(x[s, half * T:(half + 1) * T].reshape(TC, 128, D))
        if half == 1:
            xp = np.ascontiguousarray(x[s, 0:T].reshape(TC, 128, D))
            mk = np.ones((128, 1), dtype=f32)
        else:
            xp = xo
            mk = np.zeros((128, 1), dtype=f32)
        m = dict(consts)
        m["x_own"] = xo
        m["x_prev"] = xp
        m["maskc"] = mk
        in_maps.append(m)
    return in_maps


_NC_CACHE = [None]


def kernel(L0_attn_x, L0_attn_Wqkv, L0_attn_Wg, L0_attn_Wout, L0_attn_norm_w,
           L0_ffn_gate_w, L0_ffn_up_w, L0_ffn_down_w):
    in_maps = _host_prep(L0_attn_x, L0_attn_Wqkv, L0_attn_Wg, L0_attn_Wout, L0_attn_norm_w,
                         L0_ffn_gate_w, L0_ffn_up_w, L0_ffn_down_w)
    if _NC_CACHE[0] is None:
        _NC_CACHE[0] = build_program()
    nc = _NC_CACHE[0]
    res = run_bass_kernel_spmd(nc, in_maps, core_ids=list(range(8)))
    out = np.empty((4, 2048, D), dtype=np.float32)
    for c in range(8):
        s, half = c // 2, c % 2
        out[s, half * T:(half + 1) * T] = np.asarray(res.results[c]["out"]).reshape(T, D)
    return out


# version 25
# speedup vs baseline: 1.0803x; 1.0102x over previous
import math
import numpy as np
import concourse.bass as bass
import concourse.mybir as mybir
from concourse.bass_utils import run_bass_kernel_spmd

F32 = mybir.dt.float32
F32R = mybir.dt.float32r
BF16 = mybir.dt.bfloat16
FP16 = mybir.dt.float16
AL = mybir.AluOpType
AF = mybir.ActivationFunctionType

T = 1024
D = 2048
H = 16
BLK = 256
NB = T // BLK
TC = T // 128
FC = D // 128
ICN = 64
NHG = 8
EPS_LN = 1e-5
EPS_RMS = 1e-6

ENGS = ("sync", "tensor", "vector", "scalar", "gpsimd")


def _slopes():
    start = 2 ** (-2 ** -(math.log2(H) - 3))
    return [start * start ** i for i in range(H)]


def build_program(stop_phase=None):
    nc = bass.Bass("TRN2", target_bir_lowering=False)
    slopes = _slopes()
    bdec = [float(np.exp(np.float32(-s * BLK))) for s in slopes]

    x_own = nc.dram_tensor("x_own", [TC, 128, D], F32, kind="ExternalInput").ap()
    x_prev = nc.dram_tensor("x_prev", [TC, 128, D], F32, kind="ExternalInput").ap()
    maskc = nc.dram_tensor("maskc", [128, 1], F32, kind="ExternalInput").ap()
    id_b = nc.dram_tensor("id_b", [128, 128], BF16, kind="ExternalInput").ap()
    ones_r = nc.dram_tensor("ones_r", [128, 128], BF16, kind="ExternalInput").ap()
    kdec_d = nc.dram_tensor("kdec", [128, 2 * H], F32, kind="ExternalInput").ap()
    kdecp_d = nc.dram_tensor("kdecp", [128, 128], F32, kind="ExternalInput").ap()
    normw_d = nc.dram_tensor("normw", [128, FC], F32, kind="ExternalInput").ap()
    qdec_d = nc.dram_tensor("qdec", [NHG, 128, 512], FP16, kind="ExternalInput").ap()
    ddt_d = nc.dram_tensor("ddt", [NHG, 128, 1024], FP16, kind="ExternalInput").ap()
    w_kv = nc.dram_tensor("w_kv", [NHG, 4, 128, 2048], F32, kind="ExternalInput").ap()
    w_qk = nc.dram_tensor("w_qk", [NHG, 128, 4 * FC * 128], F32, kind="ExternalInput").ap()
    w_v = nc.dram_tensor("w_v", [NHG, 128, FC * 256], F32, kind="ExternalInput").ap()
    w_g = nc.dram_tensor("w_g", [FC, 128, FC * 128], F32, kind="ExternalInput").ap()
    w_o = nc.dram_tensor("w_o", [4, 128, FC * 512], F32, kind="ExternalInput").ap()
    w_gate = nc.dram_tensor("w_gate", [ICN, 128, FC * 128], F32, kind="ExternalInput").ap()
    w_up = nc.dram_tensor("w_up", [ICN, 128, FC * 128], F32, kind="ExternalInput").ap()
    w_down = nc.dram_tensor("w_down", [2, ICN, 128, 1024], F32, kind="ExternalInput").ap()
    out_d = nc.dram_tensor("out", [TC, 128, D], F32, kind="ExternalOutput").ap()

    AWORDS = 53120
    _ptr = [0]
    def region(words, align=128):
        p = (_ptr[0] + align - 1) // align * align
        assert p + words <= AWORDS, f"arena overflow {p}+{words}"
        _ptr[0] = p + words
        return p

    o_idb = region(64)
    o_ones = region(128)
    o_kdec = region(32)
    o_kdecp = region(128)
    o_normw = region(16)
    o_mask = region(8)
    o_stats = region(64)
    o_mv = region(16)
    o_inv = region(16)
    o_xln = region(8192)
    o_ofm = region(8192)
    o_grp = region(8192)
    o_scr = region(8192)
    o_xt = region(4096)
    o_xlntm = region(2048)
    o_nrm = region(2048)
    o_qkw = region(4096)
    o_kvw = region(2048)
    o_gu = region(2048)
    o_sil = region(2048)
    o_vw = region(512)

    o_srow = o_nrm
    o_at4 = o_ofm
    o_x2b = o_xln
    o_hsb = o_ofm
    o_ogt = o_grp
    o_scr_qdec = o_scr
    o_scr_dds = o_scr + 512
    o_scr_kvini = o_scr + 1536
    o_scr_kvpp = o_scr + 2560
    o_scr_kvppb = o_scr + 3072
    o_scr_sT = o_scr + 3328
    o_scr_ksc = o_scr + 3584
    o_scr_vsc = o_scr + 3840
    o_pro_ksc = o_scr + 4096
    o_kvp = o_nrm
    o_xres = o_scr
    o_o2 = o_xlntm
    o_fft = o_qkw
    o_gu2 = o_kvw
    o_outt = o_xt
    o_wow = o_vw

    prog = {e: [] for e in ENGS}
    last_wait = {e: {} for e in ENGS}

    class Sem:
        def __init__(self, name):
            self.name = name
            self.n = 0
            self.h = None

    TE, VE, SE = Sem("TE"), Sem("VE"), Sem("SE")
    SEMS = [TE, VE, SE]

    def mksem(name):
        sm = Sem(name)
        SEMS.append(sm)
        return sm

    CONST = mksem("cst")
    S_xt = [mksem("sxt0"), mksem("sxt1")]
    S_kv = [mksem("skv0"), mksem("skv1")]
    S_kv2 = [mksem("skv20"), mksem("skv21")]
    S_dq = [mksem("sdq0"), mksem("sdq1")]
    S_qk = [mksem("sqk0"), mksem("sqk1")]
    S_vw = [mksem("svw0"), mksem("svw1")]
    S_wo = [mksem("swo0"), mksem("swo1")]
    S_gu = [mksem("sgu0"), mksem("sgu1")]
    S_gu2 = [mksem("sgu20"), mksem("sgu21")]
    S_dw = [mksem("sdw0"), mksem("sdw1")]
    S_dw2 = [mksem("sdw20"), mksem("sdw21")]
    S_wo2 = [mksem("swo20"), mksem("swo21")]
    S_out = [mksem("sou0"), mksem("sou1")]

    def E(eng, fn):
        prog[eng].append(fn)

    def W(eng, sem, val):
        if val is None or val <= 0:
            return
        assert val <= sem.n, f"wait {sem.name}>={val} but only {sem.n} planned"
        if last_wait[eng].get(sem.name, -1) >= val:
            return
        last_wait[eng][sem.name] = val
        E(eng, lambda e, s=sem, v=val: e.wait_ge(s.h, v))

    def OP(eng, fn, sem=None, k=1):
        if sem is None:
            E(eng, lambda e, f=fn: f(e))
            return None
        sem.n += k
        def run(e, f=fn, s=sem, kk=k):
            f(e).then_inc(s.h, kk)
        E(eng, run)
        return sem.n

    def DRAIN(eng):
        E(eng, lambda e: e.drain())

    ARh = [None]
    PS = []

    def ap(off, words, dtype=F32, parts=(0, 128)):
        a = ARh[0][parts[0]:parts[1], off:off + words]
        if dtype == F32:
            return a
        return a.bitcast(dtype)

    bank_free = {}

    def bank_wait(eng, bank):
        ent = bank_free.get(bank)
        if ent is not None:
            W(eng, ent[0], ent[1])

    def logical():
        for dst, src in (
            (ap(o_idb, 64, BF16), id_b),
            (ap(o_ones, 64, BF16), ones_r),
            (ap(o_kdec, 2 * H), kdec_d),
            (ap(o_kdecp, 128), kdecp_d),
            (ap(o_normw, FC), normw_d),
            (ap(o_mask, 1), maskc),
        ):
            OP("sync", lambda e, d=dst, s=src: e.dma_start(out=d, in_=s), CONST, 16)
        const_n = CONST.n

        ln_par = [0]
        def ln_tile(src_ap, out_ap, eps):
            ln_par[0] ^= 1
            pr = ln_par[0]
            st = ap(o_stats + pr * 32, 24); mv = ap(o_mv + pr * 8, 2); inv = ap(o_inv + pr * 8, 1)
            for g in range(4):
                OP("vector", lambda e, a=st[:, g * 6:(g + 1) * 6],
                   b=src_ap[:, g * 512:(g + 1) * 512]: e.bn_stats(a, b))
            DRAIN("vector")
            OP("vector", lambda e, a=mv, b=st: e.bn_aggr(a, b))
            DRAIN("vector")
            OP("vector", lambda e, a=inv, b=mv[:, 1:2], ep=eps: e.tensor_scalar(a, b, ep, None, AL.add))
            DRAIN("vector")
            v1 = OP("vector", lambda e, a=inv: e.reciprocal(a, a), VE)
            W("scalar", VE, v1)
            s1 = OP("scalar", lambda e, a=inv: e.activation(out=a, in_=a, func=AF.Sqrt), SE)
            W("vector", SE, s1)
            return OP("vector", lambda e, o=out_ap, x=src_ap, m=mv[:, 0:1], iv=inv:
                      e.tensor_scalar(o, x, m, iv, AL.subtract, AL.mult), VE)

        def transpose_tile(src_ap, dst_ap, tcol, banks):
            W("tensor", CONST, const_n)
            d3 = dst_ap.rearrange("p (f t) -> p f t", f=16)
            for q in range(4):
                bank = banks[q % len(banks)]
                bank_wait("tensor", bank)
                te = None
                for j in range(4):
                    f = q * 4 + j
                    pdst = PS[bank][:, j * 64:(j + 1) * 64].bitcast(BF16)
                    te = OP("tensor", lambda e, p=pdst, s=src_ap[:, f * 128:(f + 1) * 128],
                            i=ap(o_idb, 64, BF16): e.transpose(p, s, i),
                            TE if j == 3 else None)
                W("scalar", TE, te)
                psrc = PS[bank][:, 0:256].bitcast(BF16).rearrange("p (f c) -> p f c", f=4)
                dsl = d3[:, q * 4:(q + 1) * 4, tcol * 128:(tcol + 1) * 128]
                ev = OP("scalar", lambda e, o=dsl, s=psrc: e.copy(o, s), SE)
                bank_free[bank] = (SE, ev)

        xt_slot_ve = {}
        def ln_transpose_stream(x_dram, dst_ap, slot_te):
            for tc in range(TC):
                slot = tc % 2
                xt = ap(o_xt + slot * 2048, 2048)
                W("sync", VE, xt_slot_ve.get(slot))
                g1 = OP("sync", lambda e, d=xt, s=x_dram[tc]: e.dma_start(out=d, in_=s), S_xt[slot], 16)
                W("vector", S_xt[slot], g1)
                W("vector", TE, slot_te.get(slot))
                xo = ap(o_xlntm + slot * 1024, 1024, BF16)
                vdone = ln_tile(xt, xo, EPS_LN)
                xt_slot_ve[slot] = vdone
                W("tensor", VE, vdone)
                transpose_tile(xo, dst_ap, tc, (4, 5, 6, 7))
                slot_te[slot] = TE.n
            return SE.n

        xf_b = ap(o_xln, 8192, BF16)
        xf = xf_b.rearrange("p (f t) -> p f t", f=16)
        xp_b = ap(o_ofm, 8192, BF16)
        xp = xp_b.rearrange("p (f t) -> p f t", f=16)
        lnt_slot_te = {}
        lnp_se = ln_transpose_stream(x_prev, xp_b, lnt_slot_te)

        if stop_phase == "lnp":
            return
        kv_slot_te = {}
        pro_ksc_te = {}
        W("tensor", SE, lnp_se)
        kv_half_te = {}
        for hg in range(NHG):
            mm_te = None
            for fq in range(4):
                grp = fq % 2
                W("gpsimd", TE, kv_half_te.get(grp))
                ld1 = OP("gpsimd", lambda e, d=ap(o_kvw + grp * 1024, 512, BF16),
                         s=w_kv[hg, fq, :, 0:1024]: e.dma_start(out=d, in_=s), S_kv[grp], 16)
                W("gpsimd", TE, kv_slot_te.get(grp))
                ld2 = OP("gpsimd", lambda e, d=ap(o_kvw + grp * 1024 + 512, 512, BF16),
                         s=w_kv[hg, fq, :, 1024:2048]: e.dma_start(out=d, in_=s), S_kv2[grp], 16)
                for jp in range(2):
                    W("tensor", (S_kv if jp == 0 else S_kv2)[grp], ld1 if jp == 0 else ld2)
                    for b in range(NB):
                        for u in range(2):
                            bank = b * 2 + u
                            if fq == 0 and jp == 0:
                                bank_wait("tensor", bank)
                            lhs0 = b * 256 + u * 128
                            for j in (jp * 2, jp * 2 + 1):
                                f = fq * 4 + j
                                is_mark = (b == NB - 1 and u == 1 and j == jp * 2 + 1)
                                mm_te = OP("tensor", lambda e, p=PS[bank][:, 0:512],
                                           l=xp[:, f, lhs0:lhs0 + 128],
                                           r=ap(o_kvw + grp * 1024, 1024, BF16)[:, j * 512:(j + 1) * 512],
                                           st=(f == 0), sp=(f == FC - 1): e.matmul(p, l, r, start=st, stop=sp),
                                           TE if is_mark else None)
                    if jp == 0:
                        kv_half_te[grp] = mm_te
                kv_slot_te[grp] = mm_te
            a_mm_te = None
            for b in range(NB):
                sdone = None
                W("scalar", CONST, const_n)
                for u in range(2):
                    bank = b * 2 + u
                    W("scalar", TE, kv_slot_te.get(0))
                    W("scalar", TE, kv_slot_te.get(1))
                    W("scalar", TE, pro_ksc_te.get("t"))
                    kvsil = ap(o_scr_ksc + u * 256, 256, BF16)
                    e1 = OP("scalar", lambda e, o=kvsil, p=PS[bank][:, 0:512]:
                            e.activation(out=o, in_=p, func=AF.Silu), SE)
                    bank_free[bank] = (SE, e1)
                    W("vector", SE, e1)
                    W("vector", TE, pro_ksc_te.get("t"))
                    for hh in range(2):
                        h = hg * 2 + hh
                        ksc = ap(o_pro_ksc + (u * 2 + hh) * 64, 64, BF16)
                        kcol = ap(o_kdecp, 128)[:, (h * 8 + b * 2 + u):(h * 8 + b * 2 + u) + 1]
                        sdone = OP("vector", lambda e, o=ksc, i=kvsil[:, hh * 256:hh * 256 + 128], sc=kcol:
                                   e.tensor_scalar(o, i, sc, None, AL.mult),
                                   VE if (u == 1 and hh == 1) else None)
                for hh in range(2):
                    bank = hh
                    if b == 0:
                        bank_wait("tensor", bank)
                    W("tensor", VE, sdone)
                    for u in range(2):
                        ksc = ap(o_pro_ksc + (u * 2 + hh) * 64, 64, BF16)
                        vsl = ap(o_scr_ksc + u * 256, 256, BF16)[:, hh * 256 + 128:hh * 256 + 256]
                        a_mm_te = OP("tensor", lambda e, p=PS[bank][:, 0:128], l=ksc, r=vsl,
                                     st=(b == 0 and u == 0), sp=(b == NB - 1 and u == 1):
                                     e.matmul(p, l, r, start=st, stop=sp),
                                     TE if (u == 1 and hh == 1) else None)
                pro_ksc_te["t"] = a_mm_te
            W("scalar", TE, a_mm_te)
            for hh in range(2):
                h = hg * 2 + hh
                vv = OP("scalar", lambda e, o=ap(o_scr_kvini + h * 64, 64, BF16), i=PS[hh][:, 0:128],
                        m=ap(o_mask, 1): e.activation(out=o, in_=i, func=AF.Copy, scale=m), SE)
                bank_free[hh] = (SE, vv)
        kv_init_done_se = SE.n

        pro_sil_se = SE.n
        if stop_phase == "prologue":
            return
        ln1_se = ln_transpose_stream(x_own, xf_b, lnt_slot_te)
        if stop_phase == "ln1":
            return
        g_q, g_qs, g_kf, g_ks, g_vt = (o_grp, o_grp + 1024, o_grp + 2048, o_grp + 3072, o_grp + 4096)

        def gsl(base, hh, lo, n):
            return ap(base, 1024, BF16)[:, hh * 1024 + lo: hh * 1024 + lo + n]

        qk_slot_te = {}
        qk_res_te = {}
        vw_slot_te = {}
        dq_slot_ve = {}
        sT_rd = {0: None, 1: None}
        kv_ve = {}
        dummy_kv_guard = kv_slot_te.get(0)
        W("tensor", SE, ln1_se)
        for hg in range(NHG):
            sdq = hg % 2
            if hg == 0:
                W("sync", SE, kv_init_done_se)
            W("sync", VE, dq_slot_ve.get(sdq))
            OP("sync", lambda e, d=ap(o_scr_dds + sdq * 512, 512, FP16), s=ddt_d[hg]:
               e.dma_start(out=d, in_=s), S_dq[sdq], 16)
            dqld = OP("sync", lambda e, d=ap(o_scr_qdec + sdq * 256, 256, FP16), s=qdec_d[hg]:
                      e.dma_start(out=d, in_=s), S_dq[sdq], 16)

            W("gpsimd", TE, qk_res_te.get(0))
            if hg == 0:
                W("gpsimd", TE, kv_slot_te.get(0))
            qk_ld = OP("gpsimd", lambda e, d=ap(o_qkw, 4096, BF16), s=w_qk[hg]:
                       e.dma_start(out=d, in_=s), S_qk[0], 16)
            W("tensor", S_qk[0], qk_ld)
            t1 = None
            for cc in range(4):
                hh, isk = cc // 2, cc % 2
                for th in range(2):
                    bank = (cc % 2) * 2 + th
                    bank_wait("tensor", bank)
                    for f in range(FC):
                        lhs = ap(o_qkw, 4096, BF16)[:, cc * 2048 + f * 128: cc * 2048 + (f + 1) * 128]
                        t1 = OP("tensor", lambda e, p=PS[bank][:, 0:512], l=lhs,
                                r=xf[:, f, th * 512:(th + 1) * 512],
                                st=(f == 0), sp=(f == FC - 1): e.matmul(p, l, r, start=st, stop=sp),
                                TE if f == FC - 1 else None)
                    W("scalar", TE, t1)
                    dst = gsl(g_kf if isk else g_q, hh, th * 512, 512)
                    ev = OP("scalar", lambda e, o=dst, p=PS[bank][:, 0:512]:
                            e.activation(out=o, in_=p, func=AF.Silu), SE)
                    bank_free[bank] = (SE, ev)
            qk_res_te[0] = t1
            qk_done_se = SE.n

            W("vector", S_dq[sdq], dqld)
            W("vector", SE, qk_done_se)
            for hh in range(2):
                for b in range(NB):
                    OP("vector", lambda e, o=gsl(g_qs, hh, b * 256, 256),
                       a=gsl(g_q, hh, b * 256, 256),
                       q=ap(o_scr_qdec + sdq * 256, 256, FP16)[:, hh * 256:hh * 256 + 256]:
                       e.tensor_tensor(o, a, q, AL.mult), VE)
            qsc_done = VE.n

            for hh in range(2):
                h = hg * 2 + hh
                for q in range(2):
                    bank = 4 + (q % 2)
                    bank_wait("tensor", bank)
                    te = None
                    for j in range(4):
                        tcl = q * 4 + j
                        te = OP("tensor", lambda e, p=PS[bank][:, j * 64:(j + 1) * 64].bitcast(BF16),
                                s=gsl(g_kf, hh, tcl * 128, 128), i=ap(o_idb, 64, BF16):
                                e.transpose(p, s, i), TE if j == 3 else None)
                    W("scalar", TE, te)
                    ev = None
                    for j in range(4):
                        tcl = q * 4 + j
                        kcol = ap(o_kdec, 2 * H)[:, h * 2 + tcl % 2: h * 2 + tcl % 2 + 1]
                        ev = OP("scalar", lambda e, o=gsl(g_ks, hh, tcl * 128, 128),
                                s=PS[bank][:, j * 64:(j + 1) * 64].bitcast(BF16), sc=kcol:
                                e.activation(out=o, in_=s, func=AF.Copy, scale=sc),
                                SE if j == 3 else None)
                    bank_free[bank] = (SE, ev)
            kst_done_se = SE.n

            W("gpsimd", TE, vw_slot_te.get(0))
            v_ld = OP("gpsimd", lambda e, d=ap(o_sil, 2048, BF16), s=w_v[hg]:
                      e.dma_start(out=d, in_=s), S_vw[0], 16)
            W("tensor", S_vw[0], v_ld)
            v_last_te = None
            for tcl in range(TC):
                bank = tcl
                bank_wait("tensor", bank)
                for f in range(FC):
                    v_last_te = OP("tensor", lambda e, p=PS[bank][:, 0:256],
                                   l=xf[:, f, tcl * 128:(tcl + 1) * 128],
                                   r=ap(o_sil, 2048, BF16)[:, f * 256:(f + 1) * 256],
                                   st=(f == 0), sp=(f == FC - 1): e.matmul(p, l, r, start=st, stop=sp),
                                   TE if f == FC - 1 else None)
            vw_slot_te[0] = v_last_te
            W("scalar", TE, v_last_te)
            v_ev_se = {}
            for tcl in range(TC):
                bank = tcl
                pv = PS[bank][:, 0:256].rearrange("p (hh c) -> p hh c", hh=2)
                dv = ap(g_vt, 1024, BF16).rearrange("p (hh t) -> p hh t", hh=2)[:, :, tcl * 128:(tcl + 1) * 128]
                ev = OP("scalar", lambda e, o=dv, s=pv: e.activation(out=o, in_=s, func=AF.Silu), SE)
                bank_free[bank] = (SE, ev)
                v_ev_se[tcl] = ev
            v_done_se = SE.n

            W("vector", S_dq[sdq], dqld)
            for b in range(NB):
                sT_ve = {}
                kvmm_te = {}
                for hh in range(2):
                    h = hg * 2 + hh
                    sb = (4, 5) if hh == 0 else (0, 1)
                    for nh in range(2):
                        bank = sb[nh]
                        bank_wait("tensor", bank)
                        OP("tensor", lambda e, p=PS[bank][:, 0:256],
                           l=gsl(g_kf, hh, b * 256 + nh * 128, 128),
                           r=gsl(g_q, hh, b * 256, 256): e.matmul(p, l, r, start=True, stop=True),
                           TE if nh == 1 else None)
                    sT_ve[hh] = TE.n
                W("tensor", SE, v_ev_se[2 * b + 1])
                for hh in range(2):
                    kb = 7 if hh == 0 else 3
                    bank_wait("tensor", kb)
                    tK = None
                    for u in range(2):
                        tK = OP("tensor", lambda e, p=PS[kb][:, 0:128],
                                l=gsl(g_ks, hh, (2 * b + u) * 128, 128),
                                r=gsl(g_vt, hh, (2 * b + u) * 128, 128),
                                st=(u == 0), sp=(u == 1): e.matmul(p, l, r, start=st, stop=sp),
                                TE if u == 1 else None)
                    kvmm_te[hh] = tK
                sT_done = {}
                for hh in range(2):
                    sb = (4, 5) if hh == 0 else (0, 1)
                    W("vector", TE, sT_ve[hh])
                    W("vector", TE, sT_rd[hh])
                    for nh in range(2):
                        vv = OP("vector", lambda e, o=ap(o_scr_sT + (hh * 2 + nh) * 128, 128, BF16),
                                p=PS[sb[nh]][:, 0:256],
                                d=ap(o_scr_dds + sdq * 512, 512, FP16)[:, (hh * 2 + nh) * 256:(hh * 2 + nh + 1) * 256]:
                                e.tensor_tensor(o, p, d, AL.mult), VE)
                        bank_free[sb[nh]] = (VE, vv)
                    sT_done[hh] = VE.n
                for hh in range(2):
                    h = hg * 2 + hh
                    ob = 6 if hh == 0 else 2
                    bank_wait("tensor", ob)
                    W("tensor", VE, sT_done[hh])
                    if b == 0:
                        kv_cur_b = ap(o_scr_kvini + h * 64, 64, BF16)
                        if hg == 0:
                            W("tensor", SE, kv_init_done_se)
                    else:
                        kv_cur_b = ap(o_scr_kvppb + (hh * 2 + (b - 1) % 2) * 64, 64, BF16)
                        W("tensor", VE, kv_ve.get(hh))
                    for nh in range(2):
                        OP("tensor", lambda e, p=PS[ob][:, 0:256],
                           l=gsl(g_vt, hh, (2 * b + nh) * 128, 128),
                           r=ap(o_scr_sT + (hh * 2 + nh) * 128, 128, BF16),
                           st=(nh == 0): e.matmul(p, l, r, start=st, stop=False))
                    tO = OP("tensor", lambda e, p=PS[ob][:, 0:256], l=kv_cur_b,
                            r=gsl(g_qs, hh, b * 256, 256): e.matmul(p, l, r, start=False, stop=True), TE)
                    sT_rd[hh] = tO
                    W("scalar", TE, tO)
                    odst = ap(o_ofm, 8192, BF16).rearrange("p (hx t) -> p hx t", hx=16)[:, h, b * 256:(b + 1) * 256]
                    evO = OP("scalar", lambda e, o=odst, p=PS[ob][:, 0:256]: e.copy(o, p), SE)
                    bank_free[ob] = (SE, evO)
                for hh in range(2):
                    h = hg * 2 + hh
                    kb = 7 if hh == 0 else 3
                    W("vector", TE, kvmm_te[hh])
                    if b == 0:
                        kv_cur_x = ap(o_scr_kvini + h * 64, 64, BF16)
                    else:
                        kv_cur_x = ap(o_scr_kvpp + (hh * 2 + (b - 1) % 2) * 128, 128)
                    kv_nxt = ap(o_scr_kvpp + (hh * 2 + b % 2) * 128, 128)
                    kv_nxt_b = ap(o_scr_kvppb + (hh * 2 + b % 2) * 64, 64, BF16)
                    DRAIN("vector")
                    OP("vector", lambda e, o=kv_nxt, i=kv_cur_x, p=PS[kb][:, 0:128],
                       bd=bdec[h]: e.scalar_tensor_tensor(o, i, bd, p, AL.mult, AL.add), VE)
                    DRAIN("vector")
                    vKb = OP("vector", lambda e, o=kv_nxt_b, i=kv_nxt: e.tensor_copy(o, i), VE)
                    kv_ve[hh] = vKb
                    bank_free[kb] = (VE, vKb)
            dq_slot_ve[sdq] = VE.n
        attn_se = SE.n
        if stop_phase == "attn":
            return

        ofm_h = ap(o_ofm, 8192, BF16)
        o2_slot_te = {}
        st_te = None
        for fc in range(FC):
            slot = fc % 2
            o2 = ap(o_o2 + slot * 512, 512, BF16)
            W("scalar", TE, o2_slot_te.get(slot))
            if fc < 2:
                W("scalar", TE, lnt_slot_te.get(fc % 2))
            sq = OP("scalar", lambda e, o=o2, i=ofm_h[:, fc * 1024:(fc + 1) * 1024]:
                    e.activation(out=o, in_=i, func=AF.Square), SE)
            W("tensor", SE, sq)
            if fc == 0:
                bank_wait("tensor", 0); bank_wait("tensor", 1)
            for half in range(2):
                st_te = OP("tensor", lambda e, p=PS[half][0:1, 0:512], l=ap(o_ones, 64, BF16)[:, 0:1],
                           r=o2[:, half * 512:(half + 1) * 512], st=(fc == 0), sp=(fc == FC - 1):
                           e.matmul(p, l, r, start=st, stop=sp), TE if half == 1 else None)
            o2_slot_te[slot] = st_te
        W("scalar", TE, st_te)
        srow = ap(o_srow, 1024, parts=(0, 1))
        sc1 = None
        for half in range(2):
            sc1 = OP("scalar", lambda e, o=srow[:, half * 512:(half + 1) * 512], p=PS[half][0:1, 0:512]:
                     e.copy(o, p), SE if half == 1 else None)
        bank_free[0] = (SE, sc1); bank_free[1] = (SE, sc1)
        W("vector", SE, sc1)
        OP("vector", lambda e, o=srow, s=1.0 / D, b=EPS_RMS: e.tensor_scalar(o, o, s, b, AL.mult, AL.add))
        DRAIN("vector")
        vr = OP("vector", lambda e, o=srow: e.reciprocal(o, o), VE)
        W("scalar", VE, vr)
        srow_b = ap(o_srow + 1024, 512, BF16, parts=(0, 1))
        sq2 = OP("scalar", lambda e, o=srow_b, i=srow:
                 e.activation(out=o, in_=i, func=AF.Sqrt), SE)
        W("tensor", SE, sq2)
        bank_wait("tensor", 4); bank_wait("tensor", 5)
        bc_te = None
        for half in range(2):
            bc_te = OP("tensor", lambda e, p=PS[4 + half][:, 0:512], l=ap(o_ones, 64, BF16)[0:1, 0:128],
                       r=srow_b[:, half * 512:(half + 1) * 512]:
                       e.matmul(p, l, r, start=True, stop=True), TE if half == 1 else None)
        W("vector", TE, bc_te)

        ogt_h = ap(o_ogt, 8192, BF16)
        wg_slot_te = {0: qk_res_te.get(0), 1: qk_res_te.get(0)}
        gv_prev = None
        for fc in range(FC):
            slot = fc % 2
            W("gpsimd", TE, wg_slot_te.get(slot))
            ld = OP("gpsimd", lambda e, d=ap(o_qkw + slot * 1024, 1024, BF16), s=w_g[fc]:
                    e.dma_start(out=d, in_=s), S_qk[slot], 16)
            W("tensor", S_qk[slot], ld)
            t1 = None
            for th in range(2):
                bank = slot * 2 + th
                bank_wait("tensor", bank)
                for f in range(FC):
                    lhs = ap(o_qkw + slot * 1024, 1024, BF16)[:, f * 128:(f + 1) * 128]
                    t1 = OP("tensor", lambda e, p=PS[bank][:, 0:512], l=lhs,
                            r=xf[:, f, th * 512:(th + 1) * 512],
                            st=(f == 0), sp=(f == FC - 1): e.matmul(p, l, r, start=st, stop=sp),
                            TE if (th == 1 and f == FC - 1) else None)
            wg_slot_te[slot] = t1
            W("scalar", TE, t1)
            W("scalar", VE, gv_prev)
            sg = None
            for th in range(2):
                sg = OP("scalar", lambda e, o=ap(o_sil, 1024)[:, th * 512:(th + 1) * 512],
                        p=PS[slot * 2 + th][:, 0:512]: e.activation(out=o, in_=p, func=AF.Sigmoid),
                        SE if th == 1 else None)
            bank_free[slot * 2] = (SE, sg); bank_free[slot * 2 + 1] = (SE, sg)
            W("vector", SE, sg)
            W("vector", CONST, const_n)
            for th in range(2):
                OP("vector", lambda e, o=ap(o_sil + 1024, 1024)[:, th * 512:(th + 1) * 512],
                   i=ofm_h[:, fc * 1024 + th * 512: fc * 1024 + (th + 1) * 512],
                   nw=ap(o_normw, FC)[:, fc:fc + 1], p=PS[4 + th][:, 0:512]:
                   e.scalar_tensor_tensor(o, i, nw, p, AL.mult, AL.mult))
            DRAIN("vector")
            gv = None
            for th in range(2):
                gv = OP("vector", lambda e, o=ogt_h[:, fc * 1024 + th * 512: fc * 1024 + (th + 1) * 512],
                        a=ap(o_sil + 1024, 1024)[:, th * 512:(th + 1) * 512],
                        b=ap(o_sil, 1024)[:, th * 512:(th + 1) * 512]: e.tensor_tensor(o, a, b, AL.mult),
                        VE if th == 1 else None)
            gv_prev = gv
        ogt_done_ve = VE.n
        bank_free[4] = (VE, ogt_done_ve); bank_free[5] = (VE, ogt_done_ve)
        if stop_phase == "c4":
            return

        x2b = ap(o_x2b, 8192, BF16)
        x2f = x2b.rearrange("p (f t) -> p f t", f=16)
        d0_slot_te = {0: lnt_slot_te.get(0), 1: lnt_slot_te.get(1)}
        W("tensor", VE, ogt_done_ve)
        wow_slot_te = {}
        wow_half_te = {}
        at4_ve = None
        xresb = ap(o_xres, 8192, BF16)
        for tg in range(2):
            for oc in range(4):
                par = oc % 2
                wbase = o_qkw if par == 0 else o_kvw
                W("gpsimd", TE, wow_half_te.get(par))
                if tg == 0 and oc == 0:
                    W("gpsimd", TE, wg_slot_te.get(0))
                    W("gpsimd", TE, wg_slot_te.get(1))
                if tg == 0 and oc == 1:
                    W("gpsimd", TE, dummy_kv_guard)
                wld1 = OP("gpsimd", lambda e, d=ap(wbase, 2048, BF16), s=w_o[oc, :, 0:4096]:
                          e.dma_start(out=d, in_=s), S_wo[par], 16)
                W("gpsimd", TE, wow_slot_te.get(par))
                wld2 = OP("gpsimd", lambda e, d=ap(wbase + 2048, 2048, BF16), s=w_o[oc, :, 4096:8192]:
                          e.dma_start(out=d, in_=s), S_wo2[par], 16)
                te_oc = None
                for hp in range(2):
                    W("tensor", (S_wo if hp == 0 else S_wo2)[par], wld1 if hp == 0 else wld2)
                    for t in range(4):
                        bank = 4 + t
                        if hp == 0:
                            bank_wait("tensor", bank)
                        for hd in range(hp * 8, hp * 8 + 8):
                            lhs = ogt_h[:, hd * 1024 + (tg * 4 + t) * 128: hd * 1024 + (tg * 4 + t) * 128 + 128]
                            te_oc = OP("tensor", lambda e, p=PS[bank][:, 0:512], l=lhs,
                                       r=ap(wbase, 4096, BF16)[:, hd * 512:(hd + 1) * 512],
                                       st=(hd == 0), sp=(hd == FC - 1): e.matmul(p, l, r, start=st, stop=sp),
                                       TE if hd == hp * 8 + 7 else None)
                    if hp == 0:
                        wow_half_te[par] = te_oc
                wow_slot_te[par] = te_oc
                W("scalar", TE, te_oc)
                if tg == 1 and oc == 0:
                    W("scalar", VE, at4_ve)
                if tg == 0 and oc == 0:
                    W("scalar", VE, ogt_done_ve)
                ev = None
                for t in range(4):
                    bank = 4 + t
                    dst = ap(o_at4 + tg * 4096, 4096, BF16)[:, t * 2048 + oc * 512: t * 2048 + (oc + 1) * 512]
                    ev = OP("scalar", lambda e, o=dst, p=PS[bank][:, 0:512]: e.copy(o, p),
                            SE if t == 3 else None)
                for t in range(4):
                    bank_free[4 + t] = (SE, ev)
            at4_se = SE.n
            for t in range(4):
                tt = tg * 4 + t
                slot = tt % 2
                xt = ap(o_xt + slot * 2048, 2048)
                W("sync", VE, xt_slot_ve.get(slot))
                gx = OP("sync", lambda e, d=xt, s=x_own[tt]: e.dma_start(out=d, in_=s), S_xt[slot], 16)
                W("vector", SE, at4_se)
                at = ap(o_at4 + tg * 4096, 4096, BF16)[:, t * 2048:(t + 1) * 2048]
                ln_tile(at, ap(o_nrm, 2048), EPS_LN)
                W("vector", S_xt[slot], gx)
                DRAIN("vector")
                vres = OP("vector", lambda e, o=xresb[:, tt * 2048:(tt + 1) * 2048],
                          a=ap(o_nrm, 2048), b=xt: e.tensor_tensor(o, a, b, AL.add), VE)
                xt_slot_ve[slot] = vres
            at4_ve = VE.n
            if tg == 0:
                W("scalar", TE, wg_slot_te[(FC - 1) % 2])
                for t in range(4):
                    tt = t
                    slot = tt % 2
                    xo = ap(o_xlntm + slot * 1024, 1024, BF16)
                    W("vector", TE, d0_slot_te.get(slot))
                    if t < 2:
                        W("vector", TE, o2_slot_te.get(slot))
                    vdone = ln_tile(xresb[:, tt * 2048:(tt + 1) * 2048], xo, EPS_LN)
                    W("tensor", VE, vdone)
                    transpose_tile(xo, x2b, tt, (2, 3))
                    d0_slot_te[slot] = TE.n
                ln2_half_se = SE.n
        xres_done_ve = VE.n

        def emit_d0_second_half():
            for t in range(4):
                tt = 4 + t
                slot = tt % 2
                xo = ap(o_xlntm + slot * 1024, 1024, BF16)
                W("vector", TE, d0_slot_te.get(slot))
                vdone = ln_tile(xresb[:, tt * 2048:(tt + 1) * 2048], xo, EPS_LN)
                W("tensor", VE, vdone)
                transpose_tile(xo, x2b, tt, (6, 7))
                d0_slot_te[slot] = TE.n
        if stop_phase == "c5":
            return

        if stop_phase == "d0":
            return
        hsb = ap(o_hsb, 16384, BF16)
        fft_tc_se = {}
        dw_half_te = {}
        gu_slot_te = {}
        gu_gate_te = {}
        dw_slot_te = {}
        sil_slot_ve = {}
        out_lg = {}
        for th in range(2):
            for ic in range(ICN):
                slot = ic % 2
                W("gpsimd", TE, gu_gate_te.get(slot))
                if th == 1 and ic < 2:
                    W("gpsimd", TE, dw_slot_te.get(0))
                    W("gpsimd", TE, dw_slot_te.get(1))
                if th == 0 and ic < 2:
                    W("gpsimd", TE, wow_slot_te.get(0))
                    W("gpsimd", TE, wow_slot_te.get(1))
                ldg = OP("gpsimd", lambda e, d=ap(o_gu + slot * 1024, 1024, BF16), s=w_gate[ic]:
                         e.dma_start(out=d, in_=s), S_gu[slot], 16)
                W("gpsimd", TE, gu_slot_te.get(slot))
                ldu = OP("gpsimd", lambda e, d=ap(o_gu2 + slot * 1024, 1024, BF16), s=w_up[ic]:
                         e.dma_start(out=d, in_=s), S_gu2[slot], 16)
                W("tensor", S_gu[slot], ldg)
                W("tensor", S_gu2[slot], ldu)
                if th == 0 and ic == 0:
                    W("tensor", SE, ln2_half_se)
                if th == 1 and ic == 0:
                    W("tensor", SE, ln2_se)
                bga = slot * 2
                bgb = 1 + slot * 2
                bank_wait("tensor", bga)
                tg1 = None
                for f in range(FC):
                    lhs = ap(o_gu + slot * 1024, 1024, BF16)[:, f * 128:(f + 1) * 128]
                    tg1 = OP("tensor", lambda e, p=PS[bga][:, 0:512], l=lhs,
                             r=x2f[:, f, th * 512:(th + 1) * 512],
                             st=(f == 0), sp=(f == FC - 1): e.matmul(p, l, r, start=st, stop=sp),
                             TE if f == FC - 1 else None)
                bank_wait("tensor", bgb)
                tg2 = None
                for f in range(FC):
                    lhs = ap(o_gu2 + slot * 1024, 1024, BF16)[:, f * 128:(f + 1) * 128]
                    tg2 = OP("tensor", lambda e, p=PS[bgb][:, 0:512], l=lhs,
                             r=x2f[:, f, th * 512:(th + 1) * 512],
                             st=(f == 0), sp=(f == FC - 1): e.matmul(p, l, r, start=st, stop=sp),
                             TE if f == FC - 1 else None)
                gu_slot_te[slot] = tg2
                gu_gate_te[slot] = tg1
                if th == 0 and ic == 3:
                    emit_d0_second_half()
                    ln2_se = SE.n
                W("scalar", TE, tg1)
                W("scalar", VE, sil_slot_ve.get(slot))
                sil = ap(o_sil + slot * 512, 512)
                sev = OP("scalar", lambda e, o=sil, p=PS[bga][:, 0:512]:
                         e.activation(out=o, in_=p, func=AF.Silu), SE)
                bank_free[bga] = (SE, sev)
                W("vector", TE, tg2)
                W("vector", SE, sev)
                hv = OP("vector", lambda e, o=hsb[:, ic * 512:(ic + 1) * 512], p=PS[bgb][:, 0:512], s=sil:
                        e.tensor_tensor(o, p, s, AL.mult), VE)
                sil_slot_ve[slot] = hv
                bank_free[bgb] = (VE, hv)
            h_done_ve = VE.n
            W("tensor", VE, h_done_ve)
            for oh in range(2):
                dte = None
                for icq in range(16):
                    grp = icq % 2
                    doff = o_kvw if grp == 0 else o_gu
                    lds = []
                    W("gpsimd", TE, dw_half_te.get(grp))
                    if oh == 0 and icq < 2:
                        W("gpsimd", TE, gu_slot_te.get(0))
                        W("gpsimd", TE, gu_slot_te.get(1))
                    for j in range(2):
                        ic = icq * 4 + j
                        lds.append(OP("gpsimd", lambda e, d=ap(doff + j * 512, 512, BF16),
                                      s=w_down[oh, ic]: e.dma_start(out=d, in_=s), S_dw[grp], 16))
                    W("gpsimd", TE, dw_slot_te.get(grp))
                    for j in range(2, 4):
                        ic = icq * 4 + j
                        lds.append(OP("gpsimd", lambda e, d=ap(doff + j * 512, 512, BF16),
                                      s=w_down[oh, ic]: e.dma_start(out=d, in_=s), S_dw2[grp], 16))
                    for jp in range(2):
                        W("tensor", (S_dw if jp == 0 else S_dw2)[grp], lds[jp * 2 + 1])
                        for b in range(8):
                            tc4, oc = b // 2, b % 2
                            if icq == 0 and jp == 0:
                                bank_wait("tensor", b)
                            for j in (jp * 2, jp * 2 + 1):
                                ic = icq * 4 + j
                                lhs = hsb[:, ic * 512 + tc4 * 128: ic * 512 + tc4 * 128 + 128]
                                rhs = ap(doff + j * 512, 512, BF16)[:, oc * 512:(oc + 1) * 512]
                                dte = OP("tensor", lambda e, p=PS[b][:, 0:512], l=lhs, r=rhs,
                                         st=(ic == 0), sp=(ic == ICN - 1): e.matmul(p, l, r, start=st, stop=sp),
                                         TE if (b == 7 and j == jp * 2 + 1) else None)
                        if jp == 0:
                            dw_half_te[grp] = dte
                    dw_slot_te[grp] = dte
                W("scalar", TE, dte)
                if th == 0 and oh == 0:
                    W("scalar", TE, wg_slot_te[(FC - 1) % 2])
                ev = None
                for b in range(8):
                    tc4, oc = b // 2, b % 2
                    dst = ap(o_fft, 4096, BF16)[:, tc4 * 2048 + oh * 1024 + oc * 512:
                                                tc4 * 2048 + oh * 1024 + (oc + 1) * 512]
                    ev = OP("scalar", lambda e, o=dst, p=PS[b][:, 0:512]: e.copy(o, p), SE)
                    if oh == 1:
                        fft_tc_se[tc4] = ev
                for b in range(8):
                    bank_free[b] = (SE, ev)
            for tc4 in range(4):
                tt = th * 4 + tc4
                slot = tt % 2
                W("vector", SE, fft_tc_se[tc4])
                ft = ap(o_fft, 4096, BF16)[:, tc4 * 2048:(tc4 + 1) * 2048]
                ln_tile(ft, ap(o_nrm, 2048), EPS_LN)
                DRAIN("vector")
                ot = ap(o_outt + slot * 2048, 2048)
                if tt >= 2:
                    W("vector", S_out[slot], out_lg[tt - 2])
                else:
                    W("vector", VE, xt_slot_ve.get(slot))
                vo = OP("vector", lambda e, o=ot, a=ap(o_nrm, 2048),
                        b=xresb[:, tt * 2048:(tt + 1) * 2048]: e.tensor_tensor(o, a, b, AL.add), VE)
                W("gpsimd", VE, vo)
                out_lg[tt] = OP("gpsimd", lambda e, d=out_d[tt], s=ot: e.dma_start(out=d, in_=s), S_out[slot], 16)

        W("gpsimd", S_out[0], S_out[0].n)
        W("gpsimd", S_out[1], S_out[1].n)

    with (
        nc.sbuf_tensor("AR", [128, AWORDS], F32) as AR_t,
        nc.psum_tensor("p0", [128, 512], F32) as p0,
        nc.psum_tensor("p1", [128, 512], F32) as p1,
        nc.psum_tensor("p2", [128, 512], F32) as p2,
        nc.psum_tensor("p3", [128, 512], F32) as p3,
        nc.psum_tensor("p4", [128, 512], F32) as p4,
        nc.psum_tensor("p5", [128, 512], F32) as p5,
        nc.psum_tensor("p6", [128, 512], F32) as p6,
        nc.psum_tensor("p7", [128, 512], F32) as p7,
        nc.Block() as block,
    ):
        import contextlib
        with contextlib.ExitStack() as stack:
            for sm in SEMS:
                sm.h = stack.enter_context(nc.semaphore(sm.name))
            ARh[0] = AR_t.ap()
            PS.extend([p.ap() for p in (p0, p1, p2, p3, p4, p5, p6, p7)])
            logical()
            _emit_all(block, prog)
    return nc


def _emit_all(block, prog):
    if True:

        @block.sync
        def _(e):
            for fn in prog["sync"]:
                fn(e)

        @block.tensor
        def _(e):
            for fn in prog["tensor"]:
                fn(e)

        @block.vector
        def _(e):
            for fn in prog["vector"]:
                fn(e)

        @block.scalar
        def _(e):
            for fn in prog["scalar"]:
                fn(e)

        @block.gpsimd
        def _(e):
            for fn in prog["gpsimd"]:
                fn(e)



def _host_prep(L0_attn_x, L0_attn_Wqkv, L0_attn_Wg, L0_attn_Wout, L0_attn_norm_w,
               L0_ffn_gate_w, L0_ffn_up_w, L0_ffn_down_w):
    import ml_dtypes
    f32 = np.float32
    slopes = np.array(_slopes(), dtype=np.float64)
    arr = np.arange(1, BLK + 1, dtype=np.float64)

    qdec = np.exp(-slopes[:, None] * arr[None, :])
    kdec = np.exp(-slopes[:, None] * (BLK - arr[None, :]))
    idx = arr[:, None] - arr[None, :]
    dd = np.where(idx[None] >= 0, np.exp(-slopes[:, None, None] * idx[None]), 0.0)

    qdec_in = np.ascontiguousarray(np.broadcast_to(
        qdec.reshape(NHG, 2, 1, BLK), (NHG, 2, 128, BLK)).transpose(0, 2, 1, 3).reshape(
        NHG, 128, 512)).astype(np.float16)
    kdec_in = np.ascontiguousarray(kdec.reshape(H, 2, 128).transpose(2, 0, 1).reshape(128, 2 * H)).astype(f32)
    bd = np.exp(-slopes * BLK)
    kdp = np.zeros((128, 128))
    for h in range(H):
        for b in range(4):
            for u in range(2):
                kdp[:, h * 8 + b * 2 + u] = kdec[h, u * 128:(u + 1) * 128] * bd[h] ** (3 - b)
    kdp_in = np.ascontiguousarray(kdp).astype(f32)
    ddt = dd.transpose(0, 2, 1)
    ddt_in = np.ascontiguousarray(ddt.reshape(NHG, 2, 2, 128, BLK).transpose(0, 3, 1, 2, 4).reshape(
        NHG, 128, 1024)).astype(np.float16)

    Wqkv = np.asarray(L0_attn_Wqkv, dtype=f32)
    qk = np.empty((NHG, 4, FC, 128, 128), dtype=f32)
    vt = np.empty((NHG, FC, 128, 256), dtype=f32)
    kv = np.empty((NHG, FC, 128, 512), dtype=f32)
    Wq3 = Wqkv.reshape(FC, 128, 6144)
    for g in range(NHG):
        for hh in range(2):
            h = 2 * g + hh
            base = 384 * h
            qk[g, 2 * hh + 0] = Wq3[:, :, base:base + 128]
            qk[g, 2 * hh + 1] = Wq3[:, :, base + 128:base + 256]
            vt[g, :, :, hh * 128:(hh + 1) * 128] = Wq3[:, :, base + 256:base + 384]
            kv[g, :, :, hh * 256:hh * 256 + 128] = Wq3[:, :, base + 128:base + 256]
            kv[g, :, :, hh * 256 + 128:hh * 256 + 256] = Wq3[:, :, base + 256:base + 384]

    qk = np.ascontiguousarray(
        qk.transpose(0, 1, 3, 2, 4).reshape(NHG, 4, 128, FC * 128).transpose(0, 2, 1, 3).reshape(
            NHG, 128, 4 * FC * 128))
    kv = np.ascontiguousarray(
        kv.reshape(NHG, 4, 4, 128, 512).transpose(0, 1, 3, 2, 4).reshape(NHG, 4, 128, 2048))
    vt = np.ascontiguousarray(vt.transpose(0, 2, 1, 3).reshape(NHG, 128, FC * 256))
    Wg = np.asarray(L0_attn_Wg, dtype=f32).reshape(FC, 128, FC, 128)
    wg_in = np.ascontiguousarray(Wg.transpose(2, 1, 0, 3).reshape(FC, 128, FC * 128))
    Wo = np.asarray(L0_attn_Wout, dtype=f32).reshape(FC, 128, 4, 512)
    wo_in = np.ascontiguousarray(Wo.transpose(2, 1, 0, 3).reshape(4, 128, FC * 512))

    gT = np.asarray(L0_ffn_gate_w, dtype=f32).T.reshape(FC, 128, ICN, 128)
    gw_in = np.ascontiguousarray(gT.transpose(2, 1, 0, 3).reshape(ICN, 128, FC * 128))
    uT = np.asarray(L0_ffn_up_w, dtype=f32).T.reshape(FC, 128, ICN, 128)
    uw_in = np.ascontiguousarray(uT.transpose(2, 1, 0, 3).reshape(ICN, 128, FC * 128))
    dT = np.asarray(L0_ffn_down_w, dtype=f32).T.reshape(ICN, 128, 2, 1024)
    dw_in = np.ascontiguousarray(dT.transpose(2, 0, 1, 3))

    nw = np.asarray(L0_attn_norm_w, dtype=f32).reshape(FC, 128)
    nw_in = np.ascontiguousarray(nw.T)

    consts = {
        "id_b": np.eye(128, dtype=f32).astype(ml_dtypes.bfloat16),
        "ones_r": np.ones((128, 128), dtype=f32).astype(ml_dtypes.bfloat16),
        "kdec": kdec_in,
        "kdecp": kdp_in,
        "normw": nw_in,
        "qdec": qdec_in,
        "ddt": ddt_in,
        "w_qk": qk, "w_v": vt, "w_kv": kv, "w_g": wg_in, "w_o": wo_in,
        "w_gate": gw_in, "w_up": uw_in, "w_down": dw_in,
    }

    x = np.asarray(L0_attn_x, dtype=f32)
    in_maps = []
    for c in range(8):
        s, half = c // 2, c % 2
        xo = np.ascontiguousarray(x[s, half * T:(half + 1) * T].reshape(TC, 128, D))
        if half == 1:
            xp = np.ascontiguousarray(x[s, 0:T].reshape(TC, 128, D))
            mk = np.ones((128, 1), dtype=f32)
        else:
            xp = xo
            mk = np.zeros((128, 1), dtype=f32)
        m = dict(consts)
        m["x_own"] = xo
        m["x_prev"] = xp
        m["maskc"] = mk
        in_maps.append(m)
    return in_maps


_NC_CACHE = [None]


def kernel(L0_attn_x, L0_attn_Wqkv, L0_attn_Wg, L0_attn_Wout, L0_attn_norm_w,
           L0_ffn_gate_w, L0_ffn_up_w, L0_ffn_down_w):
    in_maps = _host_prep(L0_attn_x, L0_attn_Wqkv, L0_attn_Wg, L0_attn_Wout, L0_attn_norm_w,
                         L0_ffn_gate_w, L0_ffn_up_w, L0_ffn_down_w)
    if _NC_CACHE[0] is None:
        _NC_CACHE[0] = build_program()
    nc = _NC_CACHE[0]
    res = run_bass_kernel_spmd(nc, in_maps, core_ids=list(range(8)))
    out = np.empty((4, 2048, D), dtype=np.float32)
    for c in range(8):
        s, half = c // 2, c % 2
        out[s, half * T:(half + 1) * T] = np.asarray(res.results[c]["out"]).reshape(T, D)
    return out
